# version 18
# baseline (speedup 1.0000x reference)
import contextlib
import numpy as np
import ml_dtypes
import concourse.bass as bass
import concourse.mybir as mybir
from concourse.bass_utils import run_bass_kernel_spmd

F32 = mybir.dt.float32
BF16 = mybir.dt.bfloat16
I32 = mybir.dt.int32
AF = mybir.ActivationFunctionType
ALU = mybir.AluOpType
AX = mybir.AxisListType
BF = ml_dtypes.bfloat16

T = 2048
C = 1024
NT_T = 16
NT_C = 8
TB = 512
NTB = 4
EPS_LN = 1e-5

NSYNC = 6
NPOOL = 4
DSYNC = tuple(f"dsync{i}" for i in range(NSYNC))
DPOOL = tuple(f"dpool{i}" for i in range(NPOOL))
SEMS = ("pe", "dve", "act", "pool", "dstage") + DSYNC + DPOOL
ENG_SEM = {"tensor": "pe", "vector": "dve", "scalar": "act", "gpsimd": "pool",
           "sync": "dsync0"}

TRANS_W = {
    "L0_attn_Wq": (1024, 1024), "L0_attn_Wk": (1024, 1024),
    "L0_attn_Wv": (1024, 1024), "L0_attn_Wp": (1024, 1024),
    "L1_attn_W_dt": (1024, 1024),
    "L2_ffn_W1": (8 * 4096, 1024), "L2_ffn_Wo": (1024, 2048),
    "L3_attn_Wk": (1024, 1024), "L3_attn_Wv": (1024, 1024),
    "L3_attn_Wr": (1024, 1024), "L3_attn_Wo": (1024, 1024),
}
TRANS_STAGE = ("L2_ffn_W1", "L2_ffn_Wo")


def host_consts():
    d = {}
    d["id128"] = np.eye(128, dtype=BF)
    d["ones128"] = np.ones((128, 128), dtype=BF)
    g = np.zeros((1024, 16), dtype=BF)
    for k in range(8):
        for p in range(128):
            g[k * 128 + p, 2 * k + p // 64] = 1
    d["Gden"] = g
    d["Hbc"] = np.ascontiguousarray(g.T)
    g2 = np.zeros((128, 2), dtype=BF)
    g2[np.arange(128), np.arange(128) // 64] = 1
    d["G2"] = g2
    d["H2"] = np.ascontiguousarray(g2.T)
    bb = np.zeros((128, 128), dtype=BF)
    for p in range(128):
        q0 = (p // 64) * 64
        bb[p, q0:q0 + 64] = 1
    d["BB"] = bb
    rs = np.zeros((20, 20 * 128), dtype=BF)
    for e in range(20):
        rs[e, e * 128:(e + 1) * 128] = 1
    d["rowsel"] = rs
    for bkt in (15, 31):
        m = np.zeros((32, 128), dtype=BF)
        m[bkt, :] = 1
        d[f"del{bkt}"] = m

    def bucket(rel):
        nb = 16
        rb = (rel > 0).astype(np.int64) * nb
        rel = np.abs(rel)
        me = 8
        small = rel < me
        large = me + (np.log(np.maximum(rel, 1).astype(np.float64) / me)
                      / np.log(128 / me) * (nb - me)).astype(np.int64)
        large = np.minimum(large, nb - 1)
        return rb + np.where(small, rel, large)
    d["fconsts"] = np.tile(np.array([[1e-5, 1.0]], dtype=np.float32), (128, 1))
    d["onesf"] = np.ones((65, 128), dtype=np.float32)
    oh = np.zeros((32, 3 * 16384), dtype=BF)
    p = np.arange(128)[:, None]
    f = np.arange(128)[None, :]
    flat = (p * 128 + f).ravel()
    for i, dlt in enumerate((-128, 0, 128)):
        bk = bucket(dlt + p - f)
        oh[bk.ravel(), i * 16384 + flat] = 1
    d["OH3"] = oh
    return d


class Bld:
    def __init__(self, nc, blk, stack):
        self.nc = nc
        self.blk = blk
        self.stack = stack
        self.sem = {s: stack.enter_context(nc.semaphore(s)) for s in SEMS}
        self.cnt = {s: 0 for s in SEMS}
        self.lw = {}
        self.rd = {}
        self.waited = {e: {} for e in ENG_SEM}
        self.bufs = {}
        self.A = {}
        self.wbf = {}
        self.dma_rr = {"sync": 0, "gpsimd": 0}
        self._cur_dma_sem = None

    def sb(self, name, shape, dtype):
        t = self.stack.enter_context(self.nc.sbuf_tensor(name, shape, dtype))
        self.bufs[name] = t
        return t

    def ps(self, name, shape, dtype=F32):
        t = self.stack.enter_context(self.nc.psum_tensor(name, shape, dtype))
        self.bufs[name] = t
        return t

    def op(self, eng, fn, reads=(), writes=(), dma=False, stage_dma=False):
        if eng == "sync":
            dma = True
        if dma:
            if stage_dma:
                sem = "dstage"
            elif eng == "gpsimd":
                sem = DPOOL[self.dma_rr["gpsimd"] % NPOOL]
                self.dma_rr["gpsimd"] += 1
            else:
                sem = DSYNC[self.dma_rr["sync"] % NSYNC]
                self.dma_rr["sync"] += 1
            inc = 16
        else:
            sem = ENG_SEM[eng]
            inc = 1
        deps = {}

        def add(d):
            if d:
                s, c = d
                deps[s] = max(deps.get(s, 0), c)
        for r in reads:
            add(self.lw.get(r))
        for w in writes:
            add(self.lw.get(w))
            for s, c in self.rd.get(w, {}).items():
                add((s, c))
        if dma:
            add((sem, self.cnt[sem]))
        me = self
        out_cnt = [0]

        def body(e):
            for s, c in deps.items():
                if me.waited[eng].get(s, 0) < c:
                    e.wait_ge(me.sem[s], c)
                    me.waited[eng][s] = c
            me._cur_dma_sem = sem if dma else None
            ins = fn(e)
            ins.then_inc(me.sem[sem], inc)
            me.cnt[sem] += inc
            out_cnt[0] = me.cnt[sem]
        getattr(self.blk, eng)(body)
        cnt = out_cnt[0]
        for r in reads:
            d = self.rd.setdefault(r, {})
            d[sem] = max(d.get(sem, 0), cnt)
        for w in writes:
            self.lw[w] = (sem, cnt)
            self.rd[w] = {}
        return (sem, cnt)

    def multi(self, eng, items):
        sem = ENG_SEM[eng]
        me = self

        def body(e):
            for fn, reads, writes in items:
                deps = {}

                def add(d):
                    if d:
                        s, c = d
                        deps[s] = max(deps.get(s, 0), c)
                for r in reads:
                    add(me.lw.get(r))
                for w in writes:
                    add(me.lw.get(w))
                    for s, c in me.rd.get(w, {}).items():
                        add((s, c))
                for s, c in deps.items():
                    if me.waited[eng].get(s, 0) < c:
                        e.wait_ge(me.sem[s], c)
                        me.waited[eng][s] = c
                ins = fn(e)
                ins.then_inc(me.sem[sem], 1)
                me.cnt[sem] += 1
                cnt = me.cnt[sem]
                for r in reads:
                    d = me.rd.setdefault(r, {})
                    d[sem] = max(d.get(sem, 0), cnt)
                for w in writes:
                    me.lw[w] = (sem, cnt)
                    me.rd[w] = {}
        getattr(self.blk, eng)(body)

    def tag_dma(self, ins):
        ins.then_inc(self.sem[self._cur_dma_sem], 16)
        self.cnt[self._cur_dma_sem] += 16

    def barrier(self):
        marks = [(ENG_SEM[e], self.cnt[ENG_SEM[e]])
                 for e in ("tensor", "vector", "scalar", "gpsimd")]
        marks += [(s, self.cnt[s]) for s in DSYNC + DPOOL]
        marks = [(s, c) for (s, c) in marks if c > 0]
        for e in ("tensor", "vector", "scalar", "gpsimd", "sync"):
            me = self

            def body(en, e=e):
                for s, c in marks:
                    if s == ENG_SEM.get(e):
                        continue
                    if me.waited[e].get(s, 0) < c:
                        en.wait_ge(me.sem[s], c)
                        me.waited[e][s] = c
            getattr(self.blk, e)(body)


def wN_load(s, b, dst, name, rows, cols):
    src = b.A[name]
    if len(src.shape) == 3:
        src = src.rearrange("e a c -> (e a) c")
    r0, nk = rows
    c0, w = cols
    v = src[r0:r0 + nk * 128, c0:c0 + w].rearrange("(k p) a -> p k a", p=128)
    return s.dma_start(out=dst.rearrange("p (k a) -> p k a", a=w), in_=v)


def wT_load(s, b, dst, name, j, jw=128, k0=0, nk=None):
    v = b.wbf.get(name)
    if v is None:
        v = b.A[name + "_T"]
    if nk is None:
        nk = v.shape[0] // 128
    slab = v[k0 * 128:(k0 + nk) * 128,
             j * jw:(j + 1) * jw].rearrange("(k p) a -> p k a", p=128)
    return s.dma_start(out=dst.rearrange("p (k a) -> p k a", a=jw), in_=slab)


def mm_chain(b, out_ap, pairs, reads, wbank):
    def fn(t):
        last = None
        n = len(pairs)
        for i, (la, ra) in enumerate(pairs):
            last = t.matmul(out_ap, la, ra, start=(i == 0), stop=(i == n - 1))
        return last
    return b.op("tensor", fn, reads=reads, writes=(wbank,))


def bias_cols(b, st, tag, src_ap, n, rowbuf=None):
    nc = b.nc
    nk = n // 128
    rows = st.enter_context(nc.sbuf_tensor(f"{tag}_rows", [nk, 128], F32))
    col = st.enter_context(nc.sbuf_tensor(f"{tag}_col", [128, nk], F32))
    b.op("sync", lambda s: s.dma_start(
        out=rows[:], in_=src_ap.rearrange("(k f) -> k f", f=128)),
        writes=(f"{tag}_rows",))
    b.op("tensor", lambda t: t.transpose(
        b.psm[:, 0:nk], rows[0:nk, 0:128], b.bufs["c_idf"][0:nk, 0:nk]),
        reads=(f"{tag}_rows", "c_idf"), writes=("psm",))
    b.op("vector", lambda v: v.tensor_copy(col[:], b.psm[:, 0:nk]),
         reads=("psm",), writes=(f"{tag}_col",))
    return col, f"{tag}_col"


def build_program(spec, layers=(0, 1, 2, 3), taps=(), stage=None):
    nc = bass.Bass("TRN2", target_bir_lowering=False)
    stack = contextlib.ExitStack()
    with stack:
        with nc.Block() as blk:
            b = Bld(nc, blk, stack)
            b.stage = stage
            for name, (shape, dt) in spec.items():
                mdt = {"float32": F32, "bfloat16": BF16}[dt]
                b.A[name] = nc.dram_tensor(name, list(shape), mdt,
                                           kind="ExternalInput").ap()
            b.out_ap = nc.dram_tensor("out", [T, C], F32,
                                      kind="ExternalOutput").ap()
            b.tap_names = list(taps)
            build_body(b, layers)
    return nc


def build_body(b, layers):
    nc = b.nc
    A = b.A

    xres = b.sb("xres", [128, NT_T * 1024], F32)
    b.sb("xnT", [128, NT_C * 2048], BF16)
    b.sb("sT", [128, NT_C * 2048], BF16)
    b.sb("accT", [128, NT_C * 2048], BF16)
    b.sb("stat_m", [128, NT_T], F32)
    b.sb("stat_r", [128, NT_T], F32)
    b.sb("c_id128", [128, 128], BF16)
    b.sb("c_idf", [128, 128], F32)
    b.sb("c_ones128", [128, 128], BF16)
    b.sb("c_BB", [128, 128], BF16)
    b.sb("c_Hbc", [16, 1024], BF16)
    b.sb("c_H2", [2, 128], BF16)
    b.sb("c_del15", [32, 128], BF16)
    b.sb("c_del31", [32, 128], BF16)
    b.sb("e15s", [128, 16], F32)
    b.sb("e31s", [128, 16], F32)
    cfc = b.sb("c_fc", [128, 2], F32)
    b.sb("c_onesf", [65, 128], F32)
    nc.const_aps.register(F32, EPS_LN, cfc[:, 0:1])
    nc.const_aps.register(F32, 1.0, cfc[:, 1:2])

    b.pb = [b.ps(f"pb{i}", [128, 512], F32) for i in range(6)]
    b.pbf = b.ps("pbf", [128, 1024], BF16)
    b.psm = b.ps("psm", [128, 512], F32)

    for name in TRANS_STAGE:
        shp = TRANS_W[name]
        b.wbf[name] = nc.dram_tensor("wbf_" + name, [shp[1], shp[0]], BF16,
                                     kind="Internal").ap()
    b.toep = nc.dram_tensor("toep_dram", [16, 3 * 16384], F32,
                            kind="Internal").ap()

    xin = A["x_in"].rearrange("(i p) c -> p i c", p=128)
    b.op("sync", lambda s: s.dma_start(
        out=xres[:].rearrange("p (i c) -> p i c", c=1024), in_=xin),
        writes=("xres",))
    for name in TRANS_STAGE:
        src = A[name + "_T"]
        b.op("gpsimd", lambda g, s=src, n=name: g.dma_start(out=b.wbf[n], in_=s),
             writes=("wbf_" + name,), dma=True, stage_dma=True)
    for dst, src in (("c_id128", A["id128"]), ("c_ones128", A["ones128"]),
                     ("c_BB", A["BB"]),
                     ("c_Hbc", A["Hbc"]), ("c_H2", A["H2"]),
                     ("c_del15", A["del15"]),
                     ("c_del31", A["del31"])):
        b.op("sync", lambda s, d=dst, sr=src: s.dma_start(out=b.bufs[d][:], in_=sr),
             writes=(dst,))
    b.op("gpsimd", lambda g: g.dma_start(out=b.bufs["c_idf"][:], in_=A["id128"]),
         writes=("c_idf",), dma=True)
    b.op("sync", lambda s: s.dma_start(out=cfc[:], in_=A["fconsts"]),
         writes=("c_fc",))
    b.op("sync", lambda s: s.dma_start(out=b.bufs["c_onesf"][:], in_=A["onesf"]),
         writes=("c_onesf",))

    if getattr(b, "stage", None) == "toep":
        with contextlib.ExitStack() as tst:
            emit_toep(b, tst)
            b.barrier()

    emitters = {0: emit_layer0, 1: emit_layer1, 2: emit_layer2, 3: emit_layer3}
    if getattr(b, "stage", None) in ("pass", "toep"):
        layers = ()
    b._need_toep = 2 in layers
    for L in layers:
        if L == 2 and b._need_toep:
            with contextlib.ExitStack() as tst:
                emit_toep(b, tst)
                b.barrier()
            b._need_toep = False
        emit_ln(b, L)
        b.barrier()
        if getattr(b, "stage", None) == "ln":
            break
        emitters[L](b)
        b.barrier()

    oout = b.out_ap.rearrange("(i p) c -> p i c", p=128)
    b.op("sync", lambda s: s.dma_start(
        out=oout, in_=xres[:].rearrange("p (i c) -> p i c", c=1024)),
        reads=("xres",))
    for tn in b.tap_names:
        buf = b.bufs[tn]
        dr = nc.dram_tensor("tap_" + tn, [buf.shape[0], buf.shape[1]],
                            buf.dtype, kind="ExternalOutput").ap()
        b.op("sync", lambda s, dr=dr, buf=buf: s.dma_start(out=dr, in_=buf[:]),
             reads=(tn,))
    me = b

    def fin(s):
        for sm in ("dstage",) + DSYNC + DPOOL:
            if me.cnt[sm]:
                s.wait_ge(me.sem[sm], me.cnt[sm])
    b.blk.sync(fin)


def emit_ln(b, L):
    nc = b.nc
    xres, xnT = b.bufs["xres"], b.bufs["xnT"]
    stat_m, stat_r = b.bufs["stat_m"], b.bufs["stat_r"]
    pbf = b.pbf
    with contextlib.ExitStack() as st:
        ssum = st.enter_context(nc.sbuf_tensor(f"ln{L}_sum", [128, NT_T], F32))
        ssq = st.enter_context(nc.sbuf_tensor(f"ln{L}_ssq", [128, NT_T], F32))
        scr = st.enter_context(nc.sbuf_tensor(f"ln{L}_scr", [128, 1024], F32))
        svar = st.enter_context(nc.sbuf_tensor(f"ln{L}_var", [128, NT_T], F32))
        xtk = st.enter_context(nc.sbuf_tensor(f"ln{L}_xtk", [128, NT_T * 1024], BF16))

        b.multi("vector", [
            (lambda v, i=i: v.reduce_sum(
                ssum[:, i:i + 1], xres[:, i * 1024:(i + 1) * 1024], axis=AX.X),
             ("xres",), ("ln_sum",))
            for i in range(NT_T)])
        b.multi("scalar", [
            (lambda a, i=i: a.activation(
                scr[:], xres[:, i * 1024:(i + 1) * 1024], AF.Square,
                accum_out=ssq[:, i:i + 1]),
             ("xres",), ("ln_ssq", "ln_scr"))
            for i in range(NT_T)])
        b.op("vector", lambda v: v.tensor_scalar_mul(stat_m[:], ssum[:], 1.0 / 1024),
             reads=("ln_sum",), writes=("stat_m",))
        b.op("vector", lambda v: v.tensor_mul(svar[:], stat_m[:], stat_m[:]),
             reads=("stat_m",), writes=("ln_var",))
        b.op("vector", lambda v: v.scalar_tensor_tensor(
            svar[:], ssq[:], 1.0 / 1024, svar[:], ALU.mult, ALU.subtract),
            reads=("ln_ssq", "ln_var"), writes=("ln_var",))
        b.op("scalar", lambda a: a.activation(stat_r[:], svar[:], AF.Ln, bias=EPS_LN),
             reads=("ln_var", "c_fc"), writes=("stat_r",))
        b.op("scalar", lambda a: a.activation(stat_r[:], stat_r[:], AF.Exp, scale=-0.5),
             reads=("stat_r",), writes=("stat_r",))
        b.multi("vector", [
            (lambda v, i=i: v.tensor_scalar(
                xtk[:, i * 1024:(i + 1) * 1024], xres[:, i * 1024:(i + 1) * 1024],
                stat_m[:, i:i + 1], stat_r[:, i:i + 1], ALU.subtract, ALU.mult),
             ("xres", "stat_m", "stat_r"), (f"ln_xtk{i}",))
            for i in range(NT_T)])
        for i in range(NT_T):
            def tr(t, i=i):
                last = None
                for k in range(8):
                    last = t.transpose(
                        pbf[:, k * 128:(k + 1) * 128],
                        xtk[:, i * 1024 + k * 128:i * 1024 + (k + 1) * 128],
                        b.bufs["c_id128"][:])
                return last
            b.op("tensor", tr, reads=(f"ln_xtk{i}", "c_id128"), writes=("pbf",))

            def ev(v, i=i):
                last = None
                for k in range(8):
                    last = v.tensor_copy(
                        xnT[:, k * 2048 + i * 128:k * 2048 + (i + 1) * 128],
                        pbf[:, k * 128:(k + 1) * 128])
                return last
            b.op("vector", ev, reads=("pbf",), writes=("xnT",))


def emit_toep(b, st):
    nc = b.nc
    A = b.A
    psm = b.psm
    if True:
        remb = st.enter_context(nc.sbuf_tensor("t5_remb", [32, 16], BF16))
        ohc = st.enter_context(nc.sbuf_tensor("t5_oh", [32, 2 * 512], BF16))
        ev = st.enter_context(nc.sbuf_tensor("t5_ev", [16, 2 * 512], F32))
        b.op("gpsimd", lambda g: g.dma_start(out=remb[:], in_=A["L2_attn_rel_emb"]),
             writes=("t5_remb",), dma=True)
        for j in range(96):
            par = j % 2
            ohs = ohc[:, par * 512:(par + 1) * 512]
            evs = ev[:, par * 512:(par + 1) * 512]
            b.op("sync", lambda s, j=j, ohs=ohs: s.dma_start(
                out=ohs, in_=A["OH3"][:, j * 512:(j + 1) * 512]),
                writes=(f"t5_oh{par}",))
            b.op("tensor", lambda t, ohs=ohs: t.matmul(
                psm[0:16, :], remb[:], ohs, start=True, stop=True),
                reads=(f"t5_oh{par}", "t5_remb"), writes=("psm",))
            b.op("vector", lambda v, evs=evs: v.tensor_copy(evs, psm[0:16, :]),
                 reads=("psm",), writes=(f"t5_ev{par}",))
            b.op("gpsimd", lambda g, j=j, evs=evs: g.dma_start(
                out=b.toep[:, j * 512:(j + 1) * 512], in_=evs),
                reads=(f"t5_ev{par}",), writes=("toep",), dma=True)
        for dn, dst in (("c_del15", "e15s"), ("c_del31", "e31s")):
            b.op("tensor", lambda t, dn=dn: t.matmul(
                psm[:, 0:16], b.bufs[dn][:], remb[:], start=True, stop=True),
                reads=(dn, "t5_remb"), writes=("psm",))
            b.op("vector", lambda v, dst=dst: v.tensor_copy(
                b.bufs[dst][:], psm[:, 0:16]), reads=("psm",), writes=(dst,))


def emit_topk_gates(b, st, tag, logitsT_buf, logits_reg, E, renorm, eps,
                    coT_buf=None):
    nc = b.nc
    co = st.enter_context(nc.sbuf_tensor(f"{tag}_co", [128, 2 * 32], F32))
    m1 = st.enter_context(nc.sbuf_tensor(f"{tag}_m1", [128, 8], F32))
    if coT_buf is None:
        coT_buf = st.enter_context(nc.sbuf_tensor(f"{tag}_coT", [E, 2048], BF16))
    coT_reg = f"{tag}_coT"
    lg = st.enter_context(nc.sbuf_tensor(f"{tag}_lg", [128, 2 * 32], F32))
    psm = b.psm
    for i in range(NT_T):
        par = i % 2
        lgs = lg[:, par * 32:par * 32 + E]
        cos = co[:, par * 32:par * 32 + E]
        b.op("tensor", lambda t, i=i: t.transpose(
            psm[:, 0:E], logitsT_buf[0:E, i * 128:(i + 1) * 128],
            b.bufs["c_idf"][0:E, 0:E]),
            reads=(logits_reg, "c_idf"), writes=("psm",))
        b.multi("vector", [
            (lambda v, lgs=lgs: v.tensor_copy(lgs, psm[:, 0:E]),
             ("psm",), (f"{tag}_lg{par}",)),
            (lambda v, lgs=lgs: v.reduce_max(
                m1[:, 0:1], lgs, axis=AX.X, negate=True),
             (f"{tag}_lg{par}",), (f"{tag}_m1",))])
        b.op("scalar", lambda a, lgs=lgs, cos=cos: a.activation(
            cos, lgs, AF.Exp, bias=m1[:, 0:1], accum_out=m1[:, 1:2]),
            reads=(f"{tag}_lg{par}", f"{tag}_m1"),
            writes=(f"{tag}_co{par}", f"{tag}_m1d"))
        items2 = [
            (lambda v: v.reciprocal(m1[:, 1:2], m1[:, 1:2]),
             (f"{tag}_m1d",), (f"{tag}_m1d",)),
            (lambda v, cos=cos: v.tensor_scalar_mul(cos, cos, m1[:, 1:2]),
             (f"{tag}_co{par}", f"{tag}_m1d"), (f"{tag}_co{par}",)),
            (lambda v, cos=cos: v.reduce_max(m1[:, 2:3], cos, axis=AX.X),
             (f"{tag}_co{par}",), (f"{tag}_m2",)),
            (lambda v, cos=cos, lgs=lgs: v.tensor_scalar(
                lgs, cos, m1[:, 2:3], None, ALU.not_equal),
             (f"{tag}_co{par}", f"{tag}_m2"), (f"{tag}_lg{par}",)),
            (lambda v, cos=cos, lgs=lgs: v.tensor_mul(lgs, lgs, cos),
             (f"{tag}_co{par}", f"{tag}_lg{par}"), (f"{tag}_lg{par}",)),
            (lambda v, lgs=lgs: v.reduce_max(m1[:, 3:4], lgs, axis=AX.X),
             (f"{tag}_lg{par}",), (f"{tag}_m3",)),
            (lambda v, cos=cos, lgs=lgs: v.tensor_scalar(
                lgs, cos, m1[:, 3:4], None, ALU.is_ge),
             (f"{tag}_co{par}", f"{tag}_m3"), (f"{tag}_lg{par}",)),
            (lambda v, cos=cos, lgs=lgs: v.tensor_mul(cos, cos, lgs),
             (f"{tag}_co{par}", f"{tag}_lg{par}"), (f"{tag}_co{par}",)),
        ]
        if renorm:
            items2 += [
                (lambda v: v.tensor_add(m1[:, 4:5], m1[:, 2:3], m1[:, 3:4]),
                 (f"{tag}_m2", f"{tag}_m3"), (f"{tag}_m4",)),
                (lambda v: v.tensor_scalar_add(m1[:, 4:5], m1[:, 4:5], eps),
                 (f"{tag}_m4",), (f"{tag}_m4",)),
                (lambda v: v.reciprocal(m1[:, 4:5], m1[:, 4:5]),
                 (f"{tag}_m4",), (f"{tag}_m4",)),
                (lambda v, cos=cos: v.tensor_scalar_mul(cos, cos, m1[:, 4:5]),
                 (f"{tag}_co{par}", f"{tag}_m4"), (f"{tag}_co{par}",)),
            ]
        b.multi("vector", items2)
        b.op("tensor", lambda t, cos=cos: t.transpose(
            psm[0:E, 0:128], cos, b.bufs["c_idf"][:]),
            reads=(f"{tag}_co{par}", "c_idf"), writes=("psm",))
        b.op("vector", lambda v, i=i: v.tensor_copy(
            coT_buf[0:E, i * 128:(i + 1) * 128], psm[0:E, 0:128]),
            reads=("psm",), writes=(coT_reg,))
    return coT_buf, coT_reg


def emit_combine(b, L):
    xres, xnT, sT, accT = (b.bufs["xres"], b.bufs["xnT"], b.bufs["sT"],
                           b.bufs["accT"])
    pbf = b.pbf
    b.multi("scalar", [
        (lambda a, j=j: a.activation(
            xnT[:, j * 2048:(j + 1) * 2048], accT[:, j * 2048:(j + 1) * 2048],
            AF.Silu),
         ("accT",), ("xnT",))
        for j in range(8)])
    b.multi("vector", [
        (lambda v, sl=slice(j * 2048, (j + 1) * 2048): v.tensor_mul(
            xnT[:, sl], xnT[:, sl], sT[:, sl]),
         ("xnT", "sT"), ("xnT",))
        for j in range(8)])
    for i in range(NT_T):
        def tr(t, i=i):
            last = None
            for j in range(8):
                last = t.transpose(
                    pbf[:, j * 128:(j + 1) * 128],
                    xnT[:, j * 2048 + i * 128:j * 2048 + (i + 1) * 128],
                    b.bufs["c_id128"][:])
            return last
        b.op("tensor", tr, reads=("xnT", "c_id128"), writes=("pbf",))

        def ad(v, i=i):
            last = None
            for j in range(8):
                last = v.tensor_add(
                    xres[:, i * 1024 + j * 128:i * 1024 + (j + 1) * 128],
                    xres[:, i * 1024 + j * 128:i * 1024 + (j + 1) * 128],
                    pbf[:, j * 128:(j + 1) * 128])
            return last
        b.op("vector", ad, reads=("pbf", "xres"), writes=("xres",))


def emit_layer0(b):
    nc = b.nc
    A = b.A
    xnT, sT, accT = b.bufs["xnT"], b.bufs["sT"], b.bufs["accT"]
    pb, pbf, psm = b.pb, b.pbf, b.psm

    with contextlib.ExitStack() as st:
        att = accT
        w3 = st.enter_context(nc.sbuf_tensor("l0_w3", [128, 3 * 1024], BF16))
        qe = st.enter_context(nc.sbuf_tensor("l0_qe", [128, 2048], BF16))
        ke = st.enter_context(nc.sbuf_tensor("l0_ke", [128, 2048], BF16))
        vt = st.enter_context(nc.sbuf_tensor("l0_vt", [128, 2048], BF16))
        ktok = st.enter_context(nc.sbuf_tensor("l0_ktok", [128, 2048], BF16))
        vtok = st.enter_context(nc.sbuf_tensor("l0_vtok", [128, 2048], BF16))
        dvb = st.enter_context(nc.sbuf_tensor("l0_dvb", [128, 4 * 512], F32))
        ksum = st.enter_context(nc.sbuf_tensor("l0_ksum", [128, 1], F32))
        gkb = st.enter_context(nc.sbuf_tensor("l0_gkb", [128, 128], BF16))
        ctxs = st.enter_context(nc.sbuf_tensor("l0_ctx", [128, 128], BF16))
        rowb = st.enter_context(nc.sbuf_tensor("l0_rowb", [1, 128], F32))
        bq, bq_r = bias_cols(b, st, "l0_bq", A["L0_attn_bq"], 1024, rowb)
        bk, bk_r = bias_cols(b, st, "l0_bk", A["L0_attn_bk"], 1024, rowb)
        bv, bv_r = bias_cols(b, st, "l0_bv", A["L0_attn_bv"], 1024, rowb)
        bp, bp_r = bias_cols(b, st, "l0_bp", A["L0_attn_bp"], 1024, rowb)

        for ct in range(8):
            for wi, nm in enumerate(("Wq", "Wk", "Wv")):
                b.op("gpsimd", lambda s, wi=wi, nm=nm, ct=ct: wT_load(
                    s, b, w3[:, wi * 1024:(wi + 1) * 1024], f"L0_attn_{nm}", ct),
                    writes=(f"l0_w3_{wi}",), dma=True)
            for wi, (dst, dreg, bcol, breg, actf) in enumerate((
                    (qe, "l0_qe", bq, bq_r, AF.Exp),
                    (ke, "l0_ke", bk, bk_r, AF.Exp),
                    (vt, "l0_vt", bv, bv_r, None))):
                for tb in range(NTB):
                    pairs = [(w3[:, wi * 1024 + k * 128:wi * 1024 + (k + 1) * 128],
                              xnT[:, k * 2048 + tb * 512:k * 2048 + (tb + 1) * 512])
                             for k in range(8)]
                    mm_chain(b, pb[tb][:], pairs, reads=(f"l0_w3_{wi}", "xnT"),
                             wbank=f"pb{tb}")
                    if actf is None:
                        b.op("vector", lambda v, dst=dst, tb=tb, bcol=bcol,
                             ct=ct: v.tensor_scalar_add(
                            dst[:, tb * 512:(tb + 1) * 512], pb[tb][:],
                            bcol[:, ct:ct + 1]),
                            reads=(f"pb{tb}", breg), writes=(dreg,))
                if actf is not None:
                    b.multi("scalar", [
                        (lambda a, dst=dst, tb=tb, bcol=bcol, actf=actf,
                         ct=ct: a.activation(
                            dst[:, tb * 512:(tb + 1) * 512], pb[tb][:], actf,
                            bias=bcol[:, ct:ct + 1]),
                         (f"pb{tb}", breg), (dreg,))
                        for tb in range(NTB)])
            for qi, (srcb, sreg) in enumerate(((qe, "l0_qe"), (ke, "l0_ke"))):
                b.multi("tensor", [
                    (lambda t, tb=tb, srcb=srcb: t.matmul(
                        pb[tb][:], b.bufs["c_BB"][:],
                        srcb[:, tb * 512:(tb + 1) * 512],
                        start=True, stop=True),
                     ("c_BB", sreg), (f"pb{tb}",))
                    for tb in range(NTB)])
                b.multi("scalar",
                        [(lambda a, tb=tb: a.activation(
                            dvb[:, tb * 512:(tb + 1) * 512], pb[tb][:], AF.Ln),
                          (f"pb{tb}",), ("l0_dvb",)) for tb in range(NTB)]
                        + [(lambda a, tb=tb: a.activation(
                            dvb[:, tb * 512:(tb + 1) * 512],
                            dvb[:, tb * 512:(tb + 1) * 512], AF.Exp, scale=-1.0),
                            ("l0_dvb",), ("l0_dvb",)) for tb in range(NTB)])
                b.multi("gpsimd", [
                    (lambda g, srcb=srcb, tb=tb: g.tensor_mul(
                        srcb[:, tb * 512:(tb + 1) * 512],
                        srcb[:, tb * 512:(tb + 1) * 512],
                        dvb[:, tb * 512:(tb + 1) * 512]),
                     ("l0_dvb", sreg), (sreg,))
                    for tb in range(NTB)])
            b.op("vector", lambda v: v.reduce_sum(ksum[:], ke[:], axis=AX.X),
                 reads=("l0_ke",), writes=("l0_ksum",))
            b.op("vector", lambda v: v.tensor_scalar_mul(
                gkb[:], b.bufs["c_BB"][:], ksum[:, 0:1]),
                reads=("c_BB", "l0_ksum"), writes=("l0_gkb",))
            for srcb, sreg, dstb, dreg in ((ke, "l0_ke", ktok, "l0_ktok"),
                                           (vt, "l0_vt", vtok, "l0_vtok")):
                for h in range(2):
                    def tr(t, srcb=srcb, h=h):
                        last = None
                        for tt in range(8):
                            i = h * 8 + tt
                            last = t.transpose(
                                pbf[:, tt * 128:(tt + 1) * 128],
                                srcb[:, i * 128:(i + 1) * 128],
                                b.bufs["c_id128"][:])
                        return last
                    b.op("tensor", tr, reads=(sreg, "c_id128"), writes=("pbf",))

                    def evv(v, dstb=dstb, h=h):
                        last = None
                        for tt in range(8):
                            i = h * 8 + tt
                            last = v.tensor_copy(dstb[:, i * 128:(i + 1) * 128],
                                                 pbf[:, tt * 128:(tt + 1) * 128])
                        return last
                    b.op("vector", evv, reads=("pbf",), writes=(dreg,))
            mm_chain(b, pb[3][:, 0:128],
                     [(ktok[:, i * 128:(i + 1) * 128], vtok[:, i * 128:(i + 1) * 128])
                      for i in range(16)],
                     reads=("l0_ktok", "l0_vtok"), wbank="pb3")
            b.op("vector", lambda v: v.tensor_copy(ctxs[:], pb[3][:, 0:128]),
                 reads=("pb3",), writes=("l0_ctxs",))
            dbk = (5, 2, 1, 0)
            b.multi("tensor", [
                (lambda t, tb=tb: t.matmul(
                    pb[dbk[tb]][:], gkb[:], qe[:, tb * 512:(tb + 1) * 512],
                    start=True, stop=True),
                 ("l0_gkb", "l0_qe"), (f"pb{dbk[tb]}",))
                for tb in range(NTB)])
            b.multi("scalar",
                    [(lambda a, tb=tb: a.activation(
                        dvb[:, tb * 512:(tb + 1) * 512], pb[dbk[tb]][:], AF.Ln),
                      (f"pb{dbk[tb]}",), ("l0_dvb",)) for tb in range(NTB)]
                    + [(lambda a, tb=tb: a.activation(
                        dvb[:, tb * 512:(tb + 1) * 512],
                        dvb[:, tb * 512:(tb + 1) * 512], AF.Exp, scale=-1.0),
                        ("l0_dvb",), ("l0_dvb",)) for tb in range(NTB)])
            for tb in range(NTB):
                def mm2(t, tb=tb):
                    t.matmul(pb[4][0:64, :], ctxs[0:64, 0:64],
                             qe[0:64, tb * 512:(tb + 1) * 512], start=True, stop=True)
                    return t.matmul(pb[4][64:128, :], ctxs[64:128, 64:128],
                                    qe[64:128, tb * 512:(tb + 1) * 512],
                                    start=True, stop=True)
                b.op("tensor", mm2, reads=("l0_ctxs", "l0_qe"), writes=("pb4",))
                sl = slice(ct * 2048 + tb * 512, ct * 2048 + (tb + 1) * 512)
                b.op("vector", lambda v, sl=sl, tb=tb: v.tensor_mul(
                    att[:, sl], pb[4][:], dvb[:, tb * 512:(tb + 1) * 512]),
                    reads=("pb4", "l0_dvb"), writes=("accT",))
                b.op("vector", lambda v, sl=sl, tb=tb: v.tensor_add(
                    att[:, sl], att[:, sl], qe[:, tb * 512:(tb + 1) * 512]),
                    reads=("accT", "l0_qe"), writes=("accT",))
        for j in range(8):
            b.op("gpsimd", lambda s, j=j: wT_load(
                s, b, w3[:, 0:1024], "L0_attn_Wp", j),
                writes=("l0_w3_0",), dma=True)
            for tb in range(NTB):
                bank = pb[tb % 2]
                pairs = [(w3[:, k * 128:(k + 1) * 128],
                          att[:, k * 2048 + tb * 512:k * 2048 + (tb + 1) * 512])
                         for k in range(8)]
                mm_chain(b, bank[:], pairs, reads=("l0_w3_0", "accT"),
                         wbank=f"pb{tb % 2}")
                b.op("vector", lambda v, j=j, tb=tb, bank=bank: v.tensor_scalar_add(
                    sT[:, j * 2048 + tb * 512:j * 2048 + (tb + 1) * 512], bank[:],
                    bp[:, j:j + 1]),
                    reads=(f"pb{tb % 2}", bp_r), writes=("sT",))
    b.barrier()
    if getattr(b, "stage", None) == "attn":
        return
    emit_l0_ffn(b)
    if getattr(b, "stage", None) == "ffn":
        return
    emit_combine(b, 0)


def emit_l0_ffn(b):
    nc = b.nc
    A = b.A
    xnT, accT = b.bufs["xnT"], b.bufs["accT"]
    pb, psm = b.pb, b.psm
    E = 16
    with contextlib.ExitStack() as st:
        ish = st.enter_context(nc.sbuf_tensor("l0f_ish", [128, 8 * 512], BF16))
        coT = st.enter_context(nc.sbuf_tensor("l0f_coT", [16, 2048], BF16))
        with contextlib.ExitStack() as stg:
            lgT = stg.enter_context(nc.sbuf_tensor("l0f_lgT", [16, 2048], F32))
            wrt = stg.enter_context(nc.sbuf_tensor("l0f_wrt", [128, 8 * 16], BF16))
            b.op("gpsimd", lambda g: wN_load(
                g, b, wrt[:], "L0_ffn_router_w", (0, 8), (0, 16)),
                writes=("l0f_wrt",), dma=True)
            for tb in range(NTB):
                pairs = [(wrt[:, k * 16:(k + 1) * 16],
                          xnT[:, k * 2048 + tb * 512:k * 2048 + (tb + 1) * 512])
                         for k in range(8)]
                mm_chain(b, psm[0:16, :], pairs, reads=("l0f_wrt", "xnT"), wbank="psm")
                b.op("vector", lambda v, tb=tb: v.tensor_copy(
                    lgT[:, tb * 512:(tb + 1) * 512], psm[0:16, :]),
                    reads=("psm",), writes=("l0f_lgT",))
            coT, coreg = emit_topk_gates(b, stg, "l0f", lgT, "l0f_lgT", E,
                                         renorm=False, eps=0.0, coT_buf=coT)
            b.barrier()

        st2 = st.enter_context(contextlib.ExitStack())
        if getattr(b, "_need_toep", False):
            emit_toep(b, st2)
            b._need_toep = False
        wdns = [st2.enter_context(nc.sbuf_tensor(f"l0f_wdn{h}", [128, 1024], BF16))
                for h in range(2)]
        wgu = st2.enter_context(nc.sbuf_tensor("l0f_wgu", [128, 2 * 2048], BF16))
        sgS = st2.enter_context(nc.sbuf_tensor("l0f_sg", [128, 4 * 512], BF16))
        for tb in range(NTB):
            for g4 in range(2):
                for j4 in range(4):
                    j = g4 * 4 + j4
                    par = j % 2
                    wg_s = wgu[:, par * 2048:par * 2048 + 1024]
                    b.op("gpsimd", lambda g, j=j, wg_s=wg_s: wN_load(
                        g, b, wg_s, "L0_ffn_shared_gate_w", (0, 8), (j * 128, 128)),
                        writes=(f"l0f_wgu{par}",), dma=True)
                    pairs = [(wg_s[:, k * 128:(k + 1) * 128],
                              xnT[:, k * 2048 + tb * 512:k * 2048 + (tb + 1) * 512])
                             for k in range(8)]
                    mm_chain(b, pb[j4][:], pairs, reads=(f"l0f_wgu{par}", "xnT"),
                             wbank=f"pb{j4}")
                b.multi("scalar", [
                    (lambda a, j4=j4: a.activation(
                        sgS[:, j4 * 512:(j4 + 1) * 512], pb[j4][:], AF.Silu),
                     (f"pb{j4}",), (f"l0f_sg{j4}",))
                    for j4 in range(4)])
                for j4 in range(4):
                    j = g4 * 4 + j4
                    par = j % 2
                    wu_s = wgu[:, par * 2048 + 1024:par * 2048 + 2048]
                    b.op("gpsimd", lambda g, j=j, wu_s=wu_s: wN_load(
                        g, b, wu_s, "L0_ffn_shared_up_w", (0, 8), (j * 128, 128)),
                        writes=(f"l0f_wguu{par}",), dma=True)
                    pairs = [(wu_s[:, k * 128:(k + 1) * 128],
                              xnT[:, k * 2048 + tb * 512:k * 2048 + (tb + 1) * 512])
                             for k in range(8)]
                    mm_chain(b, pb[j4][:], pairs, reads=(f"l0f_wguu{par}", "xnT"),
                             wbank=f"pb{j4}")
                    b.op("vector", lambda v, j=j, j4=j4: v.tensor_mul(
                        ish[:, j * 512:(j + 1) * 512], pb[j4][:],
                        sgS[:, j4 * 512:(j4 + 1) * 512]),
                        reads=(f"pb{j4}", f"l0f_sg{j4}"), writes=(f"l0f_ish{j}",))
            for hc in range(8):
                bank = pb[4 + hc % 2]
                slot, reg = wdns[hc % 2], f"l0f_wdn{hc % 2}"
                b.op("gpsimd", lambda g, hc=hc, slot=slot: wN_load(
                    g, b, slot[:], "L0_ffn_shared_down_w",
                    (0, 8), (hc * 128, 128)),
                    writes=(reg,), dma=True)
                pairs = [(slot[:, k * 128:(k + 1) * 128],
                          ish[:, k * 512:(k + 1) * 512]) for k in range(8)]
                mm_chain(b, bank[:], pairs,
                         reads=tuple([reg] + [f"l0f_ish{k}" for k in range(8)]),
                         wbank=f"pb{4 + hc % 2}")
                b.op("vector", lambda v, hc=hc, tb=tb, bank=bank: v.tensor_copy(
                    accT[:, hc * 2048 + tb * 512:hc * 2048 + (tb + 1) * 512],
                    bank[:]),
                    reads=(f"pb{4 + hc % 2}",), writes=("accT",))
        b.barrier()
        st.callback(lambda: None)
        st2.close()

        wge = st.enter_context(nc.sbuf_tensor("l0f_wge", [128, 8 * 512], BF16))
        wue = st.enter_context(nc.sbuf_tensor("l0f_wue", [128, 8 * 512], BF16))
        wde = st.enter_context(nc.sbuf_tensor("l0f_wde", [128, 4 * 1024], BF16))
        cbc = st.enter_context(nc.sbuf_tensor("l0f_cbc", [128, 512], F32))
        rsb = st.enter_context(nc.sbuf_tensor("l0f_rsb", [20, 128], BF16))
        sg4 = st.enter_context(nc.sbuf_tensor("l0f_sg4", [128, 4 * 512], BF16))
        for e in range(E):
            b.op("gpsimd", lambda g, e=e: wN_load(
                g, b, wde[:], "L0_ffn_down_w", (e * 512, 4), (0, 1024)),
                writes=("l0f_wde",), dma=True)
            b.op("gpsimd", lambda g, e=e: wN_load(
                g, b, wge[:], "L0_ffn_gate_w", (e * 1024, 8), (0, 512)),
                writes=("l0f_wge",), dma=True)
            b.op("gpsimd", lambda g, e=e: wN_load(
                g, b, wue[:], "L0_ffn_up_w", (e * 1024, 8), (0, 512)),
                writes=("l0f_wue",), dma=True)
            b.op("sync", lambda s, e=e: s.dma_start(
                out=rsb[0:E, :], in_=b.A["rowsel"][0:E, e * 128:(e + 1) * 128]),
                writes=("rsb",))
            for tb in range(NTB):
                mm_chain(b, psm[:],
                         [(rsb[0:E, 0:128],
                           coT[0:E, tb * 512:(tb + 1) * 512])],
                         reads=("rsb", coreg), wbank="psm")
                b.op("vector", lambda v: v.tensor_copy(cbc[:], psm[:]),
                     reads=("psm",), writes=("l0f_cbc",))
                for jj in range(4):
                    pairs = [(wge[:, k * 512 + jj * 128:k * 512 + (jj + 1) * 128],
                              xnT[:, k * 2048 + tb * 512:k * 2048 + (tb + 1) * 512])
                             for k in range(8)]
                    mm_chain(b, pb[jj][:], pairs, reads=("l0f_wge", "xnT"),
                             wbank=f"pb{jj}")
                b.multi("scalar", [
                    (lambda a, jj=jj: a.activation(
                        sg4[:, jj * 512:(jj + 1) * 512], pb[jj][:], AF.Silu),
                     (f"pb{jj}",), (f"l0f_sg4{jj}",))
                    for jj in range(4)])
                for jj in range(4):
                    b.op("gpsimd", lambda g, jj=jj: g.tensor_mul(
                        sg4[:, jj * 512:(jj + 1) * 512],
                        sg4[:, jj * 512:(jj + 1) * 512], cbc[:]),
                        reads=(f"l0f_sg4{jj}", "l0f_cbc"), writes=(f"l0f_sg4{jj}",))
                for jj in range(4):
                    pairs = [(wue[:, k * 512 + jj * 128:k * 512 + (jj + 1) * 128],
                              xnT[:, k * 2048 + tb * 512:k * 2048 + (tb + 1) * 512])
                             for k in range(8)]
                    mm_chain(b, pb[jj][:], pairs, reads=("l0f_wue", "xnT"),
                             wbank=f"pb{jj}")
                    b.op("vector", lambda v, jj=jj: v.tensor_mul(
                        ish[:, jj * 512:(jj + 1) * 512], pb[jj][:],
                        sg4[:, jj * 512:(jj + 1) * 512]),
                        reads=(f"pb{jj}", f"l0f_sg4{jj}"), writes=(f"l0f_ish{jj}",))
                for hc in range(8):
                    bank = pb[4 + hc % 2]
                    pairs = [(wde[:, k * 1024 + hc * 128:k * 1024 + (hc + 1) * 128],
                              ish[:, k * 512:(k + 1) * 512]) for k in range(4)]
                    mm_chain(b, bank[:], pairs,
                             reads=tuple(["l0f_wde"] + [f"l0f_ish{k}" for k in range(4)]),
                             wbank=f"pb{4 + hc % 2}")
                    sl = slice(hc * 2048 + tb * 512, hc * 2048 + (tb + 1) * 512)
                    b.op("vector", lambda v, sl=sl, bank=bank: v.tensor_add(
                        accT[:, sl], accT[:, sl], bank[:]),
                        reads=(f"pb{4 + hc % 2}", "accT"), writes=("accT",))
    b.barrier()


def emit_layer1(b):
    nc = b.nc
    A = b.A
    xnT, sT = b.bufs["xnT"], b.bufs["sT"]
    pb, psm = b.pb, b.psm

    with contextlib.ExitStack() as st:
        wdt = st.enter_context(nc.sbuf_tensor("l1_wdt", [128, 1024], BF16))
        abc = st.enter_context(nc.sbuf_tensor("l1_abc", [128, 33], F32))
        dt = st.enter_context(nc.sbuf_tensor("l1_dt", [128, 2048], F32))
        du = st.enter_context(nc.sbuf_tensor("l1_du", [128, 1024], F32))
        aas = [st.enter_context(nc.sbuf_tensor(f"l1_aa{h}", [128, 1024], F32))
               for h in range(4)]
        scr = aas[2]
        sss = [st.enter_context(nc.sbuf_tensor(f"l1_ss{h}", [128, 1024], F32))
               for h in range(2)]
        yy = st.enter_context(nc.sbuf_tensor("l1_yy", [128, 1024], F32))
        sscar = st.enter_context(nc.sbuf_tensor("l1_sscar", [128, 16], F32))
        bdt, bdt_r = bias_cols(b, st, "l1_bdt", A["L1_attn_b_dt"], 1024)
        for j in range(8):
            b.op("sync", lambda s, j=j: s.dma_start(
                out=abc[:, 0:16], in_=A["L1_attn_A_log"][j * 128:(j + 1) * 128, :]),
                writes=("l1_abcA",))
            b.op("sync", lambda s, j=j: s.dma_start(
                out=abc[:, 16:32], in_=A["L1_attn_B_mat"][j * 128:(j + 1) * 128, :]),
                writes=("l1_abcB",))
            b.op("gpsimd", lambda g, j=j: g.dma_start(
                out=scr[:, 0:16], in_=A["L1_attn_C_mat"][j * 128:(j + 1) * 128, :]),
                writes=("l1_aa2",), dma=True)
            b.op("sync", lambda s, j=j: s.dma_start(
                out=abc[:, 32:33],
                in_=A["L1_attn_D_vec"][j * 128:(j + 1) * 128].rearrange("(p o) -> p o", o=1)),
                writes=("l1_abcD",))
            b.op("scalar", lambda a: a.activation(abc[:, 0:16], abc[:, 0:16], AF.Exp),
                 reads=("l1_abcA",), writes=("l1_abcA",))
            b.op("vector", lambda v: v.tensor_scalar_mul(abc[:, 0:16], abc[:, 0:16], -1.0),
                 reads=("l1_abcA",), writes=("l1_abcA",))
            b.op("vector", lambda v: v.tensor_mul(abc[:, 16:32], abc[:, 16:32], scr[:, 0:16]),
                 reads=("l1_abcB", "l1_aa2"), writes=("l1_abcB",))
            b.op("gpsimd", lambda s, j=j: wT_load(s, b, wdt[:], "L1_attn_W_dt", j),
                 writes=("l1_wdt",), dma=True)
            for tb in range(NTB):
                pairs = [(wdt[:, k * 128:(k + 1) * 128],
                          xnT[:, k * 2048 + tb * 512:k * 2048 + (tb + 1) * 512])
                         for k in range(8)]
                mm_chain(b, pb[tb % 2][:], pairs, reads=("l1_wdt", "xnT"),
                         wbank=f"pb{tb % 2}")
                b.op("vector", lambda v, tb=tb, j=j: v.tensor_scalar_add(
                    dt[:, tb * 512:(tb + 1) * 512], pb[tb % 2][:],
                    bdt[:, j:j + 1]),
                    reads=(f"pb{tb % 2}", bdt_r), writes=("l1_dt",))
            for hf in range(2):
                sl = slice(hf * 1024, (hf + 1) * 1024)
                b.multi("scalar", [
                    (lambda a, sl=sl: a.activation(scr[:], dt[:, sl], AF.Abs),
                     ("l1_dt",), ("l1_aa2",)),
                    (lambda a: a.activation(scr[:], scr[:], AF.Exp, scale=-1.0),
                     ("l1_aa2",), ("l1_aa2",)),
                    (lambda a: a.activation(scr[:], scr[:], AF.Ln, bias=1.0),
                     ("l1_aa2", "c_fc"), ("l1_aa2",))])
                b.op("vector", lambda v, sl=sl: v.tensor_relu(dt[:, sl], dt[:, sl]),
                     reads=("l1_dt",), writes=("l1_dt",))
                b.op("vector", lambda v, sl=sl: v.tensor_add(dt[:, sl], dt[:, sl], scr[:]),
                     reads=("l1_dt", "l1_aa2"), writes=("l1_dt",))
                b.op("vector", lambda v, j=j, sl=sl, hf=hf: v.tensor_mul(
                    du[:], dt[:, sl], xnT[:, j * 2048 + hf * 1024:j * 2048 + (hf + 1) * 1024]),
                    reads=("l1_dt", "xnT"), writes=("l1_du",))
                b.op("vector", lambda v, j=j, hf=hf: v.tensor_scalar_mul(
                    yy[:], xnT[:, j * 2048 + hf * 1024:j * 2048 + (hf + 1) * 1024],
                    abc[:, 32:33]),
                    reads=("xnT", "l1_abcD"), writes=("l1_yy",))
                for n2 in range(8):
                    b.multi("scalar", [
                        (lambda a, n=n2 * 2 + i, sl=sl: a.activation(
                            aas[n % 4][:], dt[:, sl], AF.Exp,
                            scale=abc[:, n:n + 1]),
                         ("l1_dt", "l1_abcA"), (f"l1_aa{(n2 * 2 + i) % 4}",))
                        for i in range(2)])
                    items = []
                    for i in range(2):
                        n = n2 * 2 + i
                        aa, ss = aas[n % 4], sss[i]
                        ai = n % 4
                        if hf == 0:
                            items.append(
                                (lambda v, aa=aa, ss=ss: v.tensor_tensor_scan(
                                    ss[:], aa[:], du[:], 0.0, ALU.mult, ALU.add),
                                 (f"l1_aa{ai}", "l1_du"), (f"l1_ss{i}",)))
                            items.append(
                                (lambda v, n=n, ss=ss: v.tensor_copy(
                                    sscar[:, n:n + 1], ss[:, 1023:1024]),
                                 (f"l1_ss{i}",), ("l1_sscar",)))
                        else:
                            items.append(
                                (lambda v, n=n, aa=aa, ss=ss:
                                 v.tensor_tensor_scan(
                                    ss[:], aa[:], du[:], sscar[:, n:n + 1],
                                    ALU.mult, ALU.add),
                                 (f"l1_aa{ai}", "l1_du", "l1_sscar"),
                                 (f"l1_ss{i}",)))
                        items.append(
                            (lambda v, n=n, ss=ss: v.scalar_tensor_tensor(
                                yy[:], ss[:], abc[:, 16 + n:17 + n], yy[:],
                                ALU.mult, ALU.add),
                             (f"l1_ss{i}", "l1_abcB", "l1_yy"), ("l1_yy",)))
                    b.multi("vector", items)
                b.op("vector", lambda v, j=j, hf=hf: v.tensor_copy(
                    sT[:, j * 2048 + hf * 1024:j * 2048 + (hf + 1) * 1024], yy[:]),
                    reads=("l1_yy",), writes=("sT",))
    b.barrier()
    emit_l1_ffn(b)
    emit_combine(b, 1)


def emit_l1_ffn(b):
    nc = b.nc
    A = b.A
    xnT, accT = b.bufs["xnT"], b.bufs["accT"]
    pb, psm = b.pb, b.psm
    E = 20
    with contextlib.ExitStack() as st:
        coT = st.enter_context(nc.sbuf_tensor("l1f_coT", [20, 2048], BF16))
        stg = st.enter_context(contextlib.ExitStack())
        wg1 = stg.enter_context(nc.sbuf_tensor("l1f_wg1", [128, 2 * 1024], BF16))
        hT = stg.enter_context(nc.sbuf_tensor("l1f_hT", [128, 2 * 2048], BF16))
        wg2 = stg.enter_context(nc.sbuf_tensor("l1f_wg2", [128, 2 * 20], BF16))
        lgT = stg.enter_context(nc.sbuf_tensor("l1f_lgT", [20, 2048], F32))
        bg2c = stg.enter_context(nc.sbuf_tensor("l1f_bg2", [20, 1], F32))
        bg1, bg1_r = bias_cols(b, stg, "l1f_bg1", A["L1_ffn_bg1"], 256)
        for j in range(2):
            b.op("gpsimd", lambda g, j=j: wN_load(
                g, b, wg1[:, j * 1024:(j + 1) * 1024], "L1_ffn_wg1",
                (0, 8), (j * 128, 128)),
                writes=("l1f_wg1",), dma=True)
        for j in range(2):
            for tb in range(NTB):
                pairs = [(wg1[:, j * 1024 + k * 128:j * 1024 + (k + 1) * 128],
                          xnT[:, k * 2048 + tb * 512:k * 2048 + (tb + 1) * 512])
                         for k in range(8)]
                mm_chain(b, pb[tb % 2][:], pairs, reads=("l1f_wg1", "xnT"),
                         wbank=f"pb{tb % 2}")
                b.op("scalar", lambda a, j=j, tb=tb: a.activation(
                    hT[:, j * 2048 + tb * 512:j * 2048 + (tb + 1) * 512],
                    pb[tb % 2][:], AF.Gelu, bias=bg1[:, j:j + 1]),
                    reads=(f"pb{tb % 2}", bg1_r), writes=("l1f_hT",))
        b.op("gpsimd", lambda g: wN_load(
            g, b, wg2[:], "L1_ffn_wg2", (0, 2), (0, 20)),
            writes=("l1f_wg2",), dma=True)
        b.op("sync", lambda s: s.dma_start(
            out=bg2c[:], in_=A["L1_ffn_bg2"][0:20].rearrange("(p o) -> p o", o=1)),
            writes=("l1f_bg2c",))
        for tb in range(NTB):
            pairs = [(wg2[:, k * 20:(k + 1) * 20],
                      hT[:, k * 2048 + tb * 512:k * 2048 + (tb + 1) * 512])
                     for k in range(2)]
            mm_chain(b, psm[0:20, :], pairs, reads=("l1f_wg2", "l1f_hT"),
                     wbank="psm")
            b.op("vector", lambda v, tb=tb: v.tensor_scalar_add(
                lgT[0:20, tb * 512:(tb + 1) * 512], psm[0:20, :],
                bg2c[0:20, 0:1]),
                reads=("psm", "l1f_bg2c"), writes=("l1f_lgT",))
        coT, coreg = emit_topk_gates(b, stg, "l1f", lgT, "l1f_lgT", E,
                                     renorm=False, eps=0.0, coT_buf=coT)
        b.barrier()
        stg.close()
        wbe = st.enter_context(nc.sbuf_tensor("l1f_wbe", [20, 1024], BF16))
        b.op("gpsimd", lambda g: g.dma_start(out=wbe[:], in_=A["L1_ffn_be"]),
             writes=("l1f_wbe",), dma=True)
        for hc in range(8):
            for tb in range(NTB):
                bank = pb[tb % 2]
                mm_chain(b, bank[:],
                         [(wbe[0:E, hc * 128:(hc + 1) * 128],
                           coT[0:E, tb * 512:(tb + 1) * 512])],
                         reads=("l1f_wbe", coreg), wbank=f"pb{tb % 2}")
                b.op("vector", lambda v, hc=hc, tb=tb, bank=bank: v.tensor_copy(
                    accT[:, hc * 2048 + tb * 512:hc * 2048 + (tb + 1) * 512],
                    bank[:]),
                    reads=(f"pb{tb % 2}",), writes=("accT",))
        we = st.enter_context(nc.sbuf_tensor("l1f_we", [128, 8 * 1024], BF16))
        xs = st.enter_context(nc.sbuf_tensor("l1f_xs", [128, 8 * 512], BF16))
        cbc = st.enter_context(nc.sbuf_tensor("l1f_cbc", [128, 512], F32))
        rsb = st.enter_context(nc.sbuf_tensor("l1f_rsb", [20, 128], BF16))
        for e in range(E):
            b.op("gpsimd", lambda g, e=e: wN_load(
                g, b, we[:], "L1_ffn_We", (e * 1024, 8), (0, 1024)),
                writes=("l1f_we",), dma=True)
            b.op("sync", lambda s, e=e: s.dma_start(
                out=rsb[0:E, :], in_=b.A["rowsel"][0:E, e * 128:(e + 1) * 128]),
                writes=("rsb",))
            for tb in range(NTB):
                mm_chain(b, psm[:],
                         [(rsb[0:E, 0:128],
                           coT[0:E, tb * 512:(tb + 1) * 512])],
                         reads=("rsb", coreg), wbank="psm")
                b.op("vector", lambda v: v.tensor_copy(cbc[:], psm[:]),
                     reads=("psm",), writes=("l1f_cbc",))
                for k in range(8):
                    b.op("gpsimd", lambda g, k=k, tb=tb: g.tensor_mul(
                        xs[:, k * 512:(k + 1) * 512],
                        xnT[:, k * 2048 + tb * 512:k * 2048 + (tb + 1) * 512],
                        cbc[:]),
                        reads=("xnT", "l1f_cbc"), writes=(f"l1f_xs{k}",))
                for hc in range(8):
                    bank = pb[2 + hc % 2]
                    pairs = [(we[:, k * 1024 + hc * 128:k * 1024 + (hc + 1) * 128],
                              xs[:, k * 512:(k + 1) * 512]) for k in range(8)]
                    mm_chain(b, bank[:], pairs,
                             reads=tuple(["l1f_we"] + [f"l1f_xs{k}" for k in range(8)]),
                             wbank=f"pb{2 + hc % 2}")
                    sl = slice(hc * 2048 + tb * 512, hc * 2048 + (tb + 1) * 512)
                    b.op("vector", lambda v, sl=sl, bank=bank: v.tensor_add(
                        accT[:, sl], accT[:, sl], bank[:]),
                        reads=(f"pb{2 + hc % 2}", "accT"), writes=("accT",))
    b.barrier()


def emit_layer2(b):
    nc = b.nc
    A = b.A
    xnT, sT, accT = b.bufs["xnT"], b.bufs["sT"], b.bufs["accT"]
    pb, pbf, psm = b.pb, b.pbf, b.psm

    trunc = 99
    stg_ = getattr(b, "stage", None) or ""
    if stg_.startswith("l2a"):
        trunc = int(stg_[3:])
    with contextlib.ExitStack() as st:
        att = accT
        wqkv = st.enter_context(nc.sbuf_tensor("l2_wqkv", [128, 3 * 1024], BF16))
        qT = st.enter_context(nc.sbuf_tensor("l2_qT", [128, 2048], BF16))
        kT = st.enter_context(nc.sbuf_tensor("l2_kT", [128, 2048], BF16))
        vtok = st.enter_context(nc.sbuf_tensor("l2_vtok", [128, 16 * 128], BF16))
        expT = st.enter_context(nc.sbuf_tensor("l2_expT", [128, 16 * 512], BF16))
        vT = expT[:, 0:2048]
        tp3 = st.enter_context(nc.sbuf_tensor("l2_tp3", [128, 3 * 128], BF16))
        rdenA = st.enter_context(nc.sbuf_tensor("l2_rdenA", [128, 512], F32))
        rdenB = st.enter_context(nc.sbuf_tensor("l2_rdenB", [1, 512], F32))
        b.op("vector", lambda v: v.memset(rdenA[:], 1.0), writes=("l2_rdenA",))
        etp3 = st.enter_context(nc.sbuf_tensor("l2_etp3", [128, 3 * 128], BF16))
        ee15 = st.enter_context(nc.sbuf_tensor("l2_ee15", [128, 16], F32))
        ee31 = st.enter_context(nc.sbuf_tensor("l2_ee31", [128, 16], F32))
        b.multi("scalar", [
            (lambda a: a.activation(ee15[:], b.bufs["e15s"][:], AF.Exp),
             ("e15s",), ("l2_ee15",)),
            (lambda a: a.activation(ee31[:], b.bufs["e31s"][:], AF.Exp),
             ("e31s",), ("l2_ee31",))])

        for ct in range(1 if trunc < 99 else 8):
            for wi in range(3):
                b.op("gpsimd", lambda g, wi=wi, ct=ct: wN_load(
                    g, b, wqkv[:, wi * 1024:(wi + 1) * 1024], "L2_attn_W_qkv",
                    (0, 8), (wi * 1024 + ct * 128, 128)),
                    writes=(f"l2_wqkv{wi}",), dma=True)
            for wi, (dst, dreg) in enumerate(((qT, "l2_qT"), (kT, "l2_kT"),
                                              (vT, "l2_expT"))):
                for tb in range(NTB):
                    bank = pb[tb % 2]
                    pairs = [(wqkv[:, wi * 1024 + k * 128:wi * 1024 + (k + 1) * 128],
                              xnT[:, k * 2048 + tb * 512:k * 2048 + (tb + 1) * 512])
                             for k in range(8)]
                    mm_chain(b, bank[:], pairs, reads=(f"l2_wqkv{wi}", "xnT"),
                             wbank=f"pb{tb % 2}")
                    b.op("vector", lambda v, dst=dst, tb=tb, bank=bank: v.tensor_copy(
                        dst[:, tb * 512:(tb + 1) * 512], bank[:]),
                        reads=(f"pb{tb % 2}",), writes=(dreg,))
            if trunc < 2:
                continue
            for kt2 in range(2):
                def tr(t, kt2=kt2):
                    last = None
                    for tt in range(8):
                        kt = kt2 * 8 + tt
                        last = t.transpose(pbf[:, tt * 128:(tt + 1) * 128],
                                           vT[:, kt * 128:(kt + 1) * 128],
                                           b.bufs["c_id128"][:])
                    return last
                b.op("tensor", tr, reads=("l2_expT", "c_id128"), writes=("pbf",))

                def evv(v, kt2=kt2):
                    last = None
                    for tt in range(8):
                        kt = kt2 * 8 + tt
                        last = v.tensor_copy(vtok[:, kt * 128:(kt + 1) * 128],
                                             pbf[:, tt * 128:(tt + 1) * 128])
                    return last
                b.op("vector", evv, reads=("pbf",), writes=("l2_vtok",))
            if trunc < 3:
                continue
            for hh in range(1 if trunc < 99 else 2):
                h = 2 * ct + hh
                base = 64 * hh
                for di in range(3):
                    b.op("gpsimd", lambda g, h=h, di=di: g.dma_start(
                        out=tp3[:, di * 128:(di + 1) * 128],
                        in_=b.toep[h:h + 1, di * 16384:(di + 1) * 16384]
                        .rearrange("o (p f) -> (o p) f", f=128)),
                        reads=("toep",), writes=("l2_tp3",), dma=True)
                b.op("scalar", lambda a, h=h: a.activation(
                    etp3[:], tp3[:], AF.Exp),
                    reads=("l2_tp3",), writes=("l2_etp3",))
                for qb in range(1 if trunc < 99 else NTB):
                    q0 = qb * 512
                    if trunc < 4 and trunc not in (31, 32, 33):
                        continue
                    for rnd in range(4):
                        kts = list(range(rnd * 4, min((rnd + 1) * 4, 16)))
                        i0 = 0
                        b.multi("tensor", [
                            (lambda t, i=i, kt=kt: t.matmul(
                                pb[i][:],
                                kT[base:base + 64, kt * 128:(kt + 1) * 128],
                                qT[base:base + 64, q0:q0 + 512],
                                start=True, stop=True),
                             ("l2_kT", "l2_qT"), (f"pb{i}",))
                            for i, kt in ((i0 + ii, kt)
                                          for ii, kt in enumerate(kts))])
                        if trunc == 31:
                            continue
                        b.multi("scalar", [
                            (lambda a, i=i, kt=kt: a.activation(
                                expT[:, kt * 512:(kt + 1) * 512], pb[i][:],
                                AF.Exp),
                             (f"pb{i}",), ("l2_expT",))
                            for i, kt in ((i0 + ii, kt)
                                          for ii, kt in enumerate(kts))])
                        fixes = []
                        for i, kt in enumerate(kts):
                            dlt = kt * 128 - q0
                            jj = 0
                            while jj < 4:
                                dsub = dlt - jj * 128
                                if abs(dsub) <= 128:
                                    di = {-128: 0, 0: 1, 128: 2}[dsub]
                                    osl = slice(kt * 512 + jj * 128,
                                                kt * 512 + (jj + 1) * 128)
                                    fixes.append(
                                        (lambda v, osl=osl, di=di: v.tensor_mul(
                                            expT[:, osl], expT[:, osl],
                                            etp3[:, di * 128:(di + 1) * 128]),
                                         ("l2_expT", "l2_etp3"), ("l2_expT",)))
                                    jj += 1
                                    continue
                                side = 0 if dsub < 0 else 1
                                j0 = jj
                                while jj < 4 and abs(dlt - jj * 128) > 128 and \
                                        (0 if dlt - jj * 128 < 0 else 1) == side:
                                    jj += 1
                                eec = ee15 if side == 0 else ee31
                                ereg = "l2_ee15" if side == 0 else "l2_ee31"
                                osl = slice(kt * 512 + j0 * 128,
                                            kt * 512 + jj * 128)
                                fixes.append(
                                    (lambda v, osl=osl, eec=eec, h=h:
                                     v.tensor_scalar_mul(
                                        expT[:, osl], expT[:, osl],
                                        eec[:, h:h + 1]),
                                     ("l2_expT", ereg), ("l2_expT",)))
                        b.multi("vector", fixes)
                    if trunc < 5 or trunc in (31, 32, 33):
                        continue

                    def avden(t, hh=hh, q0=q0):
                        for kt in range(16):
                            t.matmul(psm[0:64, :],
                                     vtok[:, kt * 128 + 64 * hh:
                                          kt * 128 + 64 * hh + 64],
                                     expT[:, kt * 512:(kt + 1) * 512],
                                     start=kt == 0, stop=kt == 15)
                        last = None
                        for kt in range(16):
                            last = t.matmul(psm[64:65, :],
                                            b.bufs["c_ones128"][:, 0:1],
                                            expT[:, kt * 512:(kt + 1) * 512],
                                            start=kt == 0, stop=kt == 15)
                        return last
                    b.op("tensor", avden,
                         reads=("l2_vtok", "l2_expT", "c_ones128"),
                         writes=("psm",))
                    sl = slice(ct * 2048 + q0, ct * 2048 + q0 + 512)
                    dslc = (rdenA[32 * qb:32 * qb + 1, :] if qb < 3
                            else rdenB[0:1, :])
                    dreg = "l2_rdenA" if qb < 3 else "l2_rdenB"
                    b.multi("vector", [
                        (lambda v, sl=sl, base=base: v.tensor_copy(
                            att[base:base + 64, sl], psm[0:64, :]),
                         ("psm",), ("accT",)),
                        (lambda v, dslc=dslc: v.tensor_copy(dslc, psm[64:65, :]),
                         ("psm",), (dreg,))])
                if trunc >= 5 and trunc not in (31, 32, 33):
                    b.multi("vector", [
                        (lambda v: v.reciprocal(rdenA[0:65, :], rdenA[0:65, :]),
                         ("l2_rdenA",), ("l2_rdenA",)),
                        (lambda v: v.reciprocal(rdenB[:], rdenB[:]),
                         ("l2_rdenB",), ("l2_rdenB",))])
                    for qb in range(NTB):
                        q0 = qb * 512
                        dslc = (rdenA[32 * qb:32 * qb + 1, :] if qb < 3
                                else rdenB[0:1, :])
                        dreg = "l2_rdenA" if qb < 3 else "l2_rdenB"
                        ob = 32 * qb if qb < 3 else 0
                        mm_chain(b, pb[4][0:64, :],
                                 [(b.bufs["c_onesf"][ob:ob + 1, 0:64], dslc)],
                                 reads=("c_onesf", dreg), wbank="pb4")
                        sl = slice(ct * 2048 + q0, ct * 2048 + q0 + 512)
                        b.op("vector", lambda v, sl=sl, base=base: v.tensor_mul(
                            att[base:base + 64, sl], att[base:base + 64, sl],
                            pb[4][0:64, :]),
                            reads=("accT", "pb4"), writes=("accT",))
        for j in range(0 if trunc < 99 else 8):
            b.op("gpsimd", lambda g, j=j: wN_load(
                g, b, wqkv[:, 0:1024], "L2_attn_W_out", (0, 8), (j * 128, 128)),
                writes=("l2_wqkv0",), dma=True)
            for tb in range(NTB):
                bank = pb[tb % 2]
                pairs = [(wqkv[:, k * 128:(k + 1) * 128],
                          att[:, k * 2048 + tb * 512:k * 2048 + (tb + 1) * 512])
                         for k in range(8)]
                mm_chain(b, bank[:], pairs, reads=("l2_wqkv0", "accT"),
                         wbank=f"pb{tb % 2}")
                b.op("vector", lambda v, j=j, tb=tb, bank=bank: v.tensor_copy(
                    sT[:, j * 2048 + tb * 512:j * 2048 + (tb + 1) * 512],
                    bank[:]),
                    reads=(f"pb{tb % 2}",), writes=("sT",))
    b.barrier()
    if getattr(b, "stage", None) == "attn2" or trunc < 99:
        return
    emit_l2_ffn(b)
    emit_combine(b, 2)


def emit_l2_ffn(b):
    nc = b.nc
    A = b.A
    xnT, accT = b.bufs["xnT"], b.bufs["accT"]
    pb, psm = b.pb, b.psm
    E = 8
    with contextlib.ExitStack() as st:
        coT = st.enter_context(nc.sbuf_tensor("l2f_coT", [8, 2048], BF16))
        boE = st.enter_context(nc.sbuf_tensor("l2f_boE", [8, 1024], BF16))
        b1c = st.enter_context(nc.sbuf_tensor("l2f_b1c", [128, 8 * 32], F32))
        stg = st.enter_context(contextlib.ExitStack())
        wg = stg.enter_context(nc.sbuf_tensor("l2f_wg", [128, 64], BF16))
        wgsrc = stg.enter_context(nc.sbuf_tensor("l2f_wgsrc", [8, 1024], BF16))
        lgT = stg.enter_context(nc.sbuf_tensor("l2f_lgT", [8, 2048], F32))
        b.op("gpsimd", lambda g: g.dma_start(out=wgsrc[:], in_=A["L2_ffn_Wg"]),
             writes=("l2f_wgsrc",), dma=True)
        for k in range(8):
            b.op("tensor", lambda t, k=k: t.transpose(
                b.pbf[:, 0:8], wgsrc[0:8, k * 128:(k + 1) * 128],
                b.bufs["c_id128"][0:8, 0:8]),
                reads=("l2f_wgsrc", "c_id128"), writes=("pbf",))
            b.op("vector", lambda v, k=k: v.tensor_copy(
                wg[:, k * 8:(k + 1) * 8], b.pbf[:, 0:8]),
                reads=("pbf",), writes=("l2f_wg",))
        for tb in range(NTB):
            pairs = [(wg[:, k * 8:(k + 1) * 8],
                      xnT[:, k * 2048 + tb * 512:k * 2048 + (tb + 1) * 512])
                     for k in range(8)]
            mm_chain(b, psm[0:8, :], pairs, reads=("l2f_wg", "xnT"), wbank="psm")
            b.op("vector", lambda v, tb=tb: v.tensor_copy(
                lgT[:, tb * 512:(tb + 1) * 512], psm[0:8, :]),
                reads=("psm",), writes=("l2f_lgT",))
        coT, coreg = emit_topk_gates(b, stg, "l2f", lgT, "l2f_lgT", E,
                                     renorm=True, eps=1e-9, coT_buf=coT)
        b.bufs["l2f_coT"] = coT
        bo = stg.enter_context(nc.sbuf_tensor("l2f_bo", [1, 1024], BF16))
        b.op("gpsimd", lambda g: g.dma_start(
            out=bo[:], in_=A["L2_ffn_bo"].rearrange("(o n) -> o n", o=1)),
            writes=("l2f_bo",), dma=True)
        for half in range(2):
            mm_chain(b, psm[0:8, :],
                     [(b.bufs["c_ones128"][0:1, 0:8],
                       bo[0:1, half * 512:(half + 1) * 512])],
                     reads=("c_ones128", "l2f_bo"), wbank="psm")
            b.op("vector", lambda v, half=half: v.tensor_copy(
                boE[0:8, half * 512:(half + 1) * 512], psm[0:8, :]),
                reads=("psm",), writes=("l2f_boE",))
        b1rows = stg.enter_context(nc.sbuf_tensor("l2f_b1r", [32, 128], F32))
        for e in range(E):
            b.op("sync", lambda s, e=e: s.dma_start(
                out=b1rows[:], in_=A["L2_ffn_b1"][e:e + 1, :]
                .rearrange("o (k f) -> (o k) f", f=128)),
                writes=("l2f_b1r",))
            b.op("tensor", lambda t: t.transpose(
                psm[:, 0:32], b1rows[0:32, 0:128], b.bufs["c_idf"][0:32, 0:32]),
                reads=("l2f_b1r", "c_idf"), writes=("psm",))
            b.op("vector", lambda v, e=e: v.tensor_copy(
                b1c[:, e * 32:(e + 1) * 32], psm[:, 0:32]),
                reads=("psm",), writes=("l2f_b1c",))
        b.barrier()
        stg.close()
        w1s = [st.enter_context(nc.sbuf_tensor(f"l2f_w1{h}", [128, 2048], BF16))
               for h in range(2)]
        wos = [st.enter_context(nc.sbuf_tensor(f"l2f_wo{h}", [128, 1024], BF16))
               for h in range(2)]
        b._l2w1p = 0
        b._l2wop = 0
        gsum = st.enter_context(nc.sbuf_tensor("l2f_gsum", [128, 16 * 512], BF16))
        b.bufs["l2f_gsum"] = gsum
        sgb = st.enter_context(nc.sbuf_tensor("l2f_sgb", [128, 4 * 512], BF16))
        ascr = st.enter_context(nc.sbuf_tensor("l2f_ascr", [128, 512], BF16))
        cbc = st.enter_context(nc.sbuf_tensor("l2f_cbc", [128, 512], F32))
        rsb = st.enter_context(nc.sbuf_tensor("l2f_rsb", [20, 128], BF16))
        for tb in range(NTB):
            for e in range(E):
                b.op("sync", lambda s, e=e: s.dma_start(
                    out=rsb[0:E, :], in_=b.A["rowsel"][0:E, e * 128:(e + 1) * 128]),
                    writes=("rsb",))
                mm_chain(b, psm[:],
                         [(rsb[0:E, 0:128],
                           coT[0:E, tb * 512:(tb + 1) * 512])],
                         reads=("rsb", coreg), wbank="psm")
                b.op("vector", lambda v: v.tensor_copy(cbc[:], psm[:]),
                     reads=("psm",), writes=("l2f_cbc",))
                for fp in range(4):
                    def w1part(j, q2):
                        pc = b._l2w1p
                        b._l2w1p += 1
                        slot, reg = w1s[pc % 2], f"l2f_w1{pc % 2}"
                        b.op("sync", lambda s, j=j, slot=slot: wT_load(
                            s, b, slot[:], "L2_ffn_W1", j, jw=256),
                            reads=("wbf_L2_ffn_W1",), writes=(reg,))
                        for jje in range(2):
                            jj = q2 * 2 + jje
                            pairs = [(slot[:, k * 256 + jje * 128:
                                         k * 256 + (jje + 1) * 128],
                                      xnT[:, k * 2048 + tb * 512:
                                          k * 2048 + (tb + 1) * 512])
                                     for k in range(8)]
                            mm_chain(b, pb[jj][:], pairs, reads=(reg, "xnT"),
                                     wbank=f"pb{jj}")
                    for q2 in range(2):
                        w1part(e * 16 + 8 + fp * 2 + q2, q2)
                    b.multi("scalar", [
                        (lambda a, jj=jj, bi=e * 32 + 16 + fp * 4 + jj: a.activation(
                            sgb[:, jj * 512:(jj + 1) * 512], pb[jj][:], AF.Sigmoid,
                            bias=b1c[:, bi:bi + 1]),
                         (f"pb{jj}", "l2f_b1c"), (f"l2f_sgb{jj}",))
                        for jj in range(4)])
                    b.multi("gpsimd", [
                        (lambda g, jj=jj: g.tensor_mul(
                            sgb[:, jj * 512:(jj + 1) * 512],
                            sgb[:, jj * 512:(jj + 1) * 512], cbc[:]),
                         (f"l2f_sgb{jj}", "l2f_cbc"), (f"l2f_sgb{jj}",))
                        for jj in range(4)])
                    for q2 in range(2):
                        w1part(e * 16 + fp * 2 + q2, q2)
                    items = []
                    for jj in range(4):
                        gi = fp * 4 + jj
                        bi = e * 32 + fp * 4 + jj
                        if e == 0:
                            items.append(
                                (lambda v, jj=jj, gi=gi, bi=bi:
                                 v.scalar_tensor_tensor(
                                    gsum[:, gi * 512:(gi + 1) * 512], pb[jj][:],
                                    b1c[:, bi:bi + 1],
                                    sgb[:, jj * 512:(jj + 1) * 512],
                                    ALU.add, ALU.mult),
                                 (f"pb{jj}", "l2f_b1c", f"l2f_sgb{jj}"),
                                 (f"l2f_gsum{gi}",)))
                        else:
                            items.append(
                                (lambda v, jj=jj, bi=bi:
                                 v.scalar_tensor_tensor(
                                    ascr[:, 0:512], pb[jj][:], b1c[:, bi:bi + 1],
                                    sgb[:, jj * 512:(jj + 1) * 512],
                                    ALU.add, ALU.mult),
                                 (f"pb{jj}", "l2f_b1c", f"l2f_sgb{jj}"),
                                 ("l2f_a",)))
                            items.append(
                                (lambda v, jj=jj, gi=gi: v.tensor_add(
                                    gsum[:, gi * 512:(gi + 1) * 512],
                                    gsum[:, gi * 512:(gi + 1) * 512],
                                    ascr[:, 0:512]),
                                 ("l2f_a", f"l2f_gsum{gi}"),
                                 (f"l2f_gsum{gi}",)))
                    b.multi("vector", items)
            for hc in range(8):
                bank = pb[4 + hc % 2]
                for kh in range(2):
                    pc = b._l2wop
                    b._l2wop += 1
                    slot, reg = wos[pc % 2], f"l2f_wo{pc % 2}"
                    b.op("sync", lambda s, hc=hc, kh=kh, slot=slot: wT_load(
                        s, b, slot[:], "L2_ffn_Wo", hc, k0=kh * 8, nk=8),
                        reads=("wbf_L2_ffn_Wo",), writes=(reg,))

                    def dn(t, hc=hc, tb=tb, bank=bank, kh=kh, slot=slot):
                        last = None
                        if kh == 0:
                            last = t.matmul(
                                bank[:], boE[0:E, hc * 128:(hc + 1) * 128],
                                coT[0:E, tb * 512:(tb + 1) * 512],
                                start=True, stop=False)
                        for k in range(8):
                            last = t.matmul(
                                bank[:], slot[:, k * 128:(k + 1) * 128],
                                gsum[:, (kh * 8 + k) * 512:(kh * 8 + k + 1) * 512],
                                start=False, stop=(kh == 1 and k == 7))
                        return last
                    b.op("tensor", dn,
                         reads=tuple([reg, "l2f_boE", coreg]
                                     + [f"l2f_gsum{kh * 8 + k}" for k in range(8)]),
                         writes=(f"pb{4 + hc % 2}",))
                b.op("vector", lambda v, hc=hc, tb=tb, bank=bank: v.tensor_copy(
                    accT[:, hc * 2048 + tb * 512:hc * 2048 + (tb + 1) * 512],
                    bank[:]),
                    reads=(f"pb{4 + hc % 2}",), writes=("accT",))
    b.barrier()


def emit_layer3(b):
    nc = b.nc
    A = b.A
    xnT, sT, accT = b.bufs["xnT"], b.bufs["sT"], b.bufs["accT"]
    pb, pbf, psm = b.pb, b.pbf, b.psm

    with contextlib.ExitStack() as st:
        mixc = st.enter_context(nc.sbuf_tensor("l3_mixc", [128, 3 * 8], F32))
        wsl = st.enter_context(nc.sbuf_tensor("l3_wsl", [128, 1024], BF16))
        wA = st.enter_context(nc.sbuf_tensor("l3_wA", [128, 1024], BF16))
        uec = st.enter_context(nc.sbuf_tensor("l3_uec", [128, 8], F32))
        ew_c = st.enter_context(nc.sbuf_tensor("l3_ewc", [128, 8], F32))
        ew16 = st.enter_context(nc.sbuf_tensor("l3_ew16", [16, 2], F32))
        qs = st.enter_context(nc.sbuf_tensor("l3_qs", [128, 2048], BF16))
        kt = st.enter_context(nc.sbuf_tensor("l3_kt", [128, 2048], F32))
        vt = st.enter_context(nc.sbuf_tensor("l3_vt", [128, 2048], BF16))
        rt = st.enter_context(nc.sbuf_tensor("l3_rt", [128, 2048], BF16))
        heu = st.enter_context(nc.sbuf_tensor("l3_heu", [128, 1024], BF16))
        hekv = st.enter_context(nc.sbuf_tensor("l3_hekv", [128, 1024], BF16))
        heks = st.enter_context(nc.sbuf_tensor("l3_heks", [128, 1024], BF16))
        hsa = st.enter_context(nc.sbuf_tensor("l3_hsa", [128, 1025], BF16))
        hsb = st.enter_context(nc.sbuf_tensor("l3_hsb", [128, 1025], BF16))
        hden = st.enter_context(nc.sbuf_tensor("l3_hden", [128, 1024], BF16))
        hn = st.enter_context(nc.sbuf_tensor("l3_hn", [128, 1024], BF16))
        hewt = st.enter_context(nc.sbuf_tensor("l3_hewt", [128, 1024], F32))

        for mi, nm in enumerate(("mix_k", "mix_v", "mix_r")):
            b.op("sync", lambda s, mi=mi, nm=nm: s.dma_start(
                out=mixc[:, mi * 8:(mi + 1) * 8],
                in_=A[f"L3_attn_{nm}"].rearrange("a c n -> (a c n)")
                .rearrange("(k p) -> p k", p=128),
                allow_slow_non_contiguous=True),
                writes=(f"l3_mix{mi}",))
        b.op("sync", lambda s: s.dma_start(
            out=ew16[:, 0:1], in_=A["L3_attn_time_decay"].rearrange("(p o) -> p o", o=1)),
            writes=("l3_ew16",))
        b.op("sync", lambda s: s.dma_start(
            out=ew16[:, 1:2], in_=A["L3_attn_time_first"].rearrange("(p o) -> p o", o=1)),
            writes=("l3_u16",))
        b.op("scalar", lambda a: a.activation(ew16[:, 0:1], ew16[:, 0:1], AF.Exp),
             reads=("l3_ew16",), writes=("l3_ew16",))
        b.op("scalar", lambda a: a.activation(ew16[:, 0:1], ew16[:, 0:1], AF.Exp,
                                              scale=-1.0),
             reads=("l3_ew16",), writes=("l3_ew16",))
        ewrow = st.enter_context(nc.sbuf_tensor("l3_ewrow", [1, 32], F32))
        for col, reg in ((0, "l3_ew16"), (1, "l3_u16")):
            b.op("tensor", lambda t, col=col: t.transpose(
                psm[0:1, 0:16], ew16[:, col:col + 1], b.bufs["c_idf"][0:16, 0:16]),
                reads=(reg, "c_idf"), writes=("psm",))
            b.op("vector", lambda v, col=col: v.tensor_copy(
                ewrow[0:1, col * 16:col * 16 + 16], psm[0:1, 0:16]),
                reads=("psm",), writes=("l3_ewrow",))
        for ct in range(8):
            for col, dst, dreg in ((0, ew_c, "l3_ewc"), (1, uec, "l3_uec")):
                def bc(t, ct=ct, col=col):
                    t.matmul(psm[0:64, 0:1], b.bufs["c_onesf"][0:1, 0:64],
                             ewrow[0:1, col * 16 + 2 * ct:col * 16 + 2 * ct + 1],
                             start=True, stop=True)
                    return t.matmul(psm[64:128, 0:1], b.bufs["c_onesf"][0:1, 0:64],
                                    ewrow[0:1, col * 16 + 2 * ct + 1:col * 16 + 2 * ct + 2],
                                    start=True, stop=True)
                b.op("tensor", bc, reads=("l3_ewrow", "c_onesf"), writes=("psm",))
                b.op("vector", lambda v, ct=ct, dst=dst, dreg=dreg: v.tensor_copy(
                    dst[:, ct:ct + 1], psm[:, 0:1]),
                    reads=("psm",), writes=(dreg,))

        for ct in range(8):
            for nm, mi in (("Wv", 1), ("Wr", 2), ("Wk", 0)):
                b.op("gpsimd", lambda s, nm=nm, ct=ct: wT_load(
                    s, b, wsl[:], f"L3_attn_{nm}", ct),
                    writes=("l3_wsl",), dma=True)
                b.multi("vector",
                        [(lambda v, k=k, mi=mi: v.tensor_scalar_mul(
                            wA[:, k * 128:(k + 1) * 128],
                            wsl[:, k * 128:(k + 1) * 128],
                            mixc[:, mi * 8 + k:mi * 8 + k + 1]),
                          ("l3_wsl", f"l3_mix{mi}"), ("l3_wA",))
                         for k in range(8)]
                        + [(lambda v, k=k: v.tensor_sub(
                            wsl[:, k * 128:(k + 1) * 128],
                            wsl[:, k * 128:(k + 1) * 128],
                            wA[:, k * 128:(k + 1) * 128]),
                            ("l3_wsl", "l3_wA"), ("l3_wsl",))
                           for k in range(8)])
                for tb in range(NTB):
                    bA, bB = 2 * (tb % 2), 2 * (tb % 2) + 1
                    pairs = [(wA[:, k * 128:(k + 1) * 128],
                              xnT[:, k * 2048 + tb * 512:k * 2048 + (tb + 1) * 512])
                             for k in range(8)]
                    mm_chain(b, pb[bA][:], pairs,
                             reads=("l3_wA", "xnT"), wbank=f"pb{bA}")
                    pairs = [(wsl[:, k * 128:(k + 1) * 128],
                              xnT[:, k * 2048 + tb * 512:k * 2048 + (tb + 1) * 512])
                             for k in range(8)]
                    mm_chain(b, pb[bB][:], pairs,
                             reads=("l3_wsl", "xnT"), wbank=f"pb{bB}")
                    b.multi("vector", [
                        (lambda v, tb=tb, bB=bB: v.tensor_copy(
                            qs[:, tb * 512:(tb + 1) * 512], pb[bB][:]),
                         (f"pb{bB}",), ("l3_qs",)),
                        (lambda v, tb=tb, bA=bA: v.tensor_copy(
                            kt[:, tb * 512:(tb + 1) * 512], pb[bA][:]),
                         (f"pb{bA}",), ("l3_kt",))])
                if nm == "Wv":
                    b.op("vector", lambda v: v.tensor_add(
                        vt[:, 1:2048], kt[:, 1:2048], qs[:, 0:2047]),
                        reads=("l3_kt", "l3_qs"), writes=("l3_vt",))
                    b.op("vector", lambda v: v.tensor_copy(
                        vt[:, 0:1], kt[:, 0:1]),
                        reads=("l3_kt",), writes=("l3_vt",))
                elif nm == "Wr":
                    b.op("vector", lambda v: v.tensor_add(
                        kt[:, 1:2048], kt[:, 1:2048], qs[:, 0:2047]),
                        reads=("l3_kt", "l3_qs"), writes=("l3_kt",))
                    b.op("scalar", lambda a: a.activation(rt[:], kt[:], AF.Sigmoid),
                         reads=("l3_kt",), writes=("l3_rt",))
                else:
                    b.op("vector", lambda v: v.tensor_add(
                        kt[:, 1:2048], kt[:, 1:2048], qs[:, 0:2047]),
                        reads=("l3_kt", "l3_qs"), writes=("l3_kt",))
            for hf in range(2):
                sl = slice(hf * 1024, (hf + 1) * 1024)
                b.op("vector", lambda v: v.memset(hewt[:], 1.0), writes=("l3_hewt",))
                b.op("vector", lambda v, ct=ct: v.tensor_scalar_mul(
                    hewt[:], hewt[:], ew_c[:, ct:ct + 1]),
                    reads=("l3_hewt", "l3_ewc"), writes=("l3_hewt",))
                b.op("scalar", lambda a, ct=ct, sl=sl: a.activation(
                    heu[:], kt[:, sl], AF.Exp, bias=uec[:, ct:ct + 1]),
                    reads=("l3_kt", "l3_uec"), writes=("l3_heu",))
                b.op("scalar", lambda a, sl=sl: a.activation(
                    qs[:, sl], kt[:, sl], AF.Exp),
                    reads=("l3_kt",), writes=("l3_qs",))
                b.op("gpsimd", lambda g, sl=sl: g.tensor_mul(
                    hekv[:], qs[:, sl], vt[:, sl]),
                    reads=("l3_qs", "l3_vt"), writes=("l3_hekv",))
                for tb2 in range(2):
                    o = hf * 1024 + tb2 * 512
                    mm_chain(b, pb[5][:],
                             [(b.bufs["c_BB"][:], qs[:, o:o + 512])],
                             reads=("c_BB", "l3_qs"), wbank="pb5")
                    b.op("vector", lambda v, tb2=tb2: v.tensor_copy(
                        heks[:, tb2 * 512:(tb2 + 1) * 512], pb[5][:]),
                        reads=("pb5",), writes=("l3_heks",))
                    mm_chain(b, pb[4][:],
                             [(b.bufs["c_BB"][:], heu[:, tb2 * 512:(tb2 + 1) * 512])],
                             reads=("c_BB", "l3_heu"), wbank="pb4")
                    b.op("vector", lambda v, tb2=tb2: v.tensor_copy(
                        hden[:, tb2 * 512:(tb2 + 1) * 512], pb[4][:]),
                        reads=("pb4",), writes=("l3_hden",))
                if hf == 0:
                    b.op("vector", lambda v: v.memset(hsa[:, 0:1], 0.0),
                         writes=("l3_hsa",))
                    b.op("vector", lambda v: v.memset(hsb[:, 0:1], 0.0),
                         writes=("l3_hsb",))
                else:
                    b.op("vector", lambda v: v.tensor_copy(
                        hsa[:, 0:1], hsa[:, 1024:1025]),
                        reads=("l3_hsa",), writes=("l3_hsa",))
                    b.op("vector", lambda v: v.tensor_copy(
                        hsb[:, 0:1], hsb[:, 1024:1025]),
                        reads=("l3_hsb",), writes=("l3_hsb",))
                b.op("vector", lambda v: v.tensor_tensor_scan(
                    hsa[:, 1:1025], hewt[:], hekv[:], hsa[:, 0:1],
                    ALU.mult, ALU.add),
                    reads=("l3_hewt", "l3_hekv", "l3_hsa"), writes=("l3_hsa",))
                b.op("vector", lambda v: v.tensor_tensor_scan(
                    hsb[:, 1:1025], hewt[:], heks[:], hsb[:, 0:1],
                    ALU.mult, ALU.add),
                    reads=("l3_hewt", "l3_heks", "l3_hsb"), writes=("l3_hsb",))
                b.op("gpsimd", lambda g: g.tensor_scalar_max(
                    hden[:], hden[:], 1e-6),
                    reads=("l3_hden",), writes=("l3_hden",))
                b.op("gpsimd", lambda g: g.tensor_add(
                    hden[:], hden[:], hsb[:, 0:1024]),
                    reads=("l3_hden", "l3_hsb"), writes=("l3_hden",))
                b.multi("scalar", [
                    (lambda a: a.activation(hewt[:], hden[:], AF.Ln),
                     ("l3_hden",), ("l3_hewt",)),
                    (lambda a: a.activation(hewt[:], hewt[:], AF.Exp, scale=-1.0),
                     ("l3_hewt",), ("l3_hewt",))])
                b.op("gpsimd", lambda g, sl=sl: g.tensor_mul(
                    hn[:], heu[:], vt[:, sl]),
                    reads=("l3_heu", "l3_vt"), writes=("l3_hn",))
                b.op("gpsimd", lambda g: g.tensor_add(hn[:], hn[:], hsa[:, 0:1024]),
                     reads=("l3_hn", "l3_hsa"), writes=("l3_hn",))
                b.op("vector", lambda v: v.tensor_mul(hn[:], hn[:], hewt[:]),
                     reads=("l3_hn", "l3_hewt"), writes=("l3_hn",))
                b.op("gpsimd", lambda g, ct=ct, sl=sl, hf=hf: g.tensor_mul(
                    accT[:, ct * 2048 + hf * 1024:ct * 2048 + (hf + 1) * 1024],
                    rt[:, sl], hn[:]),
                    reads=("l3_rt", "l3_hn"), writes=("accT",))
        for j in range(8):
            b.op("gpsimd", lambda s, j=j: wT_load(s, b, wsl[:], "L3_attn_Wo", j),
                 writes=("l3_wsl",), dma=True)
            for tb in range(NTB):
                bank = pb[tb % 2]
                pairs = [(wsl[:, k * 128:(k + 1) * 128],
                          accT[:, k * 2048 + tb * 512:k * 2048 + (tb + 1) * 512])
                         for k in range(8)]
                mm_chain(b, bank[:], pairs, reads=("l3_wsl", "accT"),
                         wbank=f"pb{tb % 2}")
                b.op("vector", lambda v, j=j, tb=tb, bank=bank: v.tensor_copy(
                    sT[:, j * 2048 + tb * 512:j * 2048 + (tb + 1) * 512],
                    bank[:]),
                    reads=(f"pb{tb % 2}",), writes=("sT",))
    b.barrier()
    emit_l3_ffn(b)
    emit_combine(b, 3)


def emit_round(b, xb, xreg, d_b, dreg, iq, ireg, out_ap, oreg, width):
    b.op("vector", lambda v: v.tensor_copy(iq[:, 0:width], xb[:, 0:width]),
         reads=(xreg,), writes=(ireg,))
    b.op("vector", lambda v: v.tensor_copy(d_b[:, 0:width], iq[:, 0:width]),
         reads=(ireg,), writes=(dreg,))
    b.op("vector", lambda v: v.tensor_sub(xb[:, 0:width], xb[:, 0:width],
                                          d_b[:, 0:width]),
         reads=(xreg, dreg), writes=(xreg,))
    b.op("vector", lambda v: v.scalar_tensor_tensor(
        d_b[:, 0:width], xb[:, 0:width], 0.5, d_b[:, 0:width],
        ALU.is_ge, ALU.add),
        reads=(xreg, dreg), writes=(dreg,))
    b.op("vector", lambda v: v.tensor_single_scalar(
        xb[:, 0:width], xb[:, 0:width], -0.5, ALU.is_le),
        reads=(xreg,), writes=(xreg,))
    b.op("vector", lambda v: v.tensor_sub(out_ap, d_b[:, 0:width], xb[:, 0:width]),
         reads=(dreg, xreg), writes=(oreg,))


def emit_l3_ffn(b):
    nc = b.nc
    A = b.A
    xres, xnT, accT = b.bufs["xres"], b.bufs["xnT"], b.bufs["accT"]
    stat_m, stat_r = b.bufs["stat_m"], b.bufs["stat_r"]
    pb, psm = b.pb, b.psm
    qT = xnT

    with contextlib.ExitStack() as st:
        scr = st.enter_context(nc.sbuf_tensor("l3f_scr", [128, 1024], F32))
        scr2 = st.enter_context(nc.sbuf_tensor("l3f_scr2", [128, 1024], F32))
        iq = st.enter_context(nc.sbuf_tensor("l3f_iq", [128, 1024], I32))
        qrhs = st.enter_context(nc.sbuf_tensor("l3f_qrhs", [128, 8 * 1024], BF16))
        csc = st.enter_context(nc.sbuf_tensor("l3f_csc", [128, 8], F32))
        row = st.enter_context(nc.sbuf_tensor("l3f_row", [1, 128], F32))

        with contextlib.ExitStack() as sta:
            crow = sta.enter_context(nc.sbuf_tensor("l3f_crow", [1, 1024], F32))
            rb = sta.enter_context(nc.sbuf_tensor("l3f_rb", [128, 1024], F32))
            for i in range(8):
                if i == 0:
                    b.op("sync", lambda s: s.dma_start(
                        out=scr2[:], in_=A["L3_ffn_rhs"][0:128, :]),
                        writes=("l3f_scr2",))
                    b.op("scalar", lambda a: a.activation(scr2[:], scr2[:], AF.Abs),
                         reads=("l3f_scr2",), writes=("l3f_scr2",))
                else:
                    b.op("sync", lambda s, i=i: s.dma_start(
                        out=scr[:], in_=A["L3_ffn_rhs"][i * 128:(i + 1) * 128, :]),
                        writes=("l3f_scr",))
                    b.op("scalar", lambda a: a.activation(scr[:], scr[:], AF.Abs),
                         reads=("l3f_scr",), writes=("l3f_scr",))
                    b.op("vector", lambda v: v.tensor_max(
                        scr2[:], scr2[:], scr[:]),
                        reads=("l3f_scr2", "l3f_scr"), writes=("l3f_scr2",))
            for j in range(8):
                b.op("tensor", lambda t, j=j: t.transpose(
                    psm[:, 0:128], scr2[:, j * 128:(j + 1) * 128],
                    b.bufs["c_idf"][:]),
                    reads=("l3f_scr2", "c_idf"), writes=("psm",))
                b.op("vector", lambda v, j=j: v.reduce_max(
                    csc[:, j:j + 1], psm[:, 0:128], axis=AX.X),
                    reads=("psm",), writes=("l3f_csc",))
            b.op("vector", lambda v: v.tensor_scalar_max(csc[:], csc[:], 1e-12),
                 reads=("l3f_csc",), writes=("l3f_csc",))
            b.op("vector", lambda v: v.tensor_scalar_mul(csc[:], csc[:], 1.0 / 127.0),
                 reads=("l3f_csc",), writes=("l3f_csc",))
            for j in range(8):
                b.op("tensor", lambda t, j=j: t.transpose(
                    psm[0:1, 0:128], csc[:, j:j + 1], b.bufs["c_idf"][:]),
                    reads=("l3f_csc", "c_idf"), writes=("psm",))
                b.op("vector", lambda v, j=j: v.tensor_copy(
                    crow[0:1, j * 128:(j + 1) * 128], psm[0:1, 0:128]),
                    reads=("psm",), writes=("l3f_crow",))
            b.op("vector", lambda v: v.reciprocal(crow[:], crow[:]),
                 reads=("l3f_crow",), writes=("l3f_crow",))
            for half in range(2):
                mm_chain(b, pb[half][:],
                         [(b.bufs["c_onesf"][0:1, :],
                           crow[0:1, half * 512:(half + 1) * 512])],
                         reads=("c_onesf", "l3f_crow"), wbank=f"pb{half}")
                b.op("vector", lambda v, half=half: v.tensor_copy(
                    rb[:, half * 512:(half + 1) * 512], pb[half][:]),
                    reads=(f"pb{half}",), writes=("l3f_rb",))
            for i in range(8):
                b.op("sync", lambda s, i=i: s.dma_start(
                    out=scr[:], in_=A["L3_ffn_rhs"][i * 128:(i + 1) * 128, :]),
                    writes=("l3f_scr",))
                b.op("vector", lambda v: v.tensor_mul(scr[:], scr[:], rb[:]),
                     reads=("l3f_scr", "l3f_rb"), writes=("l3f_scr",))
                emit_round(b, scr, "l3f_scr", scr2, "l3f_scr2", iq, "l3f_iq",
                           qrhs[:, i * 1024:(i + 1) * 1024], f"l3f_qrhs{i}", 1024)
            b.barrier()

        with contextlib.ExitStack() as stb:
            srb = stb.enter_context(nc.sbuf_tensor("l3f_srb", [128, 2048], F32))
            qtok = stb.enter_context(nc.sbuf_tensor("l3f_qtok", [128, 1024], BF16))
            am = stb.enter_context(nc.sbuf_tensor("l3f_am", [128, 4], F32))
            for i in range(NT_T):
                b.op("vector", lambda v, i=i: v.tensor_scalar(
                    scr[:], xres[:, i * 1024:(i + 1) * 1024],
                    stat_m[:, i:i + 1], stat_r[:, i:i + 1], ALU.subtract, ALU.mult),
                    reads=("xres", "stat_m", "stat_r"), writes=("l3f_scr",))
                b.op("vector", lambda v: v.reduce_max(
                    am[:, 0:1], scr[:], axis=AX.X, apply_absolute_value=True),
                    reads=("l3f_scr",), writes=("l3f_am",))
                b.op("vector", lambda v: v.tensor_scalar_max(
                    am[:, 0:1], am[:, 0:1], 1e-12),
                    reads=("l3f_am",), writes=("l3f_am",))
                b.op("vector", lambda v: v.tensor_scalar_mul(
                    am[:, 0:1], am[:, 0:1], 1.0 / 127.0),
                    reads=("l3f_am",), writes=("l3f_am",))
                b.op("vector", lambda v: v.reciprocal(am[:, 1:2], am[:, 0:1]),
                     reads=("l3f_am",), writes=("l3f_am2",))
                b.op("vector", lambda v: v.tensor_scalar_mul(
                    scr[:], scr[:], am[:, 1:2]),
                    reads=("l3f_scr", "l3f_am2"), writes=("l3f_scr",))
                emit_round(b, scr, "l3f_scr", scr2, "l3f_scr2", iq, "l3f_iq",
                           qtok[:], "l3f_qtok", 1024)
                b.op("tensor", lambda t, i=i: t.transpose(
                    psm[0:1, 0:128], am[:, 0:1], b.bufs["c_idf"][:]),
                    reads=("l3f_am", "c_idf"), writes=("psm",))
                b.op("vector", lambda v, i=i: v.tensor_copy(
                    row[:], psm[0:1, 0:128]),
                    reads=("psm",), writes=("l3f_row",))
                mm_chain(b, pb[2][:, 0:128],
                         [(b.bufs["c_onesf"][0:1, :], row[0:1, :])],
                         reads=("c_onesf", "l3f_row"), wbank="pb2")
                b.op("vector", lambda v, i=i: v.tensor_copy(
                    srb[:, i * 128:(i + 1) * 128], pb[2][:, 0:128]),
                    reads=("pb2",), writes=("l3f_srb",))

                def tr(t, i=i):
                    last = None
                    for k in range(8):
                        last = t.transpose(
                            b.pbf[:, k * 128:(k + 1) * 128],
                            qtok[:, k * 128:(k + 1) * 128], b.bufs["c_id128"][:])
                    return last
                b.op("tensor", tr, reads=("l3f_qtok", "c_id128"), writes=("pbf",))

                def ev(v, i=i):
                    last = None
                    for k in range(8):
                        last = v.tensor_copy(
                            qT[:, k * 2048 + i * 128:k * 2048 + (i + 1) * 128],
                            b.pbf[:, k * 128:(k + 1) * 128])
                    return last
                b.op("vector", ev, reads=("pbf",), writes=("xnT",))
            for j in range(8):
                for tb in range(NTB):
                    bank = pb[4 + tb % 2]
                    pairs = [(qrhs[:, k * 1024 + j * 128:k * 1024 + (j + 1) * 128],
                              qT[:, k * 2048 + tb * 512:k * 2048 + (tb + 1) * 512])
                             for k in range(8)]
                    mm_chain(b, bank[:], pairs,
                             reads=tuple([f"l3f_qrhs{k}" for k in range(8)] + ["xnT"]),
                             wbank=f"pb{4 + tb % 2}")
                    sl = slice(j * 2048 + tb * 512, j * 2048 + (tb + 1) * 512)
                    b.op("vector", lambda v, bank=bank, tb=tb: v.tensor_mul(
                        scr[:, 0:512], bank[:], srb[:, tb * 512:(tb + 1) * 512]),
                        reads=(f"pb{4 + tb % 2}", "l3f_srb"), writes=("l3f_scr",))
                    b.op("vector", lambda v, sl=sl, j=j: v.tensor_scalar_mul(
                        accT[:, sl], scr[:, 0:512], csc[:, j:j + 1]),
                        reads=("l3f_scr", "l3f_csc"), writes=("accT",))
    b.barrier()


def make_spec_inputs(inputs):
    consts = host_consts()
    spec = {}
    common = {}
    for k, v in inputs.items():
        v = np.ascontiguousarray(np.asarray(v))
        if k == "L0_attn_x":
            continue
        if k in TRANS_W:
            w = v.reshape(-1, v.shape[-1])
            vt = np.ascontiguousarray(w.T)
            spec[k + "_T"] = (vt.shape, str(vt.dtype))
            common[k + "_T"] = vt
            continue
        spec[k] = (v.shape, str(v.dtype))
        common[k] = v
    for k, v in consts.items():
        spec[k] = (v.shape, str(v.dtype))
        common[k] = v
    spec["x_in"] = ((T, C), "float32")
    return spec, common


def kernel(**inputs):
    spec, common = make_spec_inputs(inputs)
    x = np.ascontiguousarray(np.asarray(inputs["L0_attn_x"], dtype=np.float32))
    nc = build_program(spec)
    in_maps = []
    for core in range(8):
        m = dict(common)
        m["x_in"] = np.ascontiguousarray(x[core])
        in_maps.append(m)
    res = run_bass_kernel_spmd(nc, in_maps, core_ids=list(range(8)))
    return np.stack([r["out"] for r in res.results], axis=0).astype(np.float32)


# version 19 (final)
# speedup vs baseline: 1.0118x; 1.0118x over previous
import contextlib
import numpy as np
import ml_dtypes
import concourse.bass as bass
import concourse.mybir as mybir
from concourse.bass_utils import run_bass_kernel_spmd

F32 = mybir.dt.float32
BF16 = mybir.dt.bfloat16
I32 = mybir.dt.int32
AF = mybir.ActivationFunctionType
ALU = mybir.AluOpType
AX = mybir.AxisListType
BF = ml_dtypes.bfloat16

T = 2048
C = 1024
NT_T = 16
NT_C = 8
TB = 512
NTB = 4
EPS_LN = 1e-5

NSYNC = 6
NPOOL = 4
DSYNC = tuple(f"dsync{i}" for i in range(NSYNC))
DPOOL = tuple(f"dpool{i}" for i in range(NPOOL))
SEMS = ("pe", "dve", "act", "pool", "dstage") + DSYNC + DPOOL
ENG_SEM = {"tensor": "pe", "vector": "dve", "scalar": "act", "gpsimd": "pool",
           "sync": "dsync0"}

TRANS_W = {
    "L0_attn_Wq": (1024, 1024), "L0_attn_Wk": (1024, 1024),
    "L0_attn_Wv": (1024, 1024), "L0_attn_Wp": (1024, 1024),
    "L1_attn_W_dt": (1024, 1024),
    "L2_ffn_W1": (8 * 4096, 1024), "L2_ffn_Wo": (1024, 2048),
    "L3_attn_Wk": (1024, 1024), "L3_attn_Wv": (1024, 1024),
    "L3_attn_Wr": (1024, 1024), "L3_attn_Wo": (1024, 1024),
}
TRANS_STAGE = ("L2_ffn_W1", "L2_ffn_Wo")


def host_consts():
    d = {}
    d["id128"] = np.eye(128, dtype=BF)
    d["ones128"] = np.ones((128, 128), dtype=BF)
    g = np.zeros((1024, 16), dtype=BF)
    for k in range(8):
        for p in range(128):
            g[k * 128 + p, 2 * k + p // 64] = 1
    d["Gden"] = g
    d["Hbc"] = np.ascontiguousarray(g.T)
    g2 = np.zeros((128, 2), dtype=BF)
    g2[np.arange(128), np.arange(128) // 64] = 1
    d["G2"] = g2
    d["H2"] = np.ascontiguousarray(g2.T)
    bb = np.zeros((128, 128), dtype=BF)
    for p in range(128):
        q0 = (p // 64) * 64
        bb[p, q0:q0 + 64] = 1
    d["BB"] = bb
    rs = np.zeros((20, 20 * 128), dtype=BF)
    for e in range(20):
        rs[e, e * 128:(e + 1) * 128] = 1
    d["rowsel"] = rs
    for bkt in (15, 31):
        m = np.zeros((32, 128), dtype=BF)
        m[bkt, :] = 1
        d[f"del{bkt}"] = m

    def bucket(rel):
        nb = 16
        rb = (rel > 0).astype(np.int64) * nb
        rel = np.abs(rel)
        me = 8
        small = rel < me
        large = me + (np.log(np.maximum(rel, 1).astype(np.float64) / me)
                      / np.log(128 / me) * (nb - me)).astype(np.int64)
        large = np.minimum(large, nb - 1)
        return rb + np.where(small, rel, large)
    d["fconsts"] = np.tile(np.array([[1e-5, 1.0]], dtype=np.float32), (128, 1))
    d["onesf"] = np.ones((65, 128), dtype=np.float32)
    oh = np.zeros((32, 3 * 16384), dtype=BF)
    p = np.arange(128)[:, None]
    f = np.arange(128)[None, :]
    flat = (p * 128 + f).ravel()
    for i, dlt in enumerate((-128, 0, 128)):
        bk = bucket(dlt + p - f)
        oh[bk.ravel(), i * 16384 + flat] = 1
    d["OH3"] = oh
    return d


class Bld:
    def __init__(self, nc, blk, stack):
        self.nc = nc
        self.blk = blk
        self.stack = stack
        self.sem = {s: stack.enter_context(nc.semaphore(s)) for s in SEMS}
        self.cnt = {s: 0 for s in SEMS}
        self.lw = {}
        self.rd = {}
        self.waited = {e: {} for e in ENG_SEM}
        self.bufs = {}
        self.A = {}
        self.wbf = {}
        self.dma_rr = {"sync": 0, "gpsimd": 0}
        self._cur_dma_sem = None

    def sb(self, name, shape, dtype):
        t = self.stack.enter_context(self.nc.sbuf_tensor(name, shape, dtype))
        self.bufs[name] = t
        return t

    def ps(self, name, shape, dtype=F32):
        t = self.stack.enter_context(self.nc.psum_tensor(name, shape, dtype))
        self.bufs[name] = t
        return t

    def op(self, eng, fn, reads=(), writes=(), dma=False, stage_dma=False):
        if eng == "sync":
            dma = True
        if dma:
            if stage_dma:
                sem = "dstage"
            elif eng == "gpsimd":
                sem = DPOOL[self.dma_rr["gpsimd"] % NPOOL]
                self.dma_rr["gpsimd"] += 1
            else:
                sem = DSYNC[self.dma_rr["sync"] % NSYNC]
                self.dma_rr["sync"] += 1
            inc = 16
        else:
            sem = ENG_SEM[eng]
            inc = 1
        deps = {}

        def add(d):
            if d:
                s, c = d
                deps[s] = max(deps.get(s, 0), c)
        for r in reads:
            add(self.lw.get(r))
        for w in writes:
            add(self.lw.get(w))
            for s, c in self.rd.get(w, {}).items():
                add((s, c))
        if dma:
            add((sem, self.cnt[sem]))
        me = self
        out_cnt = [0]

        def body(e):
            for s, c in deps.items():
                if me.waited[eng].get(s, 0) < c:
                    e.wait_ge(me.sem[s], c)
                    me.waited[eng][s] = c
            me._cur_dma_sem = sem if dma else None
            ins = fn(e)
            ins.then_inc(me.sem[sem], inc)
            me.cnt[sem] += inc
            out_cnt[0] = me.cnt[sem]
        getattr(self.blk, eng)(body)
        cnt = out_cnt[0]
        for r in reads:
            d = self.rd.setdefault(r, {})
            d[sem] = max(d.get(sem, 0), cnt)
        for w in writes:
            self.lw[w] = (sem, cnt)
            self.rd[w] = {}
        return (sem, cnt)

    def multi(self, eng, items):
        sem = ENG_SEM[eng]
        me = self

        def body(e):
            for fn, reads, writes in items:
                deps = {}

                def add(d):
                    if d:
                        s, c = d
                        deps[s] = max(deps.get(s, 0), c)
                for r in reads:
                    add(me.lw.get(r))
                for w in writes:
                    add(me.lw.get(w))
                    for s, c in me.rd.get(w, {}).items():
                        add((s, c))
                for s, c in deps.items():
                    if me.waited[eng].get(s, 0) < c:
                        e.wait_ge(me.sem[s], c)
                        me.waited[eng][s] = c
                ins = fn(e)
                ins.then_inc(me.sem[sem], 1)
                me.cnt[sem] += 1
                cnt = me.cnt[sem]
                for r in reads:
                    d = me.rd.setdefault(r, {})
                    d[sem] = max(d.get(sem, 0), cnt)
                for w in writes:
                    me.lw[w] = (sem, cnt)
                    me.rd[w] = {}
        getattr(self.blk, eng)(body)

    def tag_dma(self, ins):
        ins.then_inc(self.sem[self._cur_dma_sem], 16)
        self.cnt[self._cur_dma_sem] += 16

    def barrier(self):
        marks = [(ENG_SEM[e], self.cnt[ENG_SEM[e]])
                 for e in ("tensor", "vector", "scalar", "gpsimd")]
        marks += [(s, self.cnt[s]) for s in DSYNC + DPOOL]
        marks = [(s, c) for (s, c) in marks if c > 0]
        for e in ("tensor", "vector", "scalar", "gpsimd", "sync"):
            me = self

            def body(en, e=e):
                for s, c in marks:
                    if s == ENG_SEM.get(e):
                        continue
                    if me.waited[e].get(s, 0) < c:
                        en.wait_ge(me.sem[s], c)
                        me.waited[e][s] = c
            getattr(self.blk, e)(body)


def wN_load(s, b, dst, name, rows, cols):
    src = b.A[name]
    if len(src.shape) == 3:
        src = src.rearrange("e a c -> (e a) c")
    r0, nk = rows
    c0, w = cols
    v = src[r0:r0 + nk * 128, c0:c0 + w].rearrange("(k p) a -> p k a", p=128)
    return s.dma_start(out=dst.rearrange("p (k a) -> p k a", a=w), in_=v)


def wT_load(s, b, dst, name, j, jw=128, k0=0, nk=None):
    v = b.wbf.get(name)
    if v is None:
        v = b.A[name + "_T"]
    if nk is None:
        nk = v.shape[0] // 128
    slab = v[k0 * 128:(k0 + nk) * 128,
             j * jw:(j + 1) * jw].rearrange("(k p) a -> p k a", p=128)
    return s.dma_start(out=dst.rearrange("p (k a) -> p k a", a=jw), in_=slab)


def mm_chain(b, out_ap, pairs, reads, wbank):
    def fn(t):
        last = None
        n = len(pairs)
        for i, (la, ra) in enumerate(pairs):
            last = t.matmul(out_ap, la, ra, start=(i == 0), stop=(i == n - 1))
        return last
    return b.op("tensor", fn, reads=reads, writes=(wbank,))


def bias_cols(b, st, tag, src_ap, n, rowbuf=None):
    nc = b.nc
    nk = n // 128
    rows = st.enter_context(nc.sbuf_tensor(f"{tag}_rows", [nk, 128], F32))
    col = st.enter_context(nc.sbuf_tensor(f"{tag}_col", [128, nk], F32))
    b.op("sync", lambda s: s.dma_start(
        out=rows[:], in_=src_ap.rearrange("(k f) -> k f", f=128)),
        writes=(f"{tag}_rows",))
    b.op("tensor", lambda t: t.transpose(
        b.psm[:, 0:nk], rows[0:nk, 0:128], b.bufs["c_idf"][0:nk, 0:nk]),
        reads=(f"{tag}_rows", "c_idf"), writes=("psm",))
    b.op("vector", lambda v: v.tensor_copy(col[:], b.psm[:, 0:nk]),
         reads=("psm",), writes=(f"{tag}_col",))
    return col, f"{tag}_col"


def build_program(spec, layers=(0, 1, 2, 3), taps=(), stage=None):
    nc = bass.Bass("TRN2", target_bir_lowering=False)
    stack = contextlib.ExitStack()
    with stack:
        with nc.Block() as blk:
            b = Bld(nc, blk, stack)
            b.stage = stage
            for name, (shape, dt) in spec.items():
                mdt = {"float32": F32, "bfloat16": BF16}[dt]
                b.A[name] = nc.dram_tensor(name, list(shape), mdt,
                                           kind="ExternalInput").ap()
            b.out_ap = nc.dram_tensor("out", [T, C], F32,
                                      kind="ExternalOutput").ap()
            b.tap_names = list(taps)
            build_body(b, layers)
    return nc


def build_body(b, layers):
    nc = b.nc
    A = b.A

    xres = b.sb("xres", [128, NT_T * 1024], F32)
    b.sb("xnT", [128, NT_C * 2048], BF16)
    b.sb("sT", [128, NT_C * 2048], BF16)
    b.sb("accT", [128, NT_C * 2048], BF16)
    b.sb("stat_m", [128, NT_T], F32)
    b.sb("stat_r", [128, NT_T], F32)
    b.sb("c_id128", [128, 128], BF16)
    b.sb("c_idf", [128, 128], F32)
    b.sb("c_ones128", [128, 128], BF16)
    b.sb("c_BB", [128, 128], BF16)
    b.sb("c_Hbc", [16, 1024], BF16)
    b.sb("c_H2", [2, 128], BF16)
    b.sb("c_del15", [32, 128], BF16)
    b.sb("c_del31", [32, 128], BF16)
    b.sb("e15s", [128, 16], F32)
    b.sb("e31s", [128, 16], F32)
    cfc = b.sb("c_fc", [128, 2], F32)
    b.sb("c_onesf", [65, 128], F32)
    nc.const_aps.register(F32, EPS_LN, cfc[:, 0:1])
    nc.const_aps.register(F32, 1.0, cfc[:, 1:2])

    b.pb = [b.ps(f"pb{i}", [128, 512], F32) for i in range(6)]
    b.pbf = b.ps("pbf", [128, 1024], BF16)
    b.psm = b.ps("psm", [128, 512], F32)

    for name in TRANS_STAGE:
        shp = TRANS_W[name]
        b.wbf[name] = nc.dram_tensor("wbf_" + name, [shp[1], shp[0]], BF16,
                                     kind="Internal").ap()
    b.toep = nc.dram_tensor("toep_dram", [16, 3 * 16384], F32,
                            kind="Internal").ap()

    def xload(s):
        last = None
        for q in range(4):
            if last is not None:
                b.tag_dma(last)
            xin = A["x_in"][q * 4 * 128:(q + 1) * 4 * 128, :].rearrange(
                "(i p) c -> p i c", p=128)
            last = s.dma_start(
                out=xres[:, q * 4096:(q + 1) * 4096].rearrange(
                    "p (i c) -> p i c", c=1024), in_=xin)
        return last
    b.op("sync", xload, writes=("xres",))
    for name in TRANS_STAGE:
        src = A[name + "_T"]
        b.op("gpsimd", lambda g, s=src, n=name: g.dma_start(out=b.wbf[n], in_=s),
             writes=("wbf_" + name,), dma=True, stage_dma=True)
    for dst, src in (("c_id128", A["id128"]), ("c_ones128", A["ones128"]),
                     ("c_BB", A["BB"]),
                     ("c_Hbc", A["Hbc"]), ("c_H2", A["H2"]),
                     ("c_del15", A["del15"]),
                     ("c_del31", A["del31"])):
        b.op("sync", lambda s, d=dst, sr=src: s.dma_start(out=b.bufs[d][:], in_=sr),
             writes=(dst,))
    b.op("gpsimd", lambda g: g.dma_start(out=b.bufs["c_idf"][:], in_=A["id128"]),
         writes=("c_idf",), dma=True)
    b.op("sync", lambda s: s.dma_start(out=cfc[:], in_=A["fconsts"]),
         writes=("c_fc",))
    b.op("sync", lambda s: s.dma_start(out=b.bufs["c_onesf"][:], in_=A["onesf"]),
         writes=("c_onesf",))

    if getattr(b, "stage", None) == "toep":
        with contextlib.ExitStack() as tst:
            emit_toep(b, tst)
            b.barrier()

    emitters = {0: emit_layer0, 1: emit_layer1, 2: emit_layer2, 3: emit_layer3}
    if getattr(b, "stage", None) in ("pass", "toep"):
        layers = ()
    b._need_toep = 2 in layers
    for L in layers:
        if L == 2 and b._need_toep:
            with contextlib.ExitStack() as tst:
                emit_toep(b, tst)
                b.barrier()
            b._need_toep = False
        emit_ln(b, L)
        b.barrier()
        if getattr(b, "stage", None) == "ln":
            break
        emitters[L](b)
        b.barrier()

    oout = b.out_ap.rearrange("(i p) c -> p i c", p=128)
    b.op("sync", lambda s: s.dma_start(
        out=oout, in_=xres[:].rearrange("p (i c) -> p i c", c=1024)),
        reads=("xres",))
    for tn in b.tap_names:
        buf = b.bufs[tn]
        dr = nc.dram_tensor("tap_" + tn, [buf.shape[0], buf.shape[1]],
                            buf.dtype, kind="ExternalOutput").ap()
        b.op("sync", lambda s, dr=dr, buf=buf: s.dma_start(out=dr, in_=buf[:]),
             reads=(tn,))
    me = b

    def fin(s):
        for sm in ("dstage",) + DSYNC + DPOOL:
            if me.cnt[sm]:
                s.wait_ge(me.sem[sm], me.cnt[sm])
    b.blk.sync(fin)


def emit_ln(b, L):
    nc = b.nc
    xres, xnT = b.bufs["xres"], b.bufs["xnT"]
    stat_m, stat_r = b.bufs["stat_m"], b.bufs["stat_r"]
    pbf = b.pbf
    with contextlib.ExitStack() as st:
        ssum = st.enter_context(nc.sbuf_tensor(f"ln{L}_sum", [128, NT_T], F32))
        ssq = st.enter_context(nc.sbuf_tensor(f"ln{L}_ssq", [128, NT_T], F32))
        scr = st.enter_context(nc.sbuf_tensor(f"ln{L}_scr", [128, 1024], F32))
        svar = st.enter_context(nc.sbuf_tensor(f"ln{L}_var", [128, NT_T], F32))
        xtk = st.enter_context(nc.sbuf_tensor(f"ln{L}_xtk", [128, NT_T * 1024], BF16))

        b.multi("vector", [
            (lambda v, i=i: v.reduce_sum(
                ssum[:, i:i + 1], xres[:, i * 1024:(i + 1) * 1024], axis=AX.X),
             ("xres",), ("ln_sum",))
            for i in range(NT_T)])
        b.multi("scalar", [
            (lambda a, i=i: a.activation(
                scr[:], xres[:, i * 1024:(i + 1) * 1024], AF.Square,
                accum_out=ssq[:, i:i + 1]),
             ("xres",), ("ln_ssq", "ln_scr"))
            for i in range(NT_T)])
        b.op("vector", lambda v: v.tensor_scalar_mul(stat_m[:], ssum[:], 1.0 / 1024),
             reads=("ln_sum",), writes=("stat_m",))
        b.op("vector", lambda v: v.tensor_mul(svar[:], stat_m[:], stat_m[:]),
             reads=("stat_m",), writes=("ln_var",))
        b.op("vector", lambda v: v.scalar_tensor_tensor(
            svar[:], ssq[:], 1.0 / 1024, svar[:], ALU.mult, ALU.subtract),
            reads=("ln_ssq", "ln_var"), writes=("ln_var",))
        b.op("scalar", lambda a: a.activation(stat_r[:], svar[:], AF.Ln, bias=EPS_LN),
             reads=("ln_var", "c_fc"), writes=("stat_r",))
        b.op("scalar", lambda a: a.activation(stat_r[:], stat_r[:], AF.Exp, scale=-0.5),
             reads=("stat_r",), writes=("stat_r",))
        b.multi("vector", [
            (lambda v, i=i: v.tensor_scalar(
                xtk[:, i * 1024:(i + 1) * 1024], xres[:, i * 1024:(i + 1) * 1024],
                stat_m[:, i:i + 1], stat_r[:, i:i + 1], ALU.subtract, ALU.mult),
             ("xres", "stat_m", "stat_r"), (f"ln_xtk{i}",))
            for i in range(NT_T)])
        for i in range(NT_T):
            def tr(t, i=i):
                last = None
                for k in range(8):
                    last = t.transpose(
                        pbf[:, k * 128:(k + 1) * 128],
                        xtk[:, i * 1024 + k * 128:i * 1024 + (k + 1) * 128],
                        b.bufs["c_id128"][:])
                return last
            b.op("tensor", tr, reads=(f"ln_xtk{i}", "c_id128"), writes=("pbf",))

            def ev(v, i=i):
                last = None
                for k in range(8):
                    last = v.tensor_copy(
                        xnT[:, k * 2048 + i * 128:k * 2048 + (i + 1) * 128],
                        pbf[:, k * 128:(k + 1) * 128])
                return last
            b.op("vector", ev, reads=("pbf",), writes=("xnT",))


def emit_toep(b, st):
    nc = b.nc
    A = b.A
    psm = b.psm
    if True:
        remb = st.enter_context(nc.sbuf_tensor("t5_remb", [32, 16], BF16))
        ohc = st.enter_context(nc.sbuf_tensor("t5_oh", [32, 2 * 512], BF16))
        ev = st.enter_context(nc.sbuf_tensor("t5_ev", [16, 2 * 512], F32))
        b.op("gpsimd", lambda g: g.dma_start(out=remb[:], in_=A["L2_attn_rel_emb"]),
             writes=("t5_remb",), dma=True)
        for j in range(96):
            par = j % 2
            ohs = ohc[:, par * 512:(par + 1) * 512]
            evs = ev[:, par * 512:(par + 1) * 512]
            b.op("sync", lambda s, j=j, ohs=ohs: s.dma_start(
                out=ohs, in_=A["OH3"][:, j * 512:(j + 1) * 512]),
                writes=(f"t5_oh{par}",))
            b.op("tensor", lambda t, ohs=ohs: t.matmul(
                psm[0:16, :], remb[:], ohs, start=True, stop=True),
                reads=(f"t5_oh{par}", "t5_remb"), writes=("psm",))
            b.op("vector", lambda v, evs=evs: v.tensor_copy(evs, psm[0:16, :]),
                 reads=("psm",), writes=(f"t5_ev{par}",))
            b.op("gpsimd", lambda g, j=j, evs=evs: g.dma_start(
                out=b.toep[:, j * 512:(j + 1) * 512], in_=evs),
                reads=(f"t5_ev{par}",), writes=("toep",), dma=True)
        for dn, dst in (("c_del15", "e15s"), ("c_del31", "e31s")):
            b.op("tensor", lambda t, dn=dn: t.matmul(
                psm[:, 0:16], b.bufs[dn][:], remb[:], start=True, stop=True),
                reads=(dn, "t5_remb"), writes=("psm",))
            b.op("vector", lambda v, dst=dst: v.tensor_copy(
                b.bufs[dst][:], psm[:, 0:16]), reads=("psm",), writes=(dst,))


def emit_topk_gates(b, st, tag, logitsT_buf, logits_reg, E, renorm, eps,
                    coT_buf=None):
    nc = b.nc
    co = st.enter_context(nc.sbuf_tensor(f"{tag}_co", [128, 2 * 32], F32))
    m1 = st.enter_context(nc.sbuf_tensor(f"{tag}_m1", [128, 8], F32))
    if coT_buf is None:
        coT_buf = st.enter_context(nc.sbuf_tensor(f"{tag}_coT", [E, 2048], BF16))
    coT_reg = f"{tag}_coT"
    lg = st.enter_context(nc.sbuf_tensor(f"{tag}_lg", [128, 2 * 32], F32))
    psm = b.psm
    for i in range(NT_T):
        par = i % 2
        lgs = lg[:, par * 32:par * 32 + E]
        cos = co[:, par * 32:par * 32 + E]
        b.op("tensor", lambda t, i=i: t.transpose(
            psm[:, 0:E], logitsT_buf[0:E, i * 128:(i + 1) * 128],
            b.bufs["c_idf"][0:E, 0:E]),
            reads=(logits_reg, "c_idf"), writes=("psm",))
        b.multi("vector", [
            (lambda v, lgs=lgs: v.tensor_copy(lgs, psm[:, 0:E]),
             ("psm",), (f"{tag}_lg{par}",)),
            (lambda v, lgs=lgs: v.reduce_max(
                m1[:, 0:1], lgs, axis=AX.X, negate=True),
             (f"{tag}_lg{par}",), (f"{tag}_m1",))])
        b.op("scalar", lambda a, lgs=lgs, cos=cos: a.activation(
            cos, lgs, AF.Exp, bias=m1[:, 0:1], accum_out=m1[:, 1:2]),
            reads=(f"{tag}_lg{par}", f"{tag}_m1"),
            writes=(f"{tag}_co{par}", f"{tag}_m1d"))
        items2 = [
            (lambda v: v.reciprocal(m1[:, 1:2], m1[:, 1:2]),
             (f"{tag}_m1d",), (f"{tag}_m1d",)),
            (lambda v, cos=cos: v.tensor_scalar_mul(cos, cos, m1[:, 1:2]),
             (f"{tag}_co{par}", f"{tag}_m1d"), (f"{tag}_co{par}",)),
            (lambda v, cos=cos: v.reduce_max(m1[:, 2:3], cos, axis=AX.X),
             (f"{tag}_co{par}",), (f"{tag}_m2",)),
            (lambda v, cos=cos, lgs=lgs: v.tensor_scalar(
                lgs, cos, m1[:, 2:3], None, ALU.not_equal),
             (f"{tag}_co{par}", f"{tag}_m2"), (f"{tag}_lg{par}",)),
            (lambda v, cos=cos, lgs=lgs: v.tensor_mul(lgs, lgs, cos),
             (f"{tag}_co{par}", f"{tag}_lg{par}"), (f"{tag}_lg{par}",)),
            (lambda v, lgs=lgs: v.reduce_max(m1[:, 3:4], lgs, axis=AX.X),
             (f"{tag}_lg{par}",), (f"{tag}_m3",)),
            (lambda v, cos=cos, lgs=lgs: v.tensor_scalar(
                lgs, cos, m1[:, 3:4], None, ALU.is_ge),
             (f"{tag}_co{par}", f"{tag}_m3"), (f"{tag}_lg{par}",)),
            (lambda v, cos=cos, lgs=lgs: v.tensor_mul(cos, cos, lgs),
             (f"{tag}_co{par}", f"{tag}_lg{par}"), (f"{tag}_co{par}",)),
        ]
        if renorm:
            items2 += [
                (lambda v: v.tensor_add(m1[:, 4:5], m1[:, 2:3], m1[:, 3:4]),
                 (f"{tag}_m2", f"{tag}_m3"), (f"{tag}_m4",)),
                (lambda v: v.tensor_scalar_add(m1[:, 4:5], m1[:, 4:5], eps),
                 (f"{tag}_m4",), (f"{tag}_m4",)),
                (lambda v: v.reciprocal(m1[:, 4:5], m1[:, 4:5]),
                 (f"{tag}_m4",), (f"{tag}_m4",)),
                (lambda v, cos=cos: v.tensor_scalar_mul(cos, cos, m1[:, 4:5]),
                 (f"{tag}_co{par}", f"{tag}_m4"), (f"{tag}_co{par}",)),
            ]
        b.multi("vector", items2)
        b.op("tensor", lambda t, cos=cos: t.transpose(
            psm[0:E, 0:128], cos, b.bufs["c_idf"][:]),
            reads=(f"{tag}_co{par}", "c_idf"), writes=("psm",))
        b.op("vector", lambda v, i=i: v.tensor_copy(
            coT_buf[0:E, i * 128:(i + 1) * 128], psm[0:E, 0:128]),
            reads=("psm",), writes=(coT_reg,))
    return coT_buf, coT_reg


def emit_combine(b, L):
    xres, xnT, sT, accT = (b.bufs["xres"], b.bufs["xnT"], b.bufs["sT"],
                           b.bufs["accT"])
    pbf = b.pbf
    b.multi("scalar", [
        (lambda a, j=j: a.activation(
            xnT[:, j * 2048:(j + 1) * 2048], accT[:, j * 2048:(j + 1) * 2048],
            AF.Silu),
         ("accT",), ("xnT",))
        for j in range(8)])
    b.multi("vector", [
        (lambda v, sl=slice(j * 2048, (j + 1) * 2048): v.tensor_mul(
            xnT[:, sl], xnT[:, sl], sT[:, sl]),
         ("xnT", "sT"), ("xnT",))
        for j in range(8)])
    for i in range(NT_T):
        def tr(t, i=i):
            last = None
            for j in range(8):
                last = t.transpose(
                    pbf[:, j * 128:(j + 1) * 128],
                    xnT[:, j * 2048 + i * 128:j * 2048 + (i + 1) * 128],
                    b.bufs["c_id128"][:])
            return last
        b.op("tensor", tr, reads=("xnT", "c_id128"), writes=("pbf",))

        def ad(v, i=i):
            last = None
            for j in range(8):
                last = v.tensor_add(
                    xres[:, i * 1024 + j * 128:i * 1024 + (j + 1) * 128],
                    xres[:, i * 1024 + j * 128:i * 1024 + (j + 1) * 128],
                    pbf[:, j * 128:(j + 1) * 128])
            return last
        b.op("vector", ad, reads=("pbf", "xres"), writes=("xres",))


def emit_layer0(b):
    nc = b.nc
    A = b.A
    xnT, sT, accT = b.bufs["xnT"], b.bufs["sT"], b.bufs["accT"]
    pb, pbf, psm = b.pb, b.pbf, b.psm

    with contextlib.ExitStack() as st:
        att = accT
        w3 = st.enter_context(nc.sbuf_tensor("l0_w3", [128, 3 * 1024], BF16))
        qe = st.enter_context(nc.sbuf_tensor("l0_qe", [128, 2048], BF16))
        ke = st.enter_context(nc.sbuf_tensor("l0_ke", [128, 2048], BF16))
        vt = st.enter_context(nc.sbuf_tensor("l0_vt", [128, 2048], BF16))
        ktok = st.enter_context(nc.sbuf_tensor("l0_ktok", [128, 2048], BF16))
        vtok = st.enter_context(nc.sbuf_tensor("l0_vtok", [128, 2048], BF16))
        dvb = st.enter_context(nc.sbuf_tensor("l0_dvb", [128, 4 * 512], F32))
        ksum = st.enter_context(nc.sbuf_tensor("l0_ksum", [128, 1], F32))
        gkb = st.enter_context(nc.sbuf_tensor("l0_gkb", [128, 128], BF16))
        ctxs = st.enter_context(nc.sbuf_tensor("l0_ctx", [128, 128], BF16))
        rowb = st.enter_context(nc.sbuf_tensor("l0_rowb", [1, 128], F32))
        bq, bq_r = bias_cols(b, st, "l0_bq", A["L0_attn_bq"], 1024, rowb)
        bk, bk_r = bias_cols(b, st, "l0_bk", A["L0_attn_bk"], 1024, rowb)
        bv, bv_r = bias_cols(b, st, "l0_bv", A["L0_attn_bv"], 1024, rowb)
        bp, bp_r = bias_cols(b, st, "l0_bp", A["L0_attn_bp"], 1024, rowb)

        for ct in range(8):
            for wi, nm in enumerate(("Wq", "Wk", "Wv")):
                b.op("gpsimd", lambda s, wi=wi, nm=nm, ct=ct: wT_load(
                    s, b, w3[:, wi * 1024:(wi + 1) * 1024], f"L0_attn_{nm}", ct),
                    writes=(f"l0_w3_{wi}",), dma=True)
            for wi, (dst, dreg, bcol, breg, actf) in enumerate((
                    (qe, "l0_qe", bq, bq_r, AF.Exp),
                    (ke, "l0_ke", bk, bk_r, AF.Exp),
                    (vt, "l0_vt", bv, bv_r, None))):
                for tb in range(NTB):
                    pairs = [(w3[:, wi * 1024 + k * 128:wi * 1024 + (k + 1) * 128],
                              xnT[:, k * 2048 + tb * 512:k * 2048 + (tb + 1) * 512])
                             for k in range(8)]
                    mm_chain(b, pb[tb][:], pairs, reads=(f"l0_w3_{wi}", "xnT"),
                             wbank=f"pb{tb}")
                    if actf is None:
                        b.op("vector", lambda v, dst=dst, tb=tb, bcol=bcol,
                             ct=ct: v.tensor_scalar_add(
                            dst[:, tb * 512:(tb + 1) * 512], pb[tb][:],
                            bcol[:, ct:ct + 1]),
                            reads=(f"pb{tb}", breg), writes=(dreg,))
                if actf is not None:
                    b.multi("scalar", [
                        (lambda a, dst=dst, tb=tb, bcol=bcol, actf=actf,
                         ct=ct: a.activation(
                            dst[:, tb * 512:(tb + 1) * 512], pb[tb][:], actf,
                            bias=bcol[:, ct:ct + 1]),
                         (f"pb{tb}", breg), (dreg,))
                        for tb in range(NTB)])
            for qi, (srcb, sreg) in enumerate(((qe, "l0_qe"), (ke, "l0_ke"))):
                b.multi("tensor", [
                    (lambda t, tb=tb, srcb=srcb: t.matmul(
                        pb[tb][:], b.bufs["c_BB"][:],
                        srcb[:, tb * 512:(tb + 1) * 512],
                        start=True, stop=True),
                     ("c_BB", sreg), (f"pb{tb}",))
                    for tb in range(NTB)])
                b.multi("scalar",
                        [(lambda a, tb=tb: a.activation(
                            dvb[:, tb * 512:(tb + 1) * 512], pb[tb][:], AF.Ln),
                          (f"pb{tb}",), ("l0_dvb",)) for tb in range(NTB)]
                        + [(lambda a, tb=tb: a.activation(
                            dvb[:, tb * 512:(tb + 1) * 512],
                            dvb[:, tb * 512:(tb + 1) * 512], AF.Exp, scale=-1.0),
                            ("l0_dvb",), ("l0_dvb",)) for tb in range(NTB)])
                b.multi("gpsimd", [
                    (lambda g, srcb=srcb, tb=tb: g.tensor_mul(
                        srcb[:, tb * 512:(tb + 1) * 512],
                        srcb[:, tb * 512:(tb + 1) * 512],
                        dvb[:, tb * 512:(tb + 1) * 512]),
                     ("l0_dvb", sreg), (sreg,))
                    for tb in range(NTB)])
            b.op("vector", lambda v: v.reduce_sum(ksum[:], ke[:], axis=AX.X),
                 reads=("l0_ke",), writes=("l0_ksum",))
            b.op("vector", lambda v: v.tensor_scalar_mul(
                gkb[:], b.bufs["c_BB"][:], ksum[:, 0:1]),
                reads=("c_BB", "l0_ksum"), writes=("l0_gkb",))
            for srcb, sreg, dstb, dreg in ((ke, "l0_ke", ktok, "l0_ktok"),
                                           (vt, "l0_vt", vtok, "l0_vtok")):
                for h in range(2):
                    def tr(t, srcb=srcb, h=h):
                        last = None
                        for tt in range(8):
                            i = h * 8 + tt
                            last = t.transpose(
                                pbf[:, tt * 128:(tt + 1) * 128],
                                srcb[:, i * 128:(i + 1) * 128],
                                b.bufs["c_id128"][:])
                        return last
                    b.op("tensor", tr, reads=(sreg, "c_id128"), writes=("pbf",))

                    def evv(v, dstb=dstb, h=h):
                        last = None
                        for tt in range(8):
                            i = h * 8 + tt
                            last = v.tensor_copy(dstb[:, i * 128:(i + 1) * 128],
                                                 pbf[:, tt * 128:(tt + 1) * 128])
                        return last
                    b.op("vector", evv, reads=("pbf",), writes=(dreg,))
            mm_chain(b, pb[3][:, 0:128],
                     [(ktok[:, i * 128:(i + 1) * 128], vtok[:, i * 128:(i + 1) * 128])
                      for i in range(16)],
                     reads=("l0_ktok", "l0_vtok"), wbank="pb3")
            b.op("vector", lambda v: v.tensor_copy(ctxs[:], pb[3][:, 0:128]),
                 reads=("pb3",), writes=("l0_ctxs",))
            dbk = (5, 2, 1, 0)
            b.multi("tensor", [
                (lambda t, tb=tb: t.matmul(
                    pb[dbk[tb]][:], gkb[:], qe[:, tb * 512:(tb + 1) * 512],
                    start=True, stop=True),
                 ("l0_gkb", "l0_qe"), (f"pb{dbk[tb]}",))
                for tb in range(NTB)])
            b.multi("scalar",
                    [(lambda a, tb=tb: a.activation(
                        dvb[:, tb * 512:(tb + 1) * 512], pb[dbk[tb]][:], AF.Ln),
                      (f"pb{dbk[tb]}",), ("l0_dvb",)) for tb in range(NTB)]
                    + [(lambda a, tb=tb: a.activation(
                        dvb[:, tb * 512:(tb + 1) * 512],
                        dvb[:, tb * 512:(tb + 1) * 512], AF.Exp, scale=-1.0),
                        ("l0_dvb",), ("l0_dvb",)) for tb in range(NTB)])
            for tb in range(NTB):
                def mm2(t, tb=tb):
                    t.matmul(pb[4][0:64, :], ctxs[0:64, 0:64],
                             qe[0:64, tb * 512:(tb + 1) * 512], start=True, stop=True)
                    return t.matmul(pb[4][64:128, :], ctxs[64:128, 64:128],
                                    qe[64:128, tb * 512:(tb + 1) * 512],
                                    start=True, stop=True)
                b.op("tensor", mm2, reads=("l0_ctxs", "l0_qe"), writes=("pb4",))
                sl = slice(ct * 2048 + tb * 512, ct * 2048 + (tb + 1) * 512)
                b.op("vector", lambda v, sl=sl, tb=tb: v.tensor_mul(
                    att[:, sl], pb[4][:], dvb[:, tb * 512:(tb + 1) * 512]),
                    reads=("pb4", "l0_dvb"), writes=("accT",))
                b.op("vector", lambda v, sl=sl, tb=tb: v.tensor_add(
                    att[:, sl], att[:, sl], qe[:, tb * 512:(tb + 1) * 512]),
                    reads=("accT", "l0_qe"), writes=("accT",))
        for j in range(8):
            b.op("gpsimd", lambda s, j=j: wT_load(
                s, b, w3[:, 0:1024], "L0_attn_Wp", j),
                writes=("l0_w3_0",), dma=True)
            for tb in range(NTB):
                bank = pb[tb % 2]
                pairs = [(w3[:, k * 128:(k + 1) * 128],
                          att[:, k * 2048 + tb * 512:k * 2048 + (tb + 1) * 512])
                         for k in range(8)]
                mm_chain(b, bank[:], pairs, reads=("l0_w3_0", "accT"),
                         wbank=f"pb{tb % 2}")
                b.op("vector", lambda v, j=j, tb=tb, bank=bank: v.tensor_scalar_add(
                    sT[:, j * 2048 + tb * 512:j * 2048 + (tb + 1) * 512], bank[:],
                    bp[:, j:j + 1]),
                    reads=(f"pb{tb % 2}", bp_r), writes=("sT",))
    b.barrier()
    if getattr(b, "stage", None) == "attn":
        return
    emit_l0_ffn(b)
    if getattr(b, "stage", None) == "ffn":
        return
    emit_combine(b, 0)


def emit_l0_ffn(b):
    nc = b.nc
    A = b.A
    xnT, accT = b.bufs["xnT"], b.bufs["accT"]
    pb, psm = b.pb, b.psm
    E = 16
    with contextlib.ExitStack() as st:
        ish = st.enter_context(nc.sbuf_tensor("l0f_ish", [128, 8 * 512], BF16))
        coT = st.enter_context(nc.sbuf_tensor("l0f_coT", [16, 2048], BF16))
        with contextlib.ExitStack() as stg:
            lgT = stg.enter_context(nc.sbuf_tensor("l0f_lgT", [16, 2048], F32))
            wrt = stg.enter_context(nc.sbuf_tensor("l0f_wrt", [128, 8 * 16], BF16))
            b.op("gpsimd", lambda g: wN_load(
                g, b, wrt[:], "L0_ffn_router_w", (0, 8), (0, 16)),
                writes=("l0f_wrt",), dma=True)
            for tb in range(NTB):
                pairs = [(wrt[:, k * 16:(k + 1) * 16],
                          xnT[:, k * 2048 + tb * 512:k * 2048 + (tb + 1) * 512])
                         for k in range(8)]
                mm_chain(b, psm[0:16, :], pairs, reads=("l0f_wrt", "xnT"), wbank="psm")
                b.op("vector", lambda v, tb=tb: v.tensor_copy(
                    lgT[:, tb * 512:(tb + 1) * 512], psm[0:16, :]),
                    reads=("psm",), writes=("l0f_lgT",))
            coT, coreg = emit_topk_gates(b, stg, "l0f", lgT, "l0f_lgT", E,
                                         renorm=False, eps=0.0, coT_buf=coT)
            b.barrier()

        st2 = st.enter_context(contextlib.ExitStack())
        if getattr(b, "_need_toep", False):
            emit_toep(b, st2)
            b._need_toep = False
        wdns = [st2.enter_context(nc.sbuf_tensor(f"l0f_wdn{h}", [128, 1024], BF16))
                for h in range(2)]
        wgu = st2.enter_context(nc.sbuf_tensor("l0f_wgu", [128, 2 * 2048], BF16))
        sgS = st2.enter_context(nc.sbuf_tensor("l0f_sg", [128, 4 * 512], BF16))
        for tb in range(NTB):
            for g4 in range(2):
                for j4 in range(4):
                    j = g4 * 4 + j4
                    par = j % 2
                    wg_s = wgu[:, par * 2048:par * 2048 + 1024]
                    b.op("gpsimd", lambda g, j=j, wg_s=wg_s: wN_load(
                        g, b, wg_s, "L0_ffn_shared_gate_w", (0, 8), (j * 128, 128)),
                        writes=(f"l0f_wgu{par}",), dma=True)
                    pairs = [(wg_s[:, k * 128:(k + 1) * 128],
                              xnT[:, k * 2048 + tb * 512:k * 2048 + (tb + 1) * 512])
                             for k in range(8)]
                    mm_chain(b, pb[j4][:], pairs, reads=(f"l0f_wgu{par}", "xnT"),
                             wbank=f"pb{j4}")
                b.multi("scalar", [
                    (lambda a, j4=j4: a.activation(
                        sgS[:, j4 * 512:(j4 + 1) * 512], pb[j4][:], AF.Silu),
                     (f"pb{j4}",), (f"l0f_sg{j4}",))
                    for j4 in range(4)])
                for j4 in range(4):
                    j = g4 * 4 + j4
                    par = j % 2
                    wu_s = wgu[:, par * 2048 + 1024:par * 2048 + 2048]
                    b.op("gpsimd", lambda g, j=j, wu_s=wu_s: wN_load(
                        g, b, wu_s, "L0_ffn_shared_up_w", (0, 8), (j * 128, 128)),
                        writes=(f"l0f_wguu{par}",), dma=True)
                    pairs = [(wu_s[:, k * 128:(k + 1) * 128],
                              xnT[:, k * 2048 + tb * 512:k * 2048 + (tb + 1) * 512])
                             for k in range(8)]
                    mm_chain(b, pb[j4][:], pairs, reads=(f"l0f_wguu{par}", "xnT"),
                             wbank=f"pb{j4}")
                    b.op("vector", lambda v, j=j, j4=j4: v.tensor_mul(
                        ish[:, j * 512:(j + 1) * 512], pb[j4][:],
                        sgS[:, j4 * 512:(j4 + 1) * 512]),
                        reads=(f"pb{j4}", f"l0f_sg{j4}"), writes=(f"l0f_ish{j}",))
            for hc in range(8):
                bank = pb[4 + hc % 2]
                slot, reg = wdns[hc % 2], f"l0f_wdn{hc % 2}"
                b.op("gpsimd", lambda g, hc=hc, slot=slot: wN_load(
                    g, b, slot[:], "L0_ffn_shared_down_w",
                    (0, 8), (hc * 128, 128)),
                    writes=(reg,), dma=True)
                pairs = [(slot[:, k * 128:(k + 1) * 128],
                          ish[:, k * 512:(k + 1) * 512]) for k in range(8)]
                mm_chain(b, bank[:], pairs,
                         reads=tuple([reg] + [f"l0f_ish{k}" for k in range(8)]),
                         wbank=f"pb{4 + hc % 2}")
                b.op("vector", lambda v, hc=hc, tb=tb, bank=bank: v.tensor_copy(
                    accT[:, hc * 2048 + tb * 512:hc * 2048 + (tb + 1) * 512],
                    bank[:]),
                    reads=(f"pb{4 + hc % 2}",), writes=("accT",))
        b.barrier()
        st.callback(lambda: None)
        st2.close()

        wge = st.enter_context(nc.sbuf_tensor("l0f_wge", [128, 8 * 512], BF16))
        wue = st.enter_context(nc.sbuf_tensor("l0f_wue", [128, 8 * 512], BF16))
        wde = st.enter_context(nc.sbuf_tensor("l0f_wde", [128, 4 * 1024], BF16))
        cbc = st.enter_context(nc.sbuf_tensor("l0f_cbc", [128, 512], F32))
        rsb = st.enter_context(nc.sbuf_tensor("l0f_rsb", [20, 128], BF16))
        sg4 = st.enter_context(nc.sbuf_tensor("l0f_sg4", [128, 4 * 512], BF16))
        for e in range(E):
            b.op("gpsimd", lambda g, e=e: wN_load(
                g, b, wde[:], "L0_ffn_down_w", (e * 512, 4), (0, 1024)),
                writes=("l0f_wde",), dma=True)
            b.op("gpsimd", lambda g, e=e: wN_load(
                g, b, wge[:], "L0_ffn_gate_w", (e * 1024, 8), (0, 512)),
                writes=("l0f_wge",), dma=True)
            b.op("gpsimd", lambda g, e=e: wN_load(
                g, b, wue[:], "L0_ffn_up_w", (e * 1024, 8), (0, 512)),
                writes=("l0f_wue",), dma=True)
            b.op("sync", lambda s, e=e: s.dma_start(
                out=rsb[0:E, :], in_=b.A["rowsel"][0:E, e * 128:(e + 1) * 128]),
                writes=("rsb",))
            for tb in range(NTB):
                mm_chain(b, psm[:],
                         [(rsb[0:E, 0:128],
                           coT[0:E, tb * 512:(tb + 1) * 512])],
                         reads=("rsb", coreg), wbank="psm")
                b.op("vector", lambda v: v.tensor_copy(cbc[:], psm[:]),
                     reads=("psm",), writes=("l0f_cbc",))
                for jj in range(4):
                    pairs = [(wge[:, k * 512 + jj * 128:k * 512 + (jj + 1) * 128],
                              xnT[:, k * 2048 + tb * 512:k * 2048 + (tb + 1) * 512])
                             for k in range(8)]
                    mm_chain(b, pb[jj][:], pairs, reads=("l0f_wge", "xnT"),
                             wbank=f"pb{jj}")
                b.multi("scalar", [
                    (lambda a, jj=jj: a.activation(
                        sg4[:, jj * 512:(jj + 1) * 512], pb[jj][:], AF.Silu),
                     (f"pb{jj}",), (f"l0f_sg4{jj}",))
                    for jj in range(4)])
                for jj in range(4):
                    b.op("gpsimd", lambda g, jj=jj: g.tensor_mul(
                        sg4[:, jj * 512:(jj + 1) * 512],
                        sg4[:, jj * 512:(jj + 1) * 512], cbc[:]),
                        reads=(f"l0f_sg4{jj}", "l0f_cbc"), writes=(f"l0f_sg4{jj}",))
                for jj in range(4):
                    pairs = [(wue[:, k * 512 + jj * 128:k * 512 + (jj + 1) * 128],
                              xnT[:, k * 2048 + tb * 512:k * 2048 + (tb + 1) * 512])
                             for k in range(8)]
                    mm_chain(b, pb[jj][:], pairs, reads=("l0f_wue", "xnT"),
                             wbank=f"pb{jj}")
                    b.op("vector", lambda v, jj=jj: v.tensor_mul(
                        ish[:, jj * 512:(jj + 1) * 512], pb[jj][:],
                        sg4[:, jj * 512:(jj + 1) * 512]),
                        reads=(f"pb{jj}", f"l0f_sg4{jj}"), writes=(f"l0f_ish{jj}",))
                for hc in range(8):
                    bank = pb[4 + hc % 2]
                    pairs = [(wde[:, k * 1024 + hc * 128:k * 1024 + (hc + 1) * 128],
                              ish[:, k * 512:(k + 1) * 512]) for k in range(4)]
                    mm_chain(b, bank[:], pairs,
                             reads=tuple(["l0f_wde"] + [f"l0f_ish{k}" for k in range(4)]),
                             wbank=f"pb{4 + hc % 2}")
                    sl = slice(hc * 2048 + tb * 512, hc * 2048 + (tb + 1) * 512)
                    b.op("vector", lambda v, sl=sl, bank=bank: v.tensor_add(
                        accT[:, sl], accT[:, sl], bank[:]),
                        reads=(f"pb{4 + hc % 2}", "accT"), writes=("accT",))
    b.barrier()


def emit_layer1(b):
    nc = b.nc
    A = b.A
    xnT, sT = b.bufs["xnT"], b.bufs["sT"]
    pb, psm = b.pb, b.psm

    with contextlib.ExitStack() as st:
        wdt = st.enter_context(nc.sbuf_tensor("l1_wdt", [128, 1024], BF16))
        abc = st.enter_context(nc.sbuf_tensor("l1_abc", [128, 33], F32))
        dt = st.enter_context(nc.sbuf_tensor("l1_dt", [128, 2048], F32))
        du = st.enter_context(nc.sbuf_tensor("l1_du", [128, 1024], F32))
        aas = [st.enter_context(nc.sbuf_tensor(f"l1_aa{h}", [128, 1024], F32))
               for h in range(4)]
        scr = aas[2]
        sss = [st.enter_context(nc.sbuf_tensor(f"l1_ss{h}", [128, 1024], F32))
               for h in range(2)]
        yy = st.enter_context(nc.sbuf_tensor("l1_yy", [128, 1024], F32))
        sscar = st.enter_context(nc.sbuf_tensor("l1_sscar", [128, 16], F32))
        bdt, bdt_r = bias_cols(b, st, "l1_bdt", A["L1_attn_b_dt"], 1024)
        for j in range(8):
            b.op("sync", lambda s, j=j: s.dma_start(
                out=abc[:, 0:16], in_=A["L1_attn_A_log"][j * 128:(j + 1) * 128, :]),
                writes=("l1_abcA",))
            b.op("sync", lambda s, j=j: s.dma_start(
                out=abc[:, 16:32], in_=A["L1_attn_B_mat"][j * 128:(j + 1) * 128, :]),
                writes=("l1_abcB",))
            b.op("gpsimd", lambda g, j=j: g.dma_start(
                out=scr[:, 0:16], in_=A["L1_attn_C_mat"][j * 128:(j + 1) * 128, :]),
                writes=("l1_aa2",), dma=True)
            b.op("sync", lambda s, j=j: s.dma_start(
                out=abc[:, 32:33],
                in_=A["L1_attn_D_vec"][j * 128:(j + 1) * 128].rearrange("(p o) -> p o", o=1)),
                writes=("l1_abcD",))
            b.op("scalar", lambda a: a.activation(abc[:, 0:16], abc[:, 0:16], AF.Exp),
                 reads=("l1_abcA",), writes=("l1_abcA",))
            b.op("vector", lambda v: v.tensor_scalar_mul(abc[:, 0:16], abc[:, 0:16], -1.0),
                 reads=("l1_abcA",), writes=("l1_abcA",))
            b.op("vector", lambda v: v.tensor_mul(abc[:, 16:32], abc[:, 16:32], scr[:, 0:16]),
                 reads=("l1_abcB", "l1_aa2"), writes=("l1_abcB",))
            b.op("gpsimd", lambda s, j=j: wT_load(s, b, wdt[:], "L1_attn_W_dt", j),
                 writes=("l1_wdt",), dma=True)
            for tb in range(NTB):
                pairs = [(wdt[:, k * 128:(k + 1) * 128],
                          xnT[:, k * 2048 + tb * 512:k * 2048 + (tb + 1) * 512])
                         for k in range(8)]
                mm_chain(b, pb[tb % 2][:], pairs, reads=("l1_wdt", "xnT"),
                         wbank=f"pb{tb % 2}")
                b.op("vector", lambda v, tb=tb, j=j: v.tensor_scalar_add(
                    dt[:, tb * 512:(tb + 1) * 512], pb[tb % 2][:],
                    bdt[:, j:j + 1]),
                    reads=(f"pb{tb % 2}", bdt_r), writes=("l1_dt",))
            for hf in range(2):
                sl = slice(hf * 1024, (hf + 1) * 1024)
                b.multi("scalar", [
                    (lambda a, sl=sl: a.activation(scr[:], dt[:, sl], AF.Abs),
                     ("l1_dt",), ("l1_aa2",)),
                    (lambda a: a.activation(scr[:], scr[:], AF.Exp, scale=-1.0),
                     ("l1_aa2",), ("l1_aa2",)),
                    (lambda a: a.activation(scr[:], scr[:], AF.Ln, bias=1.0),
                     ("l1_aa2", "c_fc"), ("l1_aa2",))])
                b.op("vector", lambda v, sl=sl: v.tensor_relu(dt[:, sl], dt[:, sl]),
                     reads=("l1_dt",), writes=("l1_dt",))
                b.op("vector", lambda v, sl=sl: v.tensor_add(dt[:, sl], dt[:, sl], scr[:]),
                     reads=("l1_dt", "l1_aa2"), writes=("l1_dt",))
                b.op("vector", lambda v, j=j, sl=sl, hf=hf: v.tensor_mul(
                    du[:], dt[:, sl], xnT[:, j * 2048 + hf * 1024:j * 2048 + (hf + 1) * 1024]),
                    reads=("l1_dt", "xnT"), writes=("l1_du",))
                b.op("vector", lambda v, j=j, hf=hf: v.tensor_scalar_mul(
                    yy[:], xnT[:, j * 2048 + hf * 1024:j * 2048 + (hf + 1) * 1024],
                    abc[:, 32:33]),
                    reads=("xnT", "l1_abcD"), writes=("l1_yy",))
                for n2 in range(8):
                    b.multi("scalar", [
                        (lambda a, n=n2 * 2 + i, sl=sl: a.activation(
                            aas[n % 4][:], dt[:, sl], AF.Exp,
                            scale=abc[:, n:n + 1]),
                         ("l1_dt", "l1_abcA"), (f"l1_aa{(n2 * 2 + i) % 4}",))
                        for i in range(2)])
                    items = []
                    for i in range(2):
                        n = n2 * 2 + i
                        aa, ss = aas[n % 4], sss[i]
                        ai = n % 4
                        if hf == 0:
                            items.append(
                                (lambda v, aa=aa, ss=ss: v.tensor_tensor_scan(
                                    ss[:], aa[:], du[:], 0.0, ALU.mult, ALU.add),
                                 (f"l1_aa{ai}", "l1_du"), (f"l1_ss{i}",)))
                            items.append(
                                (lambda v, n=n, ss=ss: v.tensor_copy(
                                    sscar[:, n:n + 1], ss[:, 1023:1024]),
                                 (f"l1_ss{i}",), ("l1_sscar",)))
                        else:
                            items.append(
                                (lambda v, n=n, aa=aa, ss=ss:
                                 v.tensor_tensor_scan(
                                    ss[:], aa[:], du[:], sscar[:, n:n + 1],
                                    ALU.mult, ALU.add),
                                 (f"l1_aa{ai}", "l1_du", "l1_sscar"),
                                 (f"l1_ss{i}",)))
                        items.append(
                            (lambda v, n=n, ss=ss: v.scalar_tensor_tensor(
                                yy[:], ss[:], abc[:, 16 + n:17 + n], yy[:],
                                ALU.mult, ALU.add),
                             (f"l1_ss{i}", "l1_abcB", "l1_yy"), ("l1_yy",)))
                    b.multi("vector", items)
                b.op("vector", lambda v, j=j, hf=hf: v.tensor_copy(
                    sT[:, j * 2048 + hf * 1024:j * 2048 + (hf + 1) * 1024], yy[:]),
                    reads=("l1_yy",), writes=("sT",))
    b.barrier()
    emit_l1_ffn(b)
    emit_combine(b, 1)


def emit_l1_ffn(b):
    nc = b.nc
    A = b.A
    xnT, accT = b.bufs["xnT"], b.bufs["accT"]
    pb, psm = b.pb, b.psm
    E = 20
    with contextlib.ExitStack() as st:
        coT = st.enter_context(nc.sbuf_tensor("l1f_coT", [20, 2048], BF16))
        stg = st.enter_context(contextlib.ExitStack())
        wg1 = stg.enter_context(nc.sbuf_tensor("l1f_wg1", [128, 2 * 1024], BF16))
        hT = stg.enter_context(nc.sbuf_tensor("l1f_hT", [128, 2 * 2048], BF16))
        wg2 = stg.enter_context(nc.sbuf_tensor("l1f_wg2", [128, 2 * 20], BF16))
        lgT = stg.enter_context(nc.sbuf_tensor("l1f_lgT", [20, 2048], F32))
        bg2c = stg.enter_context(nc.sbuf_tensor("l1f_bg2", [20, 1], F32))
        bg1, bg1_r = bias_cols(b, stg, "l1f_bg1", A["L1_ffn_bg1"], 256)
        for j in range(2):
            b.op("gpsimd", lambda g, j=j: wN_load(
                g, b, wg1[:, j * 1024:(j + 1) * 1024], "L1_ffn_wg1",
                (0, 8), (j * 128, 128)),
                writes=("l1f_wg1",), dma=True)
        for j in range(2):
            for tb in range(NTB):
                pairs = [(wg1[:, j * 1024 + k * 128:j * 1024 + (k + 1) * 128],
                          xnT[:, k * 2048 + tb * 512:k * 2048 + (tb + 1) * 512])
                         for k in range(8)]
                mm_chain(b, pb[tb % 2][:], pairs, reads=("l1f_wg1", "xnT"),
                         wbank=f"pb{tb % 2}")
                b.op("scalar", lambda a, j=j, tb=tb: a.activation(
                    hT[:, j * 2048 + tb * 512:j * 2048 + (tb + 1) * 512],
                    pb[tb % 2][:], AF.Gelu, bias=bg1[:, j:j + 1]),
                    reads=(f"pb{tb % 2}", bg1_r), writes=("l1f_hT",))
        b.op("gpsimd", lambda g: wN_load(
            g, b, wg2[:], "L1_ffn_wg2", (0, 2), (0, 20)),
            writes=("l1f_wg2",), dma=True)
        b.op("sync", lambda s: s.dma_start(
            out=bg2c[:], in_=A["L1_ffn_bg2"][0:20].rearrange("(p o) -> p o", o=1)),
            writes=("l1f_bg2c",))
        for tb in range(NTB):
            pairs = [(wg2[:, k * 20:(k + 1) * 20],
                      hT[:, k * 2048 + tb * 512:k * 2048 + (tb + 1) * 512])
                     for k in range(2)]
            mm_chain(b, psm[0:20, :], pairs, reads=("l1f_wg2", "l1f_hT"),
                     wbank="psm")
            b.op("vector", lambda v, tb=tb: v.tensor_scalar_add(
                lgT[0:20, tb * 512:(tb + 1) * 512], psm[0:20, :],
                bg2c[0:20, 0:1]),
                reads=("psm", "l1f_bg2c"), writes=("l1f_lgT",))
        coT, coreg = emit_topk_gates(b, stg, "l1f", lgT, "l1f_lgT", E,
                                     renorm=False, eps=0.0, coT_buf=coT)
        b.barrier()
        stg.close()
        wbe = st.enter_context(nc.sbuf_tensor("l1f_wbe", [20, 1024], BF16))
        b.op("gpsimd", lambda g: g.dma_start(out=wbe[:], in_=A["L1_ffn_be"]),
             writes=("l1f_wbe",), dma=True)
        for hc in range(8):
            for tb in range(NTB):
                bank = pb[tb % 2]
                mm_chain(b, bank[:],
                         [(wbe[0:E, hc * 128:(hc + 1) * 128],
                           coT[0:E, tb * 512:(tb + 1) * 512])],
                         reads=("l1f_wbe", coreg), wbank=f"pb{tb % 2}")
                b.op("vector", lambda v, hc=hc, tb=tb, bank=bank: v.tensor_copy(
                    accT[:, hc * 2048 + tb * 512:hc * 2048 + (tb + 1) * 512],
                    bank[:]),
                    reads=(f"pb{tb % 2}",), writes=("accT",))
        we = st.enter_context(nc.sbuf_tensor("l1f_we", [128, 8 * 1024], BF16))
        xs = st.enter_context(nc.sbuf_tensor("l1f_xs", [128, 8 * 512], BF16))
        cbc = st.enter_context(nc.sbuf_tensor("l1f_cbc", [128, 512], F32))
        rsb = st.enter_context(nc.sbuf_tensor("l1f_rsb", [20, 128], BF16))
        for e in range(E):
            b.op("gpsimd", lambda g, e=e: wN_load(
                g, b, we[:], "L1_ffn_We", (e * 1024, 8), (0, 1024)),
                writes=("l1f_we",), dma=True)
            b.op("sync", lambda s, e=e: s.dma_start(
                out=rsb[0:E, :], in_=b.A["rowsel"][0:E, e * 128:(e + 1) * 128]),
                writes=("rsb",))
            for tb in range(NTB):
                mm_chain(b, psm[:],
                         [(rsb[0:E, 0:128],
                           coT[0:E, tb * 512:(tb + 1) * 512])],
                         reads=("rsb", coreg), wbank="psm")
                b.op("vector", lambda v: v.tensor_copy(cbc[:], psm[:]),
                     reads=("psm",), writes=("l1f_cbc",))
                for k in range(8):
                    b.op("gpsimd", lambda g, k=k, tb=tb: g.tensor_mul(
                        xs[:, k * 512:(k + 1) * 512],
                        xnT[:, k * 2048 + tb * 512:k * 2048 + (tb + 1) * 512],
                        cbc[:]),
                        reads=("xnT", "l1f_cbc"), writes=(f"l1f_xs{k}",))
                for hc in range(8):
                    bank = pb[2 + hc % 2]
                    pairs = [(we[:, k * 1024 + hc * 128:k * 1024 + (hc + 1) * 128],
                              xs[:, k * 512:(k + 1) * 512]) for k in range(8)]
                    mm_chain(b, bank[:], pairs,
                             reads=tuple(["l1f_we"] + [f"l1f_xs{k}" for k in range(8)]),
                             wbank=f"pb{2 + hc % 2}")
                    sl = slice(hc * 2048 + tb * 512, hc * 2048 + (tb + 1) * 512)
                    b.op("vector", lambda v, sl=sl, bank=bank: v.tensor_add(
                        accT[:, sl], accT[:, sl], bank[:]),
                        reads=(f"pb{2 + hc % 2}", "accT"), writes=("accT",))
    b.barrier()


def emit_layer2(b):
    nc = b.nc
    A = b.A
    xnT, sT, accT = b.bufs["xnT"], b.bufs["sT"], b.bufs["accT"]
    pb, pbf, psm = b.pb, b.pbf, b.psm

    trunc = 99
    stg_ = getattr(b, "stage", None) or ""
    if stg_.startswith("l2a"):
        trunc = int(stg_[3:])
    with contextlib.ExitStack() as st:
        att = accT
        wqkv = st.enter_context(nc.sbuf_tensor("l2_wqkv", [128, 3 * 1024], BF16))
        qT = st.enter_context(nc.sbuf_tensor("l2_qT", [128, 2048], BF16))
        kT = st.enter_context(nc.sbuf_tensor("l2_kT", [128, 2048], BF16))
        vtok = st.enter_context(nc.sbuf_tensor("l2_vtok", [128, 16 * 128], BF16))
        expT = st.enter_context(nc.sbuf_tensor("l2_expT", [128, 16 * 512], BF16))
        vT = expT[:, 0:2048]
        tp3 = st.enter_context(nc.sbuf_tensor("l2_tp3", [128, 3 * 128], BF16))
        rdenA = st.enter_context(nc.sbuf_tensor("l2_rdenA", [128, 512], F32))
        rdenB = st.enter_context(nc.sbuf_tensor("l2_rdenB", [1, 512], F32))
        b.op("vector", lambda v: v.memset(rdenA[:], 1.0), writes=("l2_rdenA",))
        etp3 = st.enter_context(nc.sbuf_tensor("l2_etp3", [128, 3 * 128], BF16))
        ee15 = st.enter_context(nc.sbuf_tensor("l2_ee15", [128, 16], F32))
        ee31 = st.enter_context(nc.sbuf_tensor("l2_ee31", [128, 16], F32))
        b.multi("scalar", [
            (lambda a: a.activation(ee15[:], b.bufs["e15s"][:], AF.Exp),
             ("e15s",), ("l2_ee15",)),
            (lambda a: a.activation(ee31[:], b.bufs["e31s"][:], AF.Exp),
             ("e31s",), ("l2_ee31",))])

        for ct in range(1 if trunc < 99 else 8):
            for wi in range(3):
                b.op("gpsimd", lambda g, wi=wi, ct=ct: wN_load(
                    g, b, wqkv[:, wi * 1024:(wi + 1) * 1024], "L2_attn_W_qkv",
                    (0, 8), (wi * 1024 + ct * 128, 128)),
                    writes=(f"l2_wqkv{wi}",), dma=True)
            for wi, (dst, dreg) in enumerate(((qT, "l2_qT"), (kT, "l2_kT"),
                                              (vT, "l2_expT"))):
                for tb in range(NTB):
                    bank = pb[tb % 2]
                    pairs = [(wqkv[:, wi * 1024 + k * 128:wi * 1024 + (k + 1) * 128],
                              xnT[:, k * 2048 + tb * 512:k * 2048 + (tb + 1) * 512])
                             for k in range(8)]
                    mm_chain(b, bank[:], pairs, reads=(f"l2_wqkv{wi}", "xnT"),
                             wbank=f"pb{tb % 2}")
                    b.op("vector", lambda v, dst=dst, tb=tb, bank=bank: v.tensor_copy(
                        dst[:, tb * 512:(tb + 1) * 512], bank[:]),
                        reads=(f"pb{tb % 2}",), writes=(dreg,))
            if trunc < 2:
                continue
            for kt2 in range(2):
                def tr(t, kt2=kt2):
                    last = None
                    for tt in range(8):
                        kt = kt2 * 8 + tt
                        last = t.transpose(pbf[:, tt * 128:(tt + 1) * 128],
                                           vT[:, kt * 128:(kt + 1) * 128],
                                           b.bufs["c_id128"][:])
                    return last
                b.op("tensor", tr, reads=("l2_expT", "c_id128"), writes=("pbf",))

                def evv(v, kt2=kt2):
                    last = None
                    for tt in range(8):
                        kt = kt2 * 8 + tt
                        last = v.tensor_copy(vtok[:, kt * 128:(kt + 1) * 128],
                                             pbf[:, tt * 128:(tt + 1) * 128])
                    return last
                b.op("vector", evv, reads=("pbf",), writes=("l2_vtok",))
            if trunc < 3:
                continue
            for hh in range(1 if trunc < 99 else 2):
                h = 2 * ct + hh
                base = 64 * hh
                for di in range(3):
                    b.op("gpsimd", lambda g, h=h, di=di: g.dma_start(
                        out=tp3[:, di * 128:(di + 1) * 128],
                        in_=b.toep[h:h + 1, di * 16384:(di + 1) * 16384]
                        .rearrange("o (p f) -> (o p) f", f=128)),
                        reads=("toep",), writes=("l2_tp3",), dma=True)
                b.op("scalar", lambda a, h=h: a.activation(
                    etp3[:], tp3[:], AF.Exp),
                    reads=("l2_tp3",), writes=("l2_etp3",))
                for qb in range(1 if trunc < 99 else NTB):
                    q0 = qb * 512
                    if trunc < 4 and trunc not in (31, 32, 33):
                        continue
                    for rnd in range(4):
                        kts = list(range(rnd * 4, min((rnd + 1) * 4, 16)))
                        i0 = 0
                        b.multi("tensor", [
                            (lambda t, i=i, kt=kt: t.matmul(
                                pb[i][:],
                                kT[base:base + 64, kt * 128:(kt + 1) * 128],
                                qT[base:base + 64, q0:q0 + 512],
                                start=True, stop=True),
                             ("l2_kT", "l2_qT"), (f"pb{i}",))
                            for i, kt in ((i0 + ii, kt)
                                          for ii, kt in enumerate(kts))])
                        if trunc == 31:
                            continue
                        b.multi("scalar", [
                            (lambda a, i=i, kt=kt: a.activation(
                                expT[:, kt * 512:(kt + 1) * 512], pb[i][:],
                                AF.Exp),
                             (f"pb{i}",), ("l2_expT",))
                            for i, kt in ((i0 + ii, kt)
                                          for ii, kt in enumerate(kts))])
                        fixes = []
                        for i, kt in enumerate(kts):
                            dlt = kt * 128 - q0
                            jj = 0
                            while jj < 4:
                                dsub = dlt - jj * 128
                                if abs(dsub) <= 128:
                                    di = {-128: 0, 0: 1, 128: 2}[dsub]
                                    osl = slice(kt * 512 + jj * 128,
                                                kt * 512 + (jj + 1) * 128)
                                    fixes.append(
                                        (lambda v, osl=osl, di=di: v.tensor_mul(
                                            expT[:, osl], expT[:, osl],
                                            etp3[:, di * 128:(di + 1) * 128]),
                                         ("l2_expT", "l2_etp3"), ("l2_expT",)))
                                    jj += 1
                                    continue
                                side = 0 if dsub < 0 else 1
                                j0 = jj
                                while jj < 4 and abs(dlt - jj * 128) > 128 and \
                                        (0 if dlt - jj * 128 < 0 else 1) == side:
                                    jj += 1
                                eec = ee15 if side == 0 else ee31
                                ereg = "l2_ee15" if side == 0 else "l2_ee31"
                                osl = slice(kt * 512 + j0 * 128,
                                            kt * 512 + jj * 128)
                                fixes.append(
                                    (lambda v, osl=osl, eec=eec, h=h:
                                     v.tensor_scalar_mul(
                                        expT[:, osl], expT[:, osl],
                                        eec[:, h:h + 1]),
                                     ("l2_expT", ereg), ("l2_expT",)))
                        b.multi("vector", fixes)
                    if trunc < 5 or trunc in (31, 32, 33):
                        continue

                    def avden(t, hh=hh, q0=q0):
                        for kt in range(16):
                            t.matmul(psm[0:64, :],
                                     vtok[:, kt * 128 + 64 * hh:
                                          kt * 128 + 64 * hh + 64],
                                     expT[:, kt * 512:(kt + 1) * 512],
                                     start=kt == 0, stop=kt == 15)
                        last = None
                        for kt in range(16):
                            last = t.matmul(psm[64:65, :],
                                            b.bufs["c_ones128"][:, 0:1],
                                            expT[:, kt * 512:(kt + 1) * 512],
                                            start=kt == 0, stop=kt == 15)
                        return last
                    b.op("tensor", avden,
                         reads=("l2_vtok", "l2_expT", "c_ones128"),
                         writes=("psm",))
                    sl = slice(ct * 2048 + q0, ct * 2048 + q0 + 512)
                    dslc = (rdenA[32 * qb:32 * qb + 1, :] if qb < 3
                            else rdenB[0:1, :])
                    dreg = "l2_rdenA" if qb < 3 else "l2_rdenB"
                    b.multi("vector", [
                        (lambda v, sl=sl, base=base: v.tensor_copy(
                            att[base:base + 64, sl], psm[0:64, :]),
                         ("psm",), ("accT",)),
                        (lambda v, dslc=dslc: v.tensor_copy(dslc, psm[64:65, :]),
                         ("psm",), (dreg,))])
                if trunc >= 5 and trunc not in (31, 32, 33):
                    b.multi("vector", [
                        (lambda v: v.reciprocal(rdenA[0:65, :], rdenA[0:65, :]),
                         ("l2_rdenA",), ("l2_rdenA",)),
                        (lambda v: v.reciprocal(rdenB[:], rdenB[:]),
                         ("l2_rdenB",), ("l2_rdenB",))])
                    for qb in range(NTB):
                        q0 = qb * 512
                        dslc = (rdenA[32 * qb:32 * qb + 1, :] if qb < 3
                                else rdenB[0:1, :])
                        dreg = "l2_rdenA" if qb < 3 else "l2_rdenB"
                        ob = 32 * qb if qb < 3 else 0
                        mm_chain(b, pb[4][0:64, :],
                                 [(b.bufs["c_onesf"][ob:ob + 1, 0:64], dslc)],
                                 reads=("c_onesf", dreg), wbank="pb4")
                        sl = slice(ct * 2048 + q0, ct * 2048 + q0 + 512)
                        b.op("vector", lambda v, sl=sl, base=base: v.tensor_mul(
                            att[base:base + 64, sl], att[base:base + 64, sl],
                            pb[4][0:64, :]),
                            reads=("accT", "pb4"), writes=("accT",))
        for j in range(0 if trunc < 99 else 8):
            b.op("gpsimd", lambda g, j=j: wN_load(
                g, b, wqkv[:, 0:1024], "L2_attn_W_out", (0, 8), (j * 128, 128)),
                writes=("l2_wqkv0",), dma=True)
            for tb in range(NTB):
                bank = pb[tb % 2]
                pairs = [(wqkv[:, k * 128:(k + 1) * 128],
                          att[:, k * 2048 + tb * 512:k * 2048 + (tb + 1) * 512])
                         for k in range(8)]
                mm_chain(b, bank[:], pairs, reads=("l2_wqkv0", "accT"),
                         wbank=f"pb{tb % 2}")
                b.op("vector", lambda v, j=j, tb=tb, bank=bank: v.tensor_copy(
                    sT[:, j * 2048 + tb * 512:j * 2048 + (tb + 1) * 512],
                    bank[:]),
                    reads=(f"pb{tb % 2}",), writes=("sT",))
    b.barrier()
    if getattr(b, "stage", None) == "attn2" or trunc < 99:
        return
    emit_l2_ffn(b)
    emit_combine(b, 2)


def emit_l2_ffn(b):
    nc = b.nc
    A = b.A
    xnT, accT = b.bufs["xnT"], b.bufs["accT"]
    pb, psm = b.pb, b.psm
    E = 8
    with contextlib.ExitStack() as st:
        coT = st.enter_context(nc.sbuf_tensor("l2f_coT", [8, 2048], BF16))
        boE = st.enter_context(nc.sbuf_tensor("l2f_boE", [8, 1024], BF16))
        b1c = st.enter_context(nc.sbuf_tensor("l2f_b1c", [128, 8 * 32], F32))
        stg = st.enter_context(contextlib.ExitStack())
        wg = stg.enter_context(nc.sbuf_tensor("l2f_wg", [128, 64], BF16))
        wgsrc = stg.enter_context(nc.sbuf_tensor("l2f_wgsrc", [8, 1024], BF16))
        lgT = stg.enter_context(nc.sbuf_tensor("l2f_lgT", [8, 2048], F32))
        b.op("gpsimd", lambda g: g.dma_start(out=wgsrc[:], in_=A["L2_ffn_Wg"]),
             writes=("l2f_wgsrc",), dma=True)
        for k in range(8):
            b.op("tensor", lambda t, k=k: t.transpose(
                b.pbf[:, 0:8], wgsrc[0:8, k * 128:(k + 1) * 128],
                b.bufs["c_id128"][0:8, 0:8]),
                reads=("l2f_wgsrc", "c_id128"), writes=("pbf",))
            b.op("vector", lambda v, k=k: v.tensor_copy(
                wg[:, k * 8:(k + 1) * 8], b.pbf[:, 0:8]),
                reads=("pbf",), writes=("l2f_wg",))
        for tb in range(NTB):
            pairs = [(wg[:, k * 8:(k + 1) * 8],
                      xnT[:, k * 2048 + tb * 512:k * 2048 + (tb + 1) * 512])
                     for k in range(8)]
            mm_chain(b, psm[0:8, :], pairs, reads=("l2f_wg", "xnT"), wbank="psm")
            b.op("vector", lambda v, tb=tb: v.tensor_copy(
                lgT[:, tb * 512:(tb + 1) * 512], psm[0:8, :]),
                reads=("psm",), writes=("l2f_lgT",))
        coT, coreg = emit_topk_gates(b, stg, "l2f", lgT, "l2f_lgT", E,
                                     renorm=True, eps=1e-9, coT_buf=coT)
        b.bufs["l2f_coT"] = coT
        bo = stg.enter_context(nc.sbuf_tensor("l2f_bo", [1, 1024], BF16))
        b.op("gpsimd", lambda g: g.dma_start(
            out=bo[:], in_=A["L2_ffn_bo"].rearrange("(o n) -> o n", o=1)),
            writes=("l2f_bo",), dma=True)
        for half in range(2):
            mm_chain(b, psm[0:8, :],
                     [(b.bufs["c_ones128"][0:1, 0:8],
                       bo[0:1, half * 512:(half + 1) * 512])],
                     reads=("c_ones128", "l2f_bo"), wbank="psm")
            b.op("vector", lambda v, half=half: v.tensor_copy(
                boE[0:8, half * 512:(half + 1) * 512], psm[0:8, :]),
                reads=("psm",), writes=("l2f_boE",))
        b1rows = stg.enter_context(nc.sbuf_tensor("l2f_b1r", [32, 128], F32))
        for e in range(E):
            b.op("sync", lambda s, e=e: s.dma_start(
                out=b1rows[:], in_=A["L2_ffn_b1"][e:e + 1, :]
                .rearrange("o (k f) -> (o k) f", f=128)),
                writes=("l2f_b1r",))
            b.op("tensor", lambda t: t.transpose(
                psm[:, 0:32], b1rows[0:32, 0:128], b.bufs["c_idf"][0:32, 0:32]),
                reads=("l2f_b1r", "c_idf"), writes=("psm",))
            b.op("vector", lambda v, e=e: v.tensor_copy(
                b1c[:, e * 32:(e + 1) * 32], psm[:, 0:32]),
                reads=("psm",), writes=("l2f_b1c",))
        b.barrier()
        stg.close()
        w1s = [st.enter_context(nc.sbuf_tensor(f"l2f_w1{h}", [128, 2048], BF16))
               for h in range(2)]
        wos = [st.enter_context(nc.sbuf_tensor(f"l2f_wo{h}", [128, 1024], BF16))
               for h in range(2)]
        b._l2w1p = 0
        b._l2wop = 0
        gsum = st.enter_context(nc.sbuf_tensor("l2f_gsum", [128, 16 * 512], BF16))
        b.bufs["l2f_gsum"] = gsum
        sgb = st.enter_context(nc.sbuf_tensor("l2f_sgb", [128, 4 * 512], BF16))
        ascr = st.enter_context(nc.sbuf_tensor("l2f_ascr", [128, 512], BF16))
        cbc = st.enter_context(nc.sbuf_tensor("l2f_cbc", [128, 512], F32))
        rsb = st.enter_context(nc.sbuf_tensor("l2f_rsb", [20, 128], BF16))
        for tb in range(NTB):
            for e in range(E):
                b.op("sync", lambda s, e=e: s.dma_start(
                    out=rsb[0:E, :], in_=b.A["rowsel"][0:E, e * 128:(e + 1) * 128]),
                    writes=("rsb",))
                mm_chain(b, psm[:],
                         [(rsb[0:E, 0:128],
                           coT[0:E, tb * 512:(tb + 1) * 512])],
                         reads=("rsb", coreg), wbank="psm")
                b.op("vector", lambda v: v.tensor_copy(cbc[:], psm[:]),
                     reads=("psm",), writes=("l2f_cbc",))
                for fp in range(4):
                    def w1part(j, q2):
                        pc = b._l2w1p
                        b._l2w1p += 1
                        slot, reg = w1s[pc % 2], f"l2f_w1{pc % 2}"
                        b.op("sync", lambda s, j=j, slot=slot: wT_load(
                            s, b, slot[:], "L2_ffn_W1", j, jw=256),
                            reads=("wbf_L2_ffn_W1",), writes=(reg,))
                        for jje in range(2):
                            jj = q2 * 2 + jje
                            pairs = [(slot[:, k * 256 + jje * 128:
                                         k * 256 + (jje + 1) * 128],
                                      xnT[:, k * 2048 + tb * 512:
                                          k * 2048 + (tb + 1) * 512])
                                     for k in range(8)]
                            mm_chain(b, pb[jj][:], pairs, reads=(reg, "xnT"),
                                     wbank=f"pb{jj}")
                    for q2 in range(2):
                        w1part(e * 16 + 8 + fp * 2 + q2, q2)
                    b.multi("scalar", [
                        (lambda a, jj=jj, bi=e * 32 + 16 + fp * 4 + jj: a.activation(
                            sgb[:, jj * 512:(jj + 1) * 512], pb[jj][:], AF.Sigmoid,
                            bias=b1c[:, bi:bi + 1]),
                         (f"pb{jj}", "l2f_b1c"), (f"l2f_sgb{jj}",))
                        for jj in range(4)])
                    b.multi("gpsimd", [
                        (lambda g, jj=jj: g.tensor_mul(
                            sgb[:, jj * 512:(jj + 1) * 512],
                            sgb[:, jj * 512:(jj + 1) * 512], cbc[:]),
                         (f"l2f_sgb{jj}", "l2f_cbc"), (f"l2f_sgb{jj}",))
                        for jj in range(4)])
                    for q2 in range(2):
                        w1part(e * 16 + fp * 2 + q2, q2)
                    items = []
                    for jj in range(4):
                        gi = fp * 4 + jj
                        bi = e * 32 + fp * 4 + jj
                        if e == 0:
                            items.append(
                                (lambda v, jj=jj, gi=gi, bi=bi:
                                 v.scalar_tensor_tensor(
                                    gsum[:, gi * 512:(gi + 1) * 512], pb[jj][:],
                                    b1c[:, bi:bi + 1],
                                    sgb[:, jj * 512:(jj + 1) * 512],
                                    ALU.add, ALU.mult),
                                 (f"pb{jj}", "l2f_b1c", f"l2f_sgb{jj}"),
                                 (f"l2f_gsum{gi}",)))
                        else:
                            items.append(
                                (lambda v, jj=jj, bi=bi:
                                 v.scalar_tensor_tensor(
                                    ascr[:, 0:512], pb[jj][:], b1c[:, bi:bi + 1],
                                    sgb[:, jj * 512:(jj + 1) * 512],
                                    ALU.add, ALU.mult),
                                 (f"pb{jj}", "l2f_b1c", f"l2f_sgb{jj}"),
                                 ("l2f_a",)))
                            items.append(
                                (lambda v, jj=jj, gi=gi: v.tensor_add(
                                    gsum[:, gi * 512:(gi + 1) * 512],
                                    gsum[:, gi * 512:(gi + 1) * 512],
                                    ascr[:, 0:512]),
                                 ("l2f_a", f"l2f_gsum{gi}"),
                                 (f"l2f_gsum{gi}",)))
                    b.multi("vector", items)
            for hc in range(8):
                bank = pb[4 + hc % 2]
                for kh in range(2):
                    pc = b._l2wop
                    b._l2wop += 1
                    slot, reg = wos[pc % 2], f"l2f_wo{pc % 2}"
                    b.op("sync", lambda s, hc=hc, kh=kh, slot=slot: wT_load(
                        s, b, slot[:], "L2_ffn_Wo", hc, k0=kh * 8, nk=8),
                        reads=("wbf_L2_ffn_Wo",), writes=(reg,))

                    def dn(t, hc=hc, tb=tb, bank=bank, kh=kh, slot=slot):
                        last = None
                        if kh == 0:
                            last = t.matmul(
                                bank[:], boE[0:E, hc * 128:(hc + 1) * 128],
                                coT[0:E, tb * 512:(tb + 1) * 512],
                                start=True, stop=False)
                        for k in range(8):
                            last = t.matmul(
                                bank[:], slot[:, k * 128:(k + 1) * 128],
                                gsum[:, (kh * 8 + k) * 512:(kh * 8 + k + 1) * 512],
                                start=False, stop=(kh == 1 and k == 7))
                        return last
                    b.op("tensor", dn,
                         reads=tuple([reg, "l2f_boE", coreg]
                                     + [f"l2f_gsum{kh * 8 + k}" for k in range(8)]),
                         writes=(f"pb{4 + hc % 2}",))
                b.op("vector", lambda v, hc=hc, tb=tb, bank=bank: v.tensor_copy(
                    accT[:, hc * 2048 + tb * 512:hc * 2048 + (tb + 1) * 512],
                    bank[:]),
                    reads=(f"pb{4 + hc % 2}",), writes=("accT",))
    b.barrier()


def emit_layer3(b):
    nc = b.nc
    A = b.A
    xnT, sT, accT = b.bufs["xnT"], b.bufs["sT"], b.bufs["accT"]
    pb, pbf, psm = b.pb, b.pbf, b.psm

    with contextlib.ExitStack() as st:
        mixc = st.enter_context(nc.sbuf_tensor("l3_mixc", [128, 3 * 8], F32))
        wsl = st.enter_context(nc.sbuf_tensor("l3_wsl", [128, 1024], BF16))
        wA = st.enter_context(nc.sbuf_tensor("l3_wA", [128, 1024], BF16))
        uec = st.enter_context(nc.sbuf_tensor("l3_uec", [128, 8], F32))
        ew_c = st.enter_context(nc.sbuf_tensor("l3_ewc", [128, 8], F32))
        ew16 = st.enter_context(nc.sbuf_tensor("l3_ew16", [16, 2], F32))
        qs = st.enter_context(nc.sbuf_tensor("l3_qs", [128, 2048], BF16))
        kt = st.enter_context(nc.sbuf_tensor("l3_kt", [128, 2048], F32))
        vt = st.enter_context(nc.sbuf_tensor("l3_vt", [128, 2048], BF16))
        rt = st.enter_context(nc.sbuf_tensor("l3_rt", [128, 2048], BF16))
        heu = st.enter_context(nc.sbuf_tensor("l3_heu", [128, 1024], BF16))
        hekv = st.enter_context(nc.sbuf_tensor("l3_hekv", [128, 1024], BF16))
        heks = st.enter_context(nc.sbuf_tensor("l3_heks", [128, 1024], BF16))
        hsa = st.enter_context(nc.sbuf_tensor("l3_hsa", [128, 1025], BF16))
        hsb = st.enter_context(nc.sbuf_tensor("l3_hsb", [128, 1025], BF16))
        hden = st.enter_context(nc.sbuf_tensor("l3_hden", [128, 1024], BF16))
        hn = st.enter_context(nc.sbuf_tensor("l3_hn", [128, 1024], BF16))
        hewt = st.enter_context(nc.sbuf_tensor("l3_hewt", [128, 1024], F32))

        for mi, nm in enumerate(("mix_k", "mix_v", "mix_r")):
            b.op("sync", lambda s, mi=mi, nm=nm: s.dma_start(
                out=mixc[:, mi * 8:(mi + 1) * 8],
                in_=A[f"L3_attn_{nm}"].rearrange("a c n -> (a c n)")
                .rearrange("(k p) -> p k", p=128),
                allow_slow_non_contiguous=True),
                writes=(f"l3_mix{mi}",))
        b.op("sync", lambda s: s.dma_start(
            out=ew16[:, 0:1], in_=A["L3_attn_time_decay"].rearrange("(p o) -> p o", o=1)),
            writes=("l3_ew16",))
        b.op("sync", lambda s: s.dma_start(
            out=ew16[:, 1:2], in_=A["L3_attn_time_first"].rearrange("(p o) -> p o", o=1)),
            writes=("l3_u16",))
        b.op("scalar", lambda a: a.activation(ew16[:, 0:1], ew16[:, 0:1], AF.Exp),
             reads=("l3_ew16",), writes=("l3_ew16",))
        b.op("scalar", lambda a: a.activation(ew16[:, 0:1], ew16[:, 0:1], AF.Exp,
                                              scale=-1.0),
             reads=("l3_ew16",), writes=("l3_ew16",))
        ewrow = st.enter_context(nc.sbuf_tensor("l3_ewrow", [1, 32], F32))
        for col, reg in ((0, "l3_ew16"), (1, "l3_u16")):
            b.op("tensor", lambda t, col=col: t.transpose(
                psm[0:1, 0:16], ew16[:, col:col + 1], b.bufs["c_idf"][0:16, 0:16]),
                reads=(reg, "c_idf"), writes=("psm",))
            b.op("vector", lambda v, col=col: v.tensor_copy(
                ewrow[0:1, col * 16:col * 16 + 16], psm[0:1, 0:16]),
                reads=("psm",), writes=("l3_ewrow",))
        for ct in range(8):
            for col, dst, dreg in ((0, ew_c, "l3_ewc"), (1, uec, "l3_uec")):
                def bc(t, ct=ct, col=col):
                    t.matmul(psm[0:64, 0:1], b.bufs["c_onesf"][0:1, 0:64],
                             ewrow[0:1, col * 16 + 2 * ct:col * 16 + 2 * ct + 1],
                             start=True, stop=True)
                    return t.matmul(psm[64:128, 0:1], b.bufs["c_onesf"][0:1, 0:64],
                                    ewrow[0:1, col * 16 + 2 * ct + 1:col * 16 + 2 * ct + 2],
                                    start=True, stop=True)
                b.op("tensor", bc, reads=("l3_ewrow", "c_onesf"), writes=("psm",))
                b.op("vector", lambda v, ct=ct, dst=dst, dreg=dreg: v.tensor_copy(
                    dst[:, ct:ct + 1], psm[:, 0:1]),
                    reads=("psm",), writes=(dreg,))

        for ct in range(8):
            for nm, mi in (("Wv", 1), ("Wr", 2), ("Wk", 0)):
                b.op("gpsimd", lambda s, nm=nm, ct=ct: wT_load(
                    s, b, wsl[:], f"L3_attn_{nm}", ct),
                    writes=("l3_wsl",), dma=True)
                b.multi("vector",
                        [(lambda v, k=k, mi=mi: v.tensor_scalar_mul(
                            wA[:, k * 128:(k + 1) * 128],
                            wsl[:, k * 128:(k + 1) * 128],
                            mixc[:, mi * 8 + k:mi * 8 + k + 1]),
                          ("l3_wsl", f"l3_mix{mi}"), ("l3_wA",))
                         for k in range(8)]
                        + [(lambda v, k=k: v.tensor_sub(
                            wsl[:, k * 128:(k + 1) * 128],
                            wsl[:, k * 128:(k + 1) * 128],
                            wA[:, k * 128:(k + 1) * 128]),
                            ("l3_wsl", "l3_wA"), ("l3_wsl",))
                           for k in range(8)])
                for tb in range(NTB):
                    bA, bB = 2 * (tb % 2), 2 * (tb % 2) + 1
                    pairs = [(wA[:, k * 128:(k + 1) * 128],
                              xnT[:, k * 2048 + tb * 512:k * 2048 + (tb + 1) * 512])
                             for k in range(8)]
                    mm_chain(b, pb[bA][:], pairs,
                             reads=("l3_wA", "xnT"), wbank=f"pb{bA}")
                    pairs = [(wsl[:, k * 128:(k + 1) * 128],
                              xnT[:, k * 2048 + tb * 512:k * 2048 + (tb + 1) * 512])
                             for k in range(8)]
                    mm_chain(b, pb[bB][:], pairs,
                             reads=("l3_wsl", "xnT"), wbank=f"pb{bB}")
                    b.multi("vector", [
                        (lambda v, tb=tb, bB=bB: v.tensor_copy(
                            qs[:, tb * 512:(tb + 1) * 512], pb[bB][:]),
                         (f"pb{bB}",), ("l3_qs",)),
                        (lambda v, tb=tb, bA=bA: v.tensor_copy(
                            kt[:, tb * 512:(tb + 1) * 512], pb[bA][:]),
                         (f"pb{bA}",), ("l3_kt",))])
                if nm == "Wv":
                    b.op("vector", lambda v: v.tensor_add(
                        vt[:, 1:2048], kt[:, 1:2048], qs[:, 0:2047]),
                        reads=("l3_kt", "l3_qs"), writes=("l3_vt",))
                    b.op("vector", lambda v: v.tensor_copy(
                        vt[:, 0:1], kt[:, 0:1]),
                        reads=("l3_kt",), writes=("l3_vt",))
                elif nm == "Wr":
                    b.op("vector", lambda v: v.tensor_add(
                        kt[:, 1:2048], kt[:, 1:2048], qs[:, 0:2047]),
                        reads=("l3_kt", "l3_qs"), writes=("l3_kt",))
                    b.op("scalar", lambda a: a.activation(rt[:], kt[:], AF.Sigmoid),
                         reads=("l3_kt",), writes=("l3_rt",))
                else:
                    b.op("vector", lambda v: v.tensor_add(
                        kt[:, 1:2048], kt[:, 1:2048], qs[:, 0:2047]),
                        reads=("l3_kt", "l3_qs"), writes=("l3_kt",))
            for hf in range(2):
                sl = slice(hf * 1024, (hf + 1) * 1024)
                b.op("vector", lambda v: v.memset(hewt[:], 1.0), writes=("l3_hewt",))
                b.op("vector", lambda v, ct=ct: v.tensor_scalar_mul(
                    hewt[:], hewt[:], ew_c[:, ct:ct + 1]),
                    reads=("l3_hewt", "l3_ewc"), writes=("l3_hewt",))
                b.op("scalar", lambda a, ct=ct, sl=sl: a.activation(
                    heu[:], kt[:, sl], AF.Exp, bias=uec[:, ct:ct + 1]),
                    reads=("l3_kt", "l3_uec"), writes=("l3_heu",))
                b.op("scalar", lambda a, sl=sl: a.activation(
                    qs[:, sl], kt[:, sl], AF.Exp),
                    reads=("l3_kt",), writes=("l3_qs",))
                b.op("gpsimd", lambda g, sl=sl: g.tensor_mul(
                    hekv[:], qs[:, sl], vt[:, sl]),
                    reads=("l3_qs", "l3_vt"), writes=("l3_hekv",))
                for tb2 in range(2):
                    o = hf * 1024 + tb2 * 512
                    mm_chain(b, pb[5][:],
                             [(b.bufs["c_BB"][:], qs[:, o:o + 512])],
                             reads=("c_BB", "l3_qs"), wbank="pb5")
                    b.op("vector", lambda v, tb2=tb2: v.tensor_copy(
                        heks[:, tb2 * 512:(tb2 + 1) * 512], pb[5][:]),
                        reads=("pb5",), writes=("l3_heks",))
                    mm_chain(b, pb[4][:],
                             [(b.bufs["c_BB"][:], heu[:, tb2 * 512:(tb2 + 1) * 512])],
                             reads=("c_BB", "l3_heu"), wbank="pb4")
                    b.op("vector", lambda v, tb2=tb2: v.tensor_copy(
                        hden[:, tb2 * 512:(tb2 + 1) * 512], pb[4][:]),
                        reads=("pb4",), writes=("l3_hden",))
                if hf == 0:
                    b.op("vector", lambda v: v.memset(hsa[:, 0:1], 0.0),
                         writes=("l3_hsa",))
                    b.op("vector", lambda v: v.memset(hsb[:, 0:1], 0.0),
                         writes=("l3_hsb",))
                else:
                    b.op("vector", lambda v: v.tensor_copy(
                        hsa[:, 0:1], hsa[:, 1024:1025]),
                        reads=("l3_hsa",), writes=("l3_hsa",))
                    b.op("vector", lambda v: v.tensor_copy(
                        hsb[:, 0:1], hsb[:, 1024:1025]),
                        reads=("l3_hsb",), writes=("l3_hsb",))
                b.op("vector", lambda v: v.tensor_tensor_scan(
                    hsa[:, 1:1025], hewt[:], hekv[:], hsa[:, 0:1],
                    ALU.mult, ALU.add),
                    reads=("l3_hewt", "l3_hekv", "l3_hsa"), writes=("l3_hsa",))
                b.op("vector", lambda v: v.tensor_tensor_scan(
                    hsb[:, 1:1025], hewt[:], heks[:], hsb[:, 0:1],
                    ALU.mult, ALU.add),
                    reads=("l3_hewt", "l3_heks", "l3_hsb"), writes=("l3_hsb",))
                b.op("gpsimd", lambda g: g.tensor_scalar_max(
                    hden[:], hden[:], 1e-6),
                    reads=("l3_hden",), writes=("l3_hden",))
                b.op("gpsimd", lambda g: g.tensor_add(
                    hden[:], hden[:], hsb[:, 0:1024]),
                    reads=("l3_hden", "l3_hsb"), writes=("l3_hden",))
                b.multi("scalar", [
                    (lambda a: a.activation(hewt[:], hden[:], AF.Ln),
                     ("l3_hden",), ("l3_hewt",)),
                    (lambda a: a.activation(hewt[:], hewt[:], AF.Exp, scale=-1.0),
                     ("l3_hewt",), ("l3_hewt",))])
                b.op("gpsimd", lambda g, sl=sl: g.tensor_mul(
                    hn[:], heu[:], vt[:, sl]),
                    reads=("l3_heu", "l3_vt"), writes=("l3_hn",))
                b.op("gpsimd", lambda g: g.tensor_add(hn[:], hn[:], hsa[:, 0:1024]),
                     reads=("l3_hn", "l3_hsa"), writes=("l3_hn",))
                b.op("vector", lambda v: v.tensor_mul(hn[:], hn[:], hewt[:]),
                     reads=("l3_hn", "l3_hewt"), writes=("l3_hn",))
                b.op("gpsimd", lambda g, ct=ct, sl=sl, hf=hf: g.tensor_mul(
                    accT[:, ct * 2048 + hf * 1024:ct * 2048 + (hf + 1) * 1024],
                    rt[:, sl], hn[:]),
                    reads=("l3_rt", "l3_hn"), writes=("accT",))
        for j in range(8):
            b.op("gpsimd", lambda s, j=j: wT_load(s, b, wsl[:], "L3_attn_Wo", j),
                 writes=("l3_wsl",), dma=True)
            for tb in range(NTB):
                bank = pb[tb % 2]
                pairs = [(wsl[:, k * 128:(k + 1) * 128],
                          accT[:, k * 2048 + tb * 512:k * 2048 + (tb + 1) * 512])
                         for k in range(8)]
                mm_chain(b, bank[:], pairs, reads=("l3_wsl", "accT"),
                         wbank=f"pb{tb % 2}")
                b.op("vector", lambda v, j=j, tb=tb, bank=bank: v.tensor_copy(
                    sT[:, j * 2048 + tb * 512:j * 2048 + (tb + 1) * 512],
                    bank[:]),
                    reads=(f"pb{tb % 2}",), writes=("sT",))
    b.barrier()
    emit_l3_ffn(b)
    emit_combine(b, 3)


def emit_round(b, xb, xreg, d_b, dreg, iq, ireg, out_ap, oreg, width):
    b.op("vector", lambda v: v.tensor_copy(iq[:, 0:width], xb[:, 0:width]),
         reads=(xreg,), writes=(ireg,))
    b.op("vector", lambda v: v.tensor_copy(d_b[:, 0:width], iq[:, 0:width]),
         reads=(ireg,), writes=(dreg,))
    b.op("vector", lambda v: v.tensor_sub(xb[:, 0:width], xb[:, 0:width],
                                          d_b[:, 0:width]),
         reads=(xreg, dreg), writes=(xreg,))
    b.op("vector", lambda v: v.scalar_tensor_tensor(
        d_b[:, 0:width], xb[:, 0:width], 0.5, d_b[:, 0:width],
        ALU.is_ge, ALU.add),
        reads=(xreg, dreg), writes=(dreg,))
    b.op("vector", lambda v: v.tensor_single_scalar(
        xb[:, 0:width], xb[:, 0:width], -0.5, ALU.is_le),
        reads=(xreg,), writes=(xreg,))
    b.op("vector", lambda v: v.tensor_sub(out_ap, d_b[:, 0:width], xb[:, 0:width]),
         reads=(dreg, xreg), writes=(oreg,))


def emit_l3_ffn(b):
    nc = b.nc
    A = b.A
    xres, xnT, accT = b.bufs["xres"], b.bufs["xnT"], b.bufs["accT"]
    stat_m, stat_r = b.bufs["stat_m"], b.bufs["stat_r"]
    pb, psm = b.pb, b.psm
    qT = xnT

    with contextlib.ExitStack() as st:
        scr = st.enter_context(nc.sbuf_tensor("l3f_scr", [128, 1024], F32))
        scr2 = st.enter_context(nc.sbuf_tensor("l3f_scr2", [128, 1024], F32))
        iq = st.enter_context(nc.sbuf_tensor("l3f_iq", [128, 1024], I32))
        qrhs = st.enter_context(nc.sbuf_tensor("l3f_qrhs", [128, 8 * 1024], BF16))
        csc = st.enter_context(nc.sbuf_tensor("l3f_csc", [128, 8], F32))
        row = st.enter_context(nc.sbuf_tensor("l3f_row", [1, 128], F32))

        with contextlib.ExitStack() as sta:
            crow = sta.enter_context(nc.sbuf_tensor("l3f_crow", [1, 1024], F32))
            rb = sta.enter_context(nc.sbuf_tensor("l3f_rb", [128, 1024], F32))
            for i in range(8):
                if i == 0:
                    b.op("sync", lambda s: s.dma_start(
                        out=scr2[:], in_=A["L3_ffn_rhs"][0:128, :]),
                        writes=("l3f_scr2",))
                    b.op("scalar", lambda a: a.activation(scr2[:], scr2[:], AF.Abs),
                         reads=("l3f_scr2",), writes=("l3f_scr2",))
                else:
                    b.op("sync", lambda s, i=i: s.dma_start(
                        out=scr[:], in_=A["L3_ffn_rhs"][i * 128:(i + 1) * 128, :]),
                        writes=("l3f_scr",))
                    b.op("scalar", lambda a: a.activation(scr[:], scr[:], AF.Abs),
                         reads=("l3f_scr",), writes=("l3f_scr",))
                    b.op("vector", lambda v: v.tensor_max(
                        scr2[:], scr2[:], scr[:]),
                        reads=("l3f_scr2", "l3f_scr"), writes=("l3f_scr2",))
            for j in range(8):
                b.op("tensor", lambda t, j=j: t.transpose(
                    psm[:, 0:128], scr2[:, j * 128:(j + 1) * 128],
                    b.bufs["c_idf"][:]),
                    reads=("l3f_scr2", "c_idf"), writes=("psm",))
                b.op("vector", lambda v, j=j: v.reduce_max(
                    csc[:, j:j + 1], psm[:, 0:128], axis=AX.X),
                    reads=("psm",), writes=("l3f_csc",))
            b.op("vector", lambda v: v.tensor_scalar_max(csc[:], csc[:], 1e-12),
                 reads=("l3f_csc",), writes=("l3f_csc",))
            b.op("vector", lambda v: v.tensor_scalar_mul(csc[:], csc[:], 1.0 / 127.0),
                 reads=("l3f_csc",), writes=("l3f_csc",))
            for j in range(8):
                b.op("tensor", lambda t, j=j: t.transpose(
                    psm[0:1, 0:128], csc[:, j:j + 1], b.bufs["c_idf"][:]),
                    reads=("l3f_csc", "c_idf"), writes=("psm",))
                b.op("vector", lambda v, j=j: v.tensor_copy(
                    crow[0:1, j * 128:(j + 1) * 128], psm[0:1, 0:128]),
                    reads=("psm",), writes=("l3f_crow",))
            b.op("vector", lambda v: v.reciprocal(crow[:], crow[:]),
                 reads=("l3f_crow",), writes=("l3f_crow",))
            for half in range(2):
                mm_chain(b, pb[half][:],
                         [(b.bufs["c_onesf"][0:1, :],
                           crow[0:1, half * 512:(half + 1) * 512])],
                         reads=("c_onesf", "l3f_crow"), wbank=f"pb{half}")
                b.op("vector", lambda v, half=half: v.tensor_copy(
                    rb[:, half * 512:(half + 1) * 512], pb[half][:]),
                    reads=(f"pb{half}",), writes=("l3f_rb",))
            for i in range(8):
                b.op("sync", lambda s, i=i: s.dma_start(
                    out=scr[:], in_=A["L3_ffn_rhs"][i * 128:(i + 1) * 128, :]),
                    writes=("l3f_scr",))
                b.op("vector", lambda v: v.tensor_mul(scr[:], scr[:], rb[:]),
                     reads=("l3f_scr", "l3f_rb"), writes=("l3f_scr",))
                emit_round(b, scr, "l3f_scr", scr2, "l3f_scr2", iq, "l3f_iq",
                           qrhs[:, i * 1024:(i + 1) * 1024], f"l3f_qrhs{i}", 1024)
            b.barrier()

        with contextlib.ExitStack() as stb:
            srb = stb.enter_context(nc.sbuf_tensor("l3f_srb", [128, 2048], F32))
            qtok = stb.enter_context(nc.sbuf_tensor("l3f_qtok", [128, 1024], BF16))
            am = stb.enter_context(nc.sbuf_tensor("l3f_am", [128, 4], F32))
            for i in range(NT_T):
                b.op("vector", lambda v, i=i: v.tensor_scalar(
                    scr[:], xres[:, i * 1024:(i + 1) * 1024],
                    stat_m[:, i:i + 1], stat_r[:, i:i + 1], ALU.subtract, ALU.mult),
                    reads=("xres", "stat_m", "stat_r"), writes=("l3f_scr",))
                b.op("vector", lambda v: v.reduce_max(
                    am[:, 0:1], scr[:], axis=AX.X, apply_absolute_value=True),
                    reads=("l3f_scr",), writes=("l3f_am",))
                b.op("vector", lambda v: v.tensor_scalar_max(
                    am[:, 0:1], am[:, 0:1], 1e-12),
                    reads=("l3f_am",), writes=("l3f_am",))
                b.op("vector", lambda v: v.tensor_scalar_mul(
                    am[:, 0:1], am[:, 0:1], 1.0 / 127.0),
                    reads=("l3f_am",), writes=("l3f_am",))
                b.op("vector", lambda v: v.reciprocal(am[:, 1:2], am[:, 0:1]),
                     reads=("l3f_am",), writes=("l3f_am2",))
                b.op("vector", lambda v: v.tensor_scalar_mul(
                    scr[:], scr[:], am[:, 1:2]),
                    reads=("l3f_scr", "l3f_am2"), writes=("l3f_scr",))
                emit_round(b, scr, "l3f_scr", scr2, "l3f_scr2", iq, "l3f_iq",
                           qtok[:], "l3f_qtok", 1024)
                b.op("tensor", lambda t, i=i: t.transpose(
                    psm[0:1, 0:128], am[:, 0:1], b.bufs["c_idf"][:]),
                    reads=("l3f_am", "c_idf"), writes=("psm",))
                b.op("vector", lambda v, i=i: v.tensor_copy(
                    row[:], psm[0:1, 0:128]),
                    reads=("psm",), writes=("l3f_row",))
                mm_chain(b, pb[2][:, 0:128],
                         [(b.bufs["c_onesf"][0:1, :], row[0:1, :])],
                         reads=("c_onesf", "l3f_row"), wbank="pb2")
                b.op("vector", lambda v, i=i: v.tensor_copy(
                    srb[:, i * 128:(i + 1) * 128], pb[2][:, 0:128]),
                    reads=("pb2",), writes=("l3f_srb",))

                def tr(t, i=i):
                    last = None
                    for k in range(8):
                        last = t.transpose(
                            b.pbf[:, k * 128:(k + 1) * 128],
                            qtok[:, k * 128:(k + 1) * 128], b.bufs["c_id128"][:])
                    return last
                b.op("tensor", tr, reads=("l3f_qtok", "c_id128"), writes=("pbf",))

                def ev(v, i=i):
                    last = None
                    for k in range(8):
                        last = v.tensor_copy(
                            qT[:, k * 2048 + i * 128:k * 2048 + (i + 1) * 128],
                            b.pbf[:, k * 128:(k + 1) * 128])
                    return last
                b.op("vector", ev, reads=("pbf",), writes=("xnT",))
            for j in range(8):
                for tb in range(NTB):
                    bank = pb[4 + tb % 2]
                    pairs = [(qrhs[:, k * 1024 + j * 128:k * 1024 + (j + 1) * 128],
                              qT[:, k * 2048 + tb * 512:k * 2048 + (tb + 1) * 512])
                             for k in range(8)]
                    mm_chain(b, bank[:], pairs,
                             reads=tuple([f"l3f_qrhs{k}" for k in range(8)] + ["xnT"]),
                             wbank=f"pb{4 + tb % 2}")
                    sl = slice(j * 2048 + tb * 512, j * 2048 + (tb + 1) * 512)
                    b.op("vector", lambda v, bank=bank, tb=tb: v.tensor_mul(
                        scr[:, 0:512], bank[:], srb[:, tb * 512:(tb + 1) * 512]),
                        reads=(f"pb{4 + tb % 2}", "l3f_srb"), writes=("l3f_scr",))
                    b.op("vector", lambda v, sl=sl, j=j: v.tensor_scalar_mul(
                        accT[:, sl], scr[:, 0:512], csc[:, j:j + 1]),
                        reads=("l3f_scr", "l3f_csc"), writes=("accT",))
    b.barrier()


def make_spec_inputs(inputs):
    consts = host_consts()
    spec = {}
    common = {}
    for k, v in inputs.items():
        v = np.ascontiguousarray(np.asarray(v))
        if k == "L0_attn_x":
            continue
        if k in TRANS_W:
            w = v.reshape(-1, v.shape[-1])
            vt = np.ascontiguousarray(w.T)
            spec[k + "_T"] = (vt.shape, str(vt.dtype))
            common[k + "_T"] = vt
            continue
        spec[k] = (v.shape, str(v.dtype))
        common[k] = v
    for k, v in consts.items():
        spec[k] = (v.shape, str(v.dtype))
        common[k] = v
    spec["x_in"] = ((T, C), "float32")
    return spec, common


def kernel(**inputs):
    spec, common = make_spec_inputs(inputs)
    x = np.ascontiguousarray(np.asarray(inputs["L0_attn_x"], dtype=np.float32))
    nc = build_program(spec)
    in_maps = []
    for core in range(8):
        m = dict(common)
        m["x_in"] = np.ascontiguousarray(x[core])
        in_maps.append(m)
    res = run_bass_kernel_spmd(nc, in_maps, core_ids=list(range(8)))
    return np.stack([r["out"] for r in res.results], axis=0).astype(np.float32)
